# version 17
# baseline (speedup 1.0000x reference)
import numpy as np
import concourse.bass as bass
import concourse.mybir as mybir
from concourse.bass_utils import run_bass_kernel_spmd

F32 = mybir.dt.float32
BF16 = mybir.dt.bfloat16
I32 = mybir.dt.int32
AF = mybir.ActivationFunctionType
OP = mybir.AluOpType
AX = mybir.AxisListType.X

NCORE = 8
S, D, HID = 8192, 2048, 5632
TC = S // NCORE
NTT = TC // 128
G = D // 128
HCN = HID // 128
HSN = HID // 512
NSL = HID * D // NCORE // 128
EPS = float(np.float32(1e-5))

_CHAIN_OPS = frozenset([
    "tensor_tensor", "tensor_add", "tensor_sub", "tensor_mul", "tensor_max",
    "tensor_scalar", "tensor_single_scalar", "tensor_scalar_add",
    "tensor_scalar_sub", "tensor_scalar_mul", "tensor_scalar_max",
    "tensor_scalar_min", "tensor_copy", "tensor_relu", "tensor_reduce",
    "reduce_max", "reduce_sum", "scalar_tensor_tensor", "tensor_tensor_scan",
    "memset", "reciprocal", "select", "copy_predicated", "bn_stats",
    "activation", "copy", "add", "mul",
    "partition_all_reduce", "partition_broadcast",
])


class _Chained:

    def __init__(self, eng, sem, cnt, name):
        self._e = eng
        self._sem = sem
        self._cnt = cnt
        self._name = name

    def __getattr__(self, attr):
        real = getattr(self._e, attr)
        if attr not in _CHAIN_OPS:
            return real
        cnt, name = self._cnt, self._name

        def wrapped(*a, **k):
            if cnt[name] > 0:
                self._e.wait_ge(self._sem, cnt[name])
            inst = real(*a, **k)
            inst.then_inc(self._sem, 1)
            cnt[name] += 1
            return inst
        return wrapped


def build_nc():
    nc = bass.Bass("TRN2", target_bir_lowering=False)

    x_in = nc.dram_tensor("x", [TC, D], F32, kind="ExternalInput").ap()
    wa_t = nc.dram_tensor("wa_t", [G, 128, G * 128], F32, kind="ExternalInput").ap()
    wx_t = nc.dram_tensor("wx_t", [G, 128, G * 128], F32, kind="ExternalInput").ap()
    w12_t = nc.dram_tensor("w12_t", [2, HSN, 128, G * 512], F32,
                           kind="ExternalInput").ap()
    w3_t = nc.dram_tensor("w3_t", [G, 128, HCN * 128], F32, kind="ExternalInput").ap()
    lam_in = nc.dram_tensor("lam", [128, G], F32, kind="ExternalInput").ap()
    ba_in = nc.dram_tensor("ba", [128, G], F32, kind="ExternalInput").ap()
    bx_in = nc.dram_tensor("bx", [128, G], F32, kind="ExternalInput").ap()
    mask_in = nc.dram_tensor("mask", [128, 2 * NCORE], F32, kind="ExternalInput").ap()
    ident_in = nc.dram_tensor("ident", [128, 128], F32, kind="ExternalInput").ap()
    onesc_in = nc.dram_tensor("onesc", [128, 1], F32, kind="ExternalInput").ap()
    onesr_in = nc.dram_tensor("onesr", [1, 128], F32, kind="ExternalInput").ap()
    w1s_in = nc.dram_tensor("w1s", [128, NSL], F32, kind="ExternalInput").ap()
    w2s_in = nc.dram_tensor("w2s", [128, NSL], F32, kind="ExternalInput").ap()
    w3s_in = nc.dram_tensor("w3s", [128, NSL], F32, kind="ExternalInput").ap()

    out_dram = nc.dram_tensor("out", [TC, D], F32, kind="ExternalOutput").ap()

    a_spill = nc.dram_tensor("a_spill", [G, 128, TC], F32).ap()
    u_spill = nc.dram_tensor("u_spill", [G, 128, TC], F32).ap()
    h_park = nc.dram_tensor("h_park", [NTT, 128, HSN * 512], F32).ap()
    x1_park = nc.dram_tensor("x1_park", [NTT, 128, D], F32).ap()
    b3_park = nc.dram_tensor("b3_park", [G, 128, TC], F32).ap()
    cc_sum_in = nc.dram_tensor("cc_sum_in", [4, 1], F32).ap()
    cc_sum_out = nc.dram_tensor("cc_sum_out", [4, 1], F32).ap()
    cc_ag_in = nc.dram_tensor("cc_ag_in", [128, 2 * G], F32).ap()
    cc_ag_out = nc.dram_tensor("cc_ag_out", [NCORE * 128, 2 * G], F32).ap()

    stack = []

    def enter(h):
        t = h.__enter__()
        stack.append(h)
        return t

    scopes = {}

    def enter_sc(name, h):
        t = h.__enter__()
        scopes.setdefault(name, []).append(h)
        return t

    def free_sc(name):
        for h in reversed(scopes.pop(name, [])):
            h.__exit__(None, None, None)

    sems, cnt = {}, {}
    for n in ["sV", "sS", "sT", "sG",
              "dPRE", "dW0", "dW1", "dX0", "dX1", "dSP0", "dSP1",
              "dRL0", "dRL1", "dCC", "dHP0", "dHP1", "dHR0", "dHR1", "dHR2",
              "dHB0", "dHB1", "dHB2", "dHB3", "dHB4", "dHB5", "dHB6", "dHB7",
              "dPK", "dPK0", "dPK1", "dOUT", "dOUT2", "dTR", "dTR2", "cc"]:
        sems[n] = enter(nc.semaphore(n))
        cnt[n] = 0

    def mark(name, inst, step):
        inst.then_inc(sems[name], step)
        cnt[name] += step
        return (name, cnt[name])

    def tok(name):
        return (name, cnt[name])

    def wait(eng, *tokens):
        for t_ in tokens:
            if t_ is None:
                continue
            name, v = t_
            if v > 0:
                eng.wait_ge(sems[name], v)

    V, SC, TE, GP = "sV", "sS", "sT", "sG"

    def CH(eng, name):
        return _Chained(eng, sems[name], cnt, name)

    def te_group_wait(t):
        if cnt[TE] > 0:
            t.wait_ge(sems[TE], cnt[TE])

    ident = enter(nc.sbuf_tensor("identt", [128, 128], F32))
    lam = enter(nc.sbuf_tensor("lamt", [128, G], F32))
    bav = enter(nc.sbuf_tensor("bav", [128, G], F32))
    bxv = enter(nc.sbuf_tensor("bxv", [128, G], F32))
    coef = enter(nc.sbuf_tensor("coef", [128, G], F32))
    smallA = enter(nc.sbuf_tensor("smallA", [128, G], F32))
    maskt = enter(nc.sbuf_tensor("maskt", [128, 2 * NCORE], F32))
    wsums = enter(nc.sbuf_tensor("wsums", [128, 4], F32))
    wsall = enter(nc.sbuf_tensor("wsall", [128, 4], F32))
    onesc = enter(nc.sbuf_tensor("onesct", [128, 1], F32))
    onesr = enter(nc.sbuf_tensor("onesrt", [1, 128], F32))
    wconst = enter(nc.sbuf_tensor("wconst", [128, 16], F32))
    colAH = enter(nc.sbuf_tensor("colAH", [128, 2 * G], F32))
    gat = enter(nc.sbuf_tensor("gat", [128, NCORE * 2 * G], F32))
    Hcol = enter(nc.sbuf_tensor("Hcol", [128, G], F32))
    tmpG = enter(nc.sbuf_tensor("tmpG", [128, G], F32))
    tmpS = enter(nc.sbuf_tensor("tmpS", [128, G], F32))
    colb = enter(nc.sbuf_tensor("colb", [128, 16], F32))
    colb2 = enter(nc.sbuf_tensor("colb2", [128, 16], F32))
    bstb = enter(nc.sbuf_tensor("bstb", [128, 32], F32))
    bstb2 = enter(nc.sbuf_tensor("bstb2", [128, 32], F32))
    sinvz = enter(nc.sbuf_tensor("sinvz", [128, NTT], F32))
    c1c = enter(nc.sbuf_tensor("c1c", [128, NTT], F32))
    c2c = enter(nc.sbuf_tensor("c2c", [128, NTT], F32))
    c3c = enter(nc.sbuf_tensor("c3c", [128, NTT], F32))
    runmax = enter(nc.sbuf_tensor("runmax", [128, NTT], F32))
    shc = enter(nc.sbuf_tensor("shc", [128, NTT], F32))
    zbuf = enter(nc.sbuf_tensor("zbuf", [128, TC], F32))
    ws = {}
    wsl = enter_sc("p0", nc.sbuf_tensor("wsl", [128, NSL], F32))

    pb = [enter(nc.psum_tensor("pb%d" % k, [128, 512], F32)) for k in range(8)]

    blk = nc.Block("ph0"); blk.__enter__()
    if True:
        @blk.sync
        def _(sy):
            for dst, src in [(ident, ident_in), (lam, lam_in), (bav, ba_in),
                             (bxv, bx_in), (maskt, mask_in),
                             (onesc, onesc_in), (onesr, onesr_in)]:
                mark("dPRE", sy.dma_start(out=dst[:, :], in_=src), 16)
            mark("dPRE", sy.dma_start(out=wsl[:, :], in_=w1s_in), 16)

        @blk.scalar
        def _(s):
            s = CH(s, SC)
            wait(s, tok("dPRE"))
            s.activation(out=smallA[:, :], in_=lam[:, :], func=AF.Exp, scale=-1.0)

        @blk.vector
        def _(v):
            v = CH(v, V)
            wait(v, tok(SC), tok("dPRE"))
            v.memset(zbuf[:, :], 0.0)
            v.memset(Hcol[:, :], 0.0)
            v.memset(runmax[:, :], 0.0)
            e, t_ = smallA, coef
            v.tensor_scalar(t_[:, :], e[:, :], -0.2, 0.25, OP.mult, OP.add)
            v.scalar_tensor_tensor(t_[:, :], e[:, :], -1.0, t_[:, :], OP.mult, OP.mult)
            v.tensor_scalar_add(t_[:, :], t_[:, :], float(np.float32(1.0 / 3)))
            v.scalar_tensor_tensor(t_[:, :], e[:, :], -1.0, t_[:, :], OP.mult, OP.mult)
            v.tensor_scalar_add(t_[:, :], t_[:, :], 0.5)
            v.scalar_tensor_tensor(t_[:, :], e[:, :], -1.0, t_[:, :], OP.mult, OP.mult)
            v.tensor_scalar_add(t_[:, :], t_[:, :], 1.0)
            v.tensor_tensor(t_[:, :], e[:, :], t_[:, :], OP.mult)
            v.tensor_scalar_mul(t_[:, :], t_[:, :], -8.0)
            v.reduce_sum(wsums[:, 0:1], wsl[:, :], axis=AX, apply_absolute_value=True)

    for j, src in [(1, w2s_in), (2, w3s_in)]:
        if True:
            @blk.sync
            def _(sy, src=src):
                wait(sy, tok(V))
                mark("dPRE", sy.dma_start(out=wsl[:, :], in_=src), 16)

            @blk.vector
            def _(v, j=j):
                v = CH(v, V)
                wait(v, tok("dPRE"))
                v.reduce_sum(wsums[:, j:j + 1], wsl[:, :], axis=AX,
                             apply_absolute_value=True)

    if True:
        @blk.vector
        def _(v):
            v = CH(v, V)
            v.memset(wsums[:, 3:4], 0.0)

        @blk.tensor
        def _(t):
            wait(t, tok(V), tok("dPRE"))
            te_group_wait(t)
            mark(TE, t.matmul(pb[0][0:4, 0:1], wsums[:, 0:4], onesc[:, 0:1],
                              start=True, stop=True), 1)

        @blk.vector
        def _(v):
            v = CH(v, V)
            wait(v, tok(TE))
            v.tensor_copy(wsall[0:4, 0:1], pb[0][0:4, 0:1])

        @blk.gpsimd
        def _(g):
            wait(g, tok(V))
            mark("dCC", g.dma_start(out=cc_sum_in, in_=wsall[0:4, 0:1]), 16)
            wait(g, tok("dCC"))
            mark("cc", g.collective_compute(
                "AllReduce", OP.add, replica_groups=[list(range(NCORE))],
                ins=[cc_sum_in.opt()], outs=[cc_sum_out.opt()]), 1)
            wait(g, tok("cc"))
            mark("dCC", g.dma_start(out=wconst[0:1, 12:16],
                                    in_=cc_sum_out.rearrange("p f -> f p")), 16)

        @blk.vector
        def _(v):
            v = CH(v, V)
            wait(v, tok("dCC"))
            r0 = wconst[0:1, :]
            v.tensor_scalar(r0[:, 0:3], r0[:, 12:15], float(np.float32(1.0 / (HID * D))),
                            1e-5, OP.mult, OP.max)
            v.tensor_scalar_mul(r0[:, 3:6], r0[:, 0:3], 0.5)
            v.tensor_scalar_mul(r0[:, 6:9], r0[:, 3:6], -1.0)

        @blk.tensor
        def _(t):
            wait(t, tok(V))
            te_group_wait(t)
            mark(TE, t.matmul(pb[1][:, 0:9], onesr[0:1, :], wconst[0:1, 0:9],
                              start=True, stop=True), 1)

        @blk.vector
        def _(v):
            v = CH(v, V)
            wait(v, tok(TE))
            v.tensor_copy(wconst[:, 0:9], pb[1][:, 0:9])
    wconst_done = tok(V)

    def ln_stats(blk, src_t, c0=0, cb=None, pre=None):
        if cb is None:
            cb = colb
        bst = bstb if cb is colb else bstb2

        @blk.vector
        def _(v):
            v = CH(v, V)
            if pre is not None:
                pre(v)
            for qq in range(4):
                v.bn_stats(bst[:, 6 * qq:6 * qq + 6],
                           src_t[:, c0 + 512 * qq:c0 + 512 * qq + 512])
            v.tensor_tensor(bst[:, 24:32], bst[:, 1:24:3], bst[:, 1:24:3],
                            OP.mult)
            v.reduce_sum(cb[:, 7:8], bst[:, 1:24:3], axis=AX)
            v.reduce_sum(cb[:, 8:9], bst[:, 24:32], axis=AX)
            v.reduce_sum(cb[:, 9:10], bst[:, 2:24:3], axis=AX)
            v.tensor_scalar_mul(cb[:, 5:6], cb[:, 7:8], -0.125)
            v.tensor_tensor(cb[:, 12:13], cb[:, 5:6], cb[:, 5:6], OP.mult)
            v.scalar_tensor_tensor(cb[:, 11:12], cb[:, 8:9], 0.125,
                                   cb[:, 12:13], OP.mult, OP.subtract)
            v.tensor_scalar(cb[:, 12:13], cb[:, 9:10],
                            float(np.float32(1.0 / D)), EPS, OP.mult, OP.add)
            v.tensor_tensor(cb[:, 11:12], cb[:, 11:12], cb[:, 12:13], OP.add)
            v.reciprocal(cb[:, 2:3], cb[:, 11:12])

        @blk.scalar
        def _(s):
            s = CH(s, SC)
            wait(s, tok(V))
            s.activation(out=cb[:, 4:5], in_=cb[:, 2:3], func=AF.Sqrt)
            s.activation(out=cb[:, 6:7], in_=cb[:, 5:6], func=AF.Identity,
                         scale=cb[:, 4:5])
        return tok(SC)

    free_sc("p0")
    xnT_hi = enter_sc("p12", nc.sbuf_tensor("xnT_hi", [128, G * TC], BF16))
    xnT_lo = enter_sc("p12", nc.sbuf_tensor("xnT_lo", [128, G * TC], BF16))
    wafh = enter_sc("p12", nc.sbuf_tensor("wafh", [128, D], F32))
    wahi = [enter_sc("p12", nc.sbuf_tensor("wahi%d" % i, [128, D], BF16)) for i in range(3)]
    walo = [enter_sc("p12", nc.sbuf_tensor("walo%d" % i, [128, D], BF16)) for i in range(3)]
    wxhi = [enter_sc("p12", nc.sbuf_tensor("wxhi%d" % i, [128, D], BF16)) for i in range(3)]
    wxlo = [enter_sc("p12", nc.sbuf_tensor("wxlo%d" % i, [128, D], BF16)) for i in range(3)]

    split_done = [None] * G
    mm_done = [None] * G

    def p2_prep(j):
        q = j % 3
        for srcw, dsth, dstl in [(wa_t, wahi, walo), (wx_t, wxhi, wxlo)]:
            if True:
                @blk.sync
                def _(sy, j=j, q=q, srcw=srcw):
                    if j >= 1 and srcw is wa_t:
                        wait(sy, split_done[j - 1])
                    elif srcw is wx_t:
                        wait(sy, tok(V))
                    if j >= 3 and srcw is wa_t:
                        wait(sy, mm_done[j - 3])
                    mark("dW%d" % (j % 2), sy.dma_start(out=wafh[:, :],
                                                        in_=srcw[j]), 16)

                @blk.scalar
                def _(s, j=j, q=q, dsth=dsth):
                    s = CH(s, SC)
                    wait(s, tok("dW%d" % (j % 2)))
                    s.activation(out=dsth[q][:, :], in_=wafh[:, :], func=AF.Copy)

                @blk.vector
                def _(v, j=j, q=q, dsth=dsth, dstl=dstl):
                    v = CH(v, V)
                    wait(v, tok(SC))
                    v.tensor_tensor(dstl[q][:, :], wafh[:, :], dsth[q][:, :],
                                    OP.subtract)
        split_done[j] = tok(V)

    hi_t = enter_sc("p1x", nc.sbuf_tensor("hi_t", [128, D], BF16))
    lo_t = enter_sc("p1x", nc.sbuf_tensor("lo_t", [128, D], BF16))
    xt1 = enter_sc("p1x", nc.sbuf_tensor("xt1", [128, D], F32))
    ws["xn32"] = enter_sc("p1x", nc.sbuf_tensor("xn32a", [128, D], F32))

    xt1b = enter_sc("p1x", nc.sbuf_tensor("xt1b", [128, D], F32))
    xn1 = [ws["xn32"],
           enter_sc("p1x", nc.sbuf_tensor("xn32a2", [128, D], F32))]
    hi2 = [hi_t, enter_sc("p1x", nc.sbuf_tensor("hi_t2", [128, D], BF16))]
    lo2 = [lo_t, enter_sc("p1x", nc.sbuf_tensor("lo_t2", [128, D], BF16))]
    xtb = [xt1, xt1b]
    colb5 = [colb, colb2]

    p1_xbar_done = []
    p1_back_s = [None] * NTT
    p1_front_v = [None] * NTT

    def p1_front(tci):
        p = tci % 2
        cb = colb5[p]
        if True:
            @blk.sync
            def _(sy, tci=tci, p=p):
                if tci >= 2:
                    wait(sy, p1_back_s[tci - 2])
                mark("dX%d" % p, sy.dma_start(
                    out=xtb[p][:, :], in_=x_in[tci * 128:(tci + 1) * 128, :]), 16)

            def pre(v, p=p):
                wait(v._e, tok("dX%d" % p))

            ln_stats(blk, xtb[p], 0, cb=cb, pre=pre)
            p1_front_v[tci] = tok(V)

    def p1_back(tci):
        p = tci % 2
        cb = colb5[p]
        if True:
            @blk.scalar
            def _(s, tci=tci, p=p, cb=cb):
                s = CH(s, SC)
                wait(s, p1_front_v[tci])
                if tci >= 2:
                    wait(s, p1_xbar_done[tci - 2])
                s.activation(out=xn1[p][:, :], in_=xtb[p][:, :], func=AF.Identity,
                             bias=cb[:, 6:7], scale=cb[:, 4:5])
                p1_back_s[tci] = tok(SC)

            @blk.vector
            def _(v, tci=tci, p=p):
                v = CH(v, V)
                wait(v, tok(SC))
                v.tensor_copy(hi2[p][:, :], xn1[p][:, :])
                v.tensor_tensor(lo2[p][:, :], xn1[p][:, :], hi2[p][:, :],
                                OP.subtract)

            @blk.sync
            def _(sy, tci=tci, p=p):
                wait(sy, tok(V))
                hi3 = xnT_hi[:, :].rearrange("q (c t) -> q c t", t=TC)[
                    :, :, tci * 128:(tci + 1) * 128]
                lo3 = xnT_lo[:, :].rearrange("q (c t) -> q c t", t=TC)[
                    :, :, tci * 128:(tci + 1) * 128]
                mark("dTR2" if tci % 2 else "dTR",
                     sy.dma_start_transpose(hi3, hi2[p][:, :]), 16)
                p1_xbar_done.append(
                    mark("dTR2" if tci % 2 else "dTR",
                         sy.dma_start_transpose(lo3, lo2[p][:, :]), 16))

    p1_front(0)
    for tci in range(1, NTT):
        p1_front(tci)
        p1_back(tci - 1)
    p1_back(NTT - 1)

    for j in range(3):
        p2_prep(j)

    free_sc("p1x")
    rgt = enter_sc("p12", nc.sbuf_tensor("rgt", [128, TC], F32))
    igt = enter_sc("p12", nc.sbuf_tensor("igt", [128, TC], F32))
    agt = [enter_sc("p12", nc.sbuf_tensor("agt%d" % i, [128, TC], F32)) for i in range(2)]
    ugt = [enter_sc("p12", nc.sbuf_tensor("ugt%d" % i, [128, TC], F32)) for i in range(2)]
    xnr = enter_sc("p12", nc.sbuf_tensor("xnr", [128, TC], F32))
    gsc = enter_sc("p12", nc.sbuf_tensor("gsc", [128, TC], F32))
    y0b = enter_sc("p12", nc.sbuf_tensor("y0b", [128, TC], F32))
    afl = enter_sc("p12", nc.sbuf_tensor("afl", [128, TC], F32))

    gate_evac = [None, None]
    spill_done = [None, None]
    scan1_done = [None] * G

    for hc in range(G):
        p = hc % 2
        q = hc % 3
        if True:
            @blk.tensor
            def _(t, hc=hc, p=p, q=q):
                wait(t, split_done[hc])
                if hc == 0:
                    wait(t, tok("dTR"), tok("dTR2"), wconst_done)
                if hc >= 2:
                    wait(t, gate_evac[p])
                te_group_wait(t)
                for bank, whi, wlo in [(0, wahi[q], walo[q]),
                                       (2, wxhi[q], wxlo[q])]:
                    for half in range(2):
                        ps = pb[bank + half + 4 * p]
                        first = True
                        for wtile, xtile in [(whi, xnT_hi), (whi, xnT_lo),
                                             (wlo, xnT_hi)]:
                            for dc in range(G):
                                inst = t.matmul(
                                    ps[:, :],
                                    wtile[:, dc * 128:(dc + 1) * 128],
                                    xtile[:, dc * TC + half * 512:
                                          dc * TC + half * 512 + 512],
                                    start=first,
                                    stop=(wtile is wlo and dc == G - 1))
                                first = False
                        mark(TE, inst, 1)
                mm_done[hc] = tok(TE)

        if hc + 3 < G:
            p2_prep(hc + 3)

        if True:
            @blk.scalar
            def _(s, hc=hc, p=p):
                s = CH(s, SC)
                wait(s, mm_done[hc])
                if spill_done[p] is not None:
                    wait(s, spill_done[p])
                for half in range(2):
                    s.activation(out=rgt[:, half * 512:half * 512 + 512],
                                 in_=pb[0 + half + 4 * p][:, :], func=AF.Sigmoid,
                                 bias=bav[:, hc:hc + 1])
                for half in range(2):
                    s.activation(out=igt[:, half * 512:half * 512 + 512],
                                 in_=pb[2 + half + 4 * p][:, :], func=AF.Sigmoid,
                                 bias=bxv[:, hc:hc + 1])
                s.activation(out=agt[p][:, :], in_=rgt[:, :], func=AF.Exp,
                             scale=coef[:, hc:hc + 1])
                gate_evac[p] = tok(SC)

            @blk.vector
            def _(v, hc=hc, p=p):
                v = CH(v, V)
                wait(v, gate_evac[p])
                v.tensor_tensor(xnr[:, :],
                                xnT_hi[:, hc * TC:(hc + 1) * TC],
                                xnT_lo[:, hc * TC:(hc + 1) * TC], OP.add)
                v.tensor_tensor(xnr[:, :], igt[:, :], xnr[:, :], OP.mult)
                v.tensor_tensor(gsc[:, :], agt[p][:, :], agt[p][:, :], OP.mult)
                v.tensor_scalar(gsc[:, :], gsc[:, :], -1.0, 1.0, OP.mult, OP.add)

            @blk.scalar
            def _(s, hc=hc):
                s = CH(s, SC)
                wait(s, tok(V))
                s.activation(out=gsc[:, :], in_=gsc[:, :], func=AF.Sqrt)

            @blk.vector
            def _(v, hc=hc, p=p):
                v = CH(v, V)
                wait(v, tok(SC))
                if spill_done[p] is not None:
                    wait(v, spill_done[p])
                v.tensor_tensor(ugt[p][:, :], gsc[:, :], xnr[:, :], OP.mult)
                v.tensor_tensor_scan(y0b[:, :], agt[p][:, :], ugt[p][:, :],
                                     0.0, OP.mult, OP.add)
                v.tensor_copy(colAH[:, G + hc:G + hc + 1], y0b[:, TC - 1:TC])
                v.tensor_tensor_scan(afl[:, :], agt[p][:, :], zbuf[:, :],
                                     1.0, OP.mult, OP.add)
                v.tensor_copy(colAH[:, hc:hc + 1], afl[:, TC - 1:TC])
                scan1_done[hc] = tok(V)

            @blk.gpsimd
            def _(g, hc=hc, p=p):
                wait(g, scan1_done[hc])
                mark("dSP%d" % p, g.dma_start(out=a_spill[hc], in_=agt[p][:, :]), 16)
                spill_done[p] = mark("dSP%d" % p,
                                     g.dma_start(out=u_spill[hc], in_=ugt[p][:, :]),
                                     16)

    if True:
        @blk.gpsimd
        def _(g):
            wait(g, scan1_done[G - 2], scan1_done[G - 1])
            mark("dCC", g.dma_start(out=cc_ag_in, in_=colAH[:, :]), 16)
            wait(g, tok("dCC"))
            mark("cc", g.collective_compute(
                "AllGather", OP.bypass, replica_groups=[list(range(NCORE))],
                ins=[cc_ag_in.opt()], outs=[cc_ag_out.opt()]), 1)
            wait(g, tok("cc"))
            mark("dCC", g.dma_start(
                out=gat[:, :],
                in_=cc_ag_out.rearrange("(r p) j -> p r j", p=128)), 16)

        @blk.vector
        def _(v):
            v = CH(v, V)
            wait(v, tok("dCC"))
            for k in range(NCORE - 1):
                base = k * 2 * G
                v.tensor_scalar(tmpG[:, :], gat[:, base:base + G],
                                maskt[:, k:k + 1],
                                maskt[:, NCORE + k:NCORE + k + 1],
                                OP.mult, OP.add)
                v.tensor_single_scalar(tmpS[:, :], gat[:, base + G:base + 2 * G],
                                       maskt[:, k:k + 1], OP.mult)
                v.tensor_tensor(Hcol[:, :], Hcol[:, :], tmpG[:, :], OP.mult)
                v.tensor_tensor(Hcol[:, :], Hcol[:, :], tmpS[:, :], OP.add)
    fold_done = tok(V)

    free_sc("p12")
    zqT = enter_sc("zq", nc.sbuf_tensor("zqT", [128, G * TC], BF16))
    yT = enter_sc("yt", nc.sbuf_tensor("yT", [128, NTT * D], F32))
    arl = [enter_sc("p4", nc.sbuf_tensor("arl%d" % i, [128, TC], F32)) for i in range(2)]
    url = [enter_sc("p4", nc.sbuf_tensor("url%d" % i, [128, TC], F32)) for i in range(2)]
    ybuf = [enter_sc("p4", nc.sbuf_tensor("ybuf%d" % i, [128, TC], F32)) for i in range(2)]

    scan2_done = [None, None]
    tr_done = [None, None]
    evac_done = [None, None]
    for hc in range(G):
        p = hc % 2
        if True:
            @blk.sync
            def _(sy, hc=hc, p=p):
                if hc == 0:
                    wait(sy, tok("dSP0"), tok("dSP1"))
                if hc >= 2:
                    wait(sy, scan2_done[p])
                mark("dRL%d" % p, sy.dma_start(out=arl[p][:, :], in_=a_spill[hc]), 16)
                mark("dRL%d" % p, sy.dma_start(out=url[p][:, :], in_=u_spill[hc]), 16)

            @blk.vector
            def _(v, hc=hc, p=p):
                v = CH(v, V)
                wait(v, tok("dRL%d" % p), fold_done)
                if hc >= 2:
                    wait(v, tr_done[p])
                v.tensor_tensor_scan(ybuf[p][:, :], arl[p][:, :], url[p][:, :],
                                     Hcol[:, hc:hc + 1], OP.mult, OP.add)
                scan2_done[p] = tok(V)

            @blk.tensor
            def _(t, hc=hc, p=p):
                wait(t, scan2_done[p])
                if hc >= 2:
                    wait(t, evac_done[p])
                te_group_wait(t)
                for tb in range(NTT):
                    inst = t.transpose(
                        pb[4 * p + tb // 4][:, (tb % 4) * 128:(tb % 4) * 128 + 128],
                        ybuf[p][:, tb * 128:(tb + 1) * 128], ident[:, :])
                    mark(TE, inst, 1)
                tr_done[p] = tok(TE)

            @blk.vector
            def _(v, hc=hc, p=p):
                v = CH(v, V)
                wait(v, tr_done[p])
                for b in range(2):
                    out3 = yT[:, :].rearrange("p (tb d) -> p tb d", d=D)[
                        :, 4 * b:4 * b + 4, hc * 128:(hc + 1) * 128]
                    v.tensor_copy(out3, pb[4 * p + b][:, :].rearrange(
                        "p (tb q) -> p tb q", q=128))
                evac_done[p] = tok(V)

    free_sc("p4")
    xt = [enter_sc("p5", nc.sbuf_tensor("xtp5_%d" % i, [128, D], F32))
          for i in range(2)]
    cent = [enter_sc("p5", nc.sbuf_tensor("cent%d" % i, [128, D], F32))
            for i in range(2)]
    x15 = [enter_sc("p5", nc.sbuf_tensor("x15_%d" % i, [128, D], F32))
           for i in range(2)]
    it2 = enter_sc("p5", nc.sbuf_tensor("it2", [128, D], I32))
    ft2 = [enter_sc("p5", nc.sbuf_tensor("ft2_%d" % i, [128, D], F32))
           for i in range(2)]
    ds2 = enter_sc("p5", nc.sbuf_tensor("ds2", [128, D], F32))
    zqt_sb = [enter_sc("p5", nc.sbuf_tensor("zqt_sb%d" % i, [128, D], BF16))
              for i in range(2)]
    colb5 = [colb, colb2]

    p5_done = []
    p5_back_s = [None] * NTT
    p5_park = [None] * NTT
    p5_vb = [None] * NTT
    p5_vc = [None] * NTT
    p5_sb = [None] * NTT
    p5_vd = [None] * NTT
    p5_bb = [None] * NTT

    def p5_sy0(t):
        p = t % 2

        @blk.sync
        def _(sy, t=t, p=p):
            if t < 2:
                wait(sy, scan2_done[0], scan2_done[1],
                     tr_done[0], tr_done[1])
            if t >= 2:
                wait(sy, p5_done[t - 2])
            mark("dX%d" % p, sy.dma_start(
                out=xt[p][:, :], in_=x_in[t * 128:(t + 1) * 128, :]), 16)

    def p5_va(t):
        p = t % 2

        def pre5(v, t=t, p=p):
            if t == 0:
                wait(v._e, evac_done[0], evac_done[1])
            if t >= 2 and p5_back_s[t - 2] is not None:
                wait(v._e, p5_back_s[t - 2])
        return ln_stats(blk, yT, t * D, cb=colb5[p], pre=pre5)

    def p5_vb_f(t, lnA_s):
        p = t % 2
        cb = colb5[p]

        @blk.vector
        def _(v, t=t, p=p, cb=cb, lnA_s=lnA_s):
            v = CH(v, V)
            wait(v, tok("dX%d" % p), lnA_s)
            v.tensor_scalar(x15[p][:, :], yT[:, t * D:(t + 1) * D],
                            cb[:, 4:5], cb[:, 6:7], OP.mult, OP.add)
            v.tensor_tensor(x15[p][:, :], x15[p][:, :], xt[p][:, :], OP.add)
        p5_vb[t] = tok(V)

    def p5_sy1(t):
        p = t % 2

        @blk.sync
        def _(sy, t=t, p=p):
            wait(sy, p5_vb[t])
            p5_park[t] = mark("dPK%d" % p, sy.dma_start(
                out=x1_park[t], in_=x15[p][:, :]), 16)

    def p5_vc_f(t):
        p = t % 2
        cb = colb5[p]
        bst = bstb if cb is colb else bstb2

        @blk.vector
        def _(v, t=t, p=p, cb=cb, bst=bst):
            v = CH(v, V)
            for qq in range(4):
                v.bn_stats(bst[:, 6 * qq:6 * qq + 6],
                           x15[p][:, 512 * qq:512 * qq + 512])
            v.tensor_tensor(bst[:, 24:32], bst[:, 1:24:3], bst[:, 1:24:3],
                            OP.mult)
            v.reduce_sum(cb[:, 0:1], bst[:, 1:24:3], axis=AX)
            v.reduce_sum(cb[:, 1:2], bst[:, 24:32], axis=AX)
            v.reduce_sum(cb[:, 3:4], bst[:, 2:24:3], axis=AX)
            v.tensor_scalar_mul(cb[:, 5:6], cb[:, 0:1], -0.125)
            v.tensor_tensor(cb[:, 12:13], cb[:, 5:6], cb[:, 5:6], OP.mult)
            v.scalar_tensor_tensor(cb[:, 11:12], cb[:, 1:2], 0.125,
                                   cb[:, 12:13], OP.mult, OP.subtract)
            v.tensor_scalar(cb[:, 13:14], cb[:, 3:4],
                            float(np.float32(1.0 / D)), EPS, OP.mult, OP.add)
            v.tensor_tensor(cb[:, 11:12], cb[:, 11:12], cb[:, 13:14], OP.add)
            v.reciprocal(cb[:, 2:3], cb[:, 11:12])
            v.tensor_single_scalar(cent[p][:, :], x15[p][:, :], cb[:, 5:6],
                                   OP.add)
            v.reduce_max(cb[:, 7:8], cent[p][:, :], axis=AX,
                         apply_absolute_value=True)
        p5_vc[t] = tok(V)

    def p5_sb_f(t):
        cb = colb5[t % 2]

        @blk.scalar
        def _(s, t=t, cb=cb):
            s = CH(s, SC)
            wait(s, p5_vc[t])
            s.activation(out=cb[:, 4:5], in_=cb[:, 2:3], func=AF.Sqrt)
        p5_sb[t] = tok(SC)

    def p5_vd_f(t):
        cb = colb5[t % 2]

        @blk.vector
        def _(v, t=t, cb=cb):
            v = CH(v, V)
            wait(v, p5_sb[t])
            v.tensor_tensor(cb[:, 7:8], cb[:, 7:8], cb[:, 4:5], OP.mult)
            v.tensor_scalar_max(cb[:, 7:8], cb[:, 7:8], 1e-5)
            v.reciprocal(cb[:, 8:9], cb[:, 7:8])
            v.tensor_scalar_mul(cb[:, 8:9], cb[:, 8:9], 127.0)
            v.reciprocal(sinvz[:, t:t + 1], cb[:, 8:9])
            v.tensor_tensor(cb[:, 9:10], cb[:, 4:5], cb[:, 8:9], OP.mult)
            v.scalar_tensor_tensor(c1c[:, t:t + 1], sinvz[:, t:t + 1], -1.0,
                                   wconst[:, 0:1], OP.mult, OP.mult)
            v.scalar_tensor_tensor(c2c[:, t:t + 1], sinvz[:, t:t + 1], -1.0,
                                   wconst[:, 1:2], OP.mult, OP.mult)
        p5_vd[t] = tok(V)

    def p5_ba(t):
        p = t % 2
        cb = colb5[p]

        @blk.scalar
        def _(s, t=t, p=p, cb=cb):
            s = CH(s, SC)
            wait(s, p5_vd[t], p5_park[t])
            s.activation(out=x15[p][:, :], in_=cent[p][:, :], func=AF.Identity,
                         scale=cb[:, 9:10])
            s.copy(it2[:, :], x15[p][:, :])
            s.copy(ft2[p][:, :], it2[:, :])
        p5_back_s[t] = tok(SC)

    def p5_bb_f(t):
        p = t % 2

        @blk.vector
        def _(v, t=t, p=p):
            v = CH(v, V)
            wait(v, p5_back_s[t])
            if t >= 2:
                wait(v, p5_done[t - 2])
            v.tensor_tensor(ds2[:, :], x15[p][:, :], ft2[p][:, :], OP.subtract)
            v.scalar_tensor_tensor(ft2[p][:, :], ds2[:, :], 0.5, ft2[p][:, :],
                                   OP.is_gt, OP.add)
            v.scalar_tensor_tensor(zqt_sb[p][:, :], ds2[:, :], -0.5,
                                   ft2[p][:, :], OP.is_lt, OP.subtract)
        p5_bb[t] = tok(V)

    def p5_sy2(t):
        p = t % 2

        @blk.sync
        def _(sy, t=t, p=p):
            wait(sy, p5_bb[t])
            z3 = zqT[:, :].rearrange("q (c t) -> q c t", t=TC)[
                :, :, t * 128:(t + 1) * 128]
            p5_done.append(mark("dTR2" if t % 2 else "dTR",
                                sy.dma_start_transpose(z3, zqt_sb[p][:, :]),
                                16))

    p5_sy0(0)
    p5_sy0(1)
    p5_vb_f(0, p5_va(0))
    p5_sy1(0)
    p5_vc_f(0)
    p5_sb_f(0)
    p5_vd_f(0)
    for i in range(NTT):
        if i + 1 < NTT:
            p5_ba(i)
            p5_vb_f(i + 1, p5_va(i + 1))
            p5_sy1(i + 1)
            p5_vc_f(i + 1)
            p5_sb_f(i + 1)
            p5_bb_f(i)
            p5_sy2(i)
            if i + 2 < NTT:
                p5_sy0(i + 2)
            p5_vd_f(i + 1)
        else:
            p5_ba(i)
            p5_bb_f(i)
            p5_sy2(i)

    free_sc("p5")
    free_sc("yt")
    w12fh = [enter_sc("p6", nc.sbuf_tensor("w12fh%d" % i, [128, G * 256], F32))
             for i in range(2)]
    s1bh = [enter_sc("p6", nc.sbuf_tensor("s1bh%d" % i, [128, G * 256], BF16))
            for i in range(2)]
    w1q = [enter_sc("p6", nc.sbuf_tensor("w1q%d" % i, [128, G * 512], BF16))
           for i in range(3)]
    w2q = [enter_sc("p6", nc.sbuf_tensor("w2q%d" % i, [128, G * 512], BF16))
           for i in range(3)]
    silu_b = enter_sc("p6", nc.sbuf_tensor("silu_b", [128, 512], F32))
    hblk = [enter_sc("p6", nc.sbuf_tensor("hblk%d" % i, [128, 512], F32))
            for i in range(4)]

    tern_done = [None] * HSN
    p6_piece_done = [None, None]
    b12_done = {}
    silu_done = {}
    hdone = {}
    hpark = {}
    HW2 = G * 256

    def p6_tern(j):
        q3 = j % 3
        for pc in range(4):
            wi, half = pc // 2, pc % 2
            wpar = (4 * j + pc) % 2
            dstq = [w1q, w2q][wi]
            nthc = [wconst[:, 6:7], wconst[:, 7:8]][wi]
            thc_ = [wconst[:, 3:4], wconst[:, 4:5]][wi]
            if True:
                @blk.sync
                def _(sy, j=j, pc=pc, wi=wi, half=half, wpar=wpar):
                    if j == 0 and pc == 0:
                        wait(sy, tok("dTR"), tok("dTR2"), tok(V))
                    if p6_piece_done[wpar] is not None:
                        wait(sy, p6_piece_done[wpar])
                    if pc == 0 and j >= 3:
                        wait(sy, b12_done[(j - 3, NTT - 1)])
                    mark("dW%d" % wpar, sy.dma_start(
                        out=w12fh[wpar][:, :],
                        in_=w12_t[wi, j][:, half * HW2:(half + 1) * HW2]), 16)

                @blk.vector
                def _(v, j=j, q3=q3, pc=pc, wi=wi, half=half, wpar=wpar,
                      dstq=dstq, thc_=thc_, nthc=nthc):
                    v = CH(v, V)
                    wait(v, tok("dW%d" % wpar))
                    v.tensor_single_scalar(s1bh[wpar][:, :], w12fh[wpar][:, :],
                                           thc_, OP.is_gt)
                    v.tensor_single_scalar(
                        dstq[q3][:, half * HW2:(half + 1) * HW2],
                        w12fh[wpar][:, :], nthc, OP.is_lt)
                    v.tensor_tensor(dstq[q3][:, half * HW2:(half + 1) * HW2],
                                    s1bh[wpar][:, :],
                                    dstq[q3][:, half * HW2:(half + 1) * HW2],
                                    OP.subtract)
                    p6_piece_done[wpar] = tok(V)
        tern_done[j] = p6_piece_done[(4 * j + 3) % 2]

    for j in range(3):
        p6_tern(j)

    for hs in range(HSN):
        q3 = hs % 3
        for tci in range(NTT):
            idx = hs * NTT + tci
            bsel = 2 * (tci % 4)
            if True:
                @blk.tensor
                def _(t, hs=hs, q3=q3, tci=tci, idx=idx, bsel=bsel):
                    wait(t, tern_done[hs])
                    if hs == 0:
                        wait(t, p5_done[tci])
                    if idx >= 4:
                        wait(t, hdone[idx - 4])
                    te_group_wait(t)
                    for bank, wq in [(bsel + 0, w1q[q3]), (bsel + 1, w2q[q3])]:
                        for dc in range(G):
                            inst = t.matmul(
                                pb[bank][:, :],
                                zqT[:, dc * TC + tci * 128:dc * TC + (tci + 1) * 128],
                                wq[:, dc * 512:(dc + 1) * 512],
                                start=(dc == 0), stop=(dc == G - 1))
                        mark(TE, inst, 1)
                    b12_done[(hs, tci)] = tok(TE)

                @blk.scalar
                def _(s, hs=hs, tci=tci, idx=idx, bsel=bsel):
                    s = CH(s, SC)
                    wait(s, b12_done[(hs, tci)])
                    if idx >= 1:
                        wait(s, hdone[idx - 1])
                    s.activation(out=silu_b[:, :], in_=pb[bsel][:, :], func=AF.Silu,
                                 scale=c1c[:, tci:tci + 1])
                    silu_done[idx] = tok(SC)

                @blk.vector
                def _(v, hs=hs, tci=tci, idx=idx, bsel=bsel):
                    v = CH(v, V)
                    wait(v, silu_done[idx])
                    if idx >= 4:
                        wait(v, hpark[idx - 4])
                    hb = hblk[tci % 4]
                    v.scalar_tensor_tensor(hb[:, :], pb[bsel + 1][:, :],
                                           c2c[:, tci:tci + 1], silu_b[:, :],
                                           OP.mult, OP.mult)
                    v.reduce_max(colb[:, 10:11], hb[:, :], axis=AX,
                                 apply_absolute_value=True)
                    v.tensor_tensor(runmax[:, tci:tci + 1], runmax[:, tci:tci + 1],
                                    colb[:, 10:11], OP.max)
                    hdone[idx] = tok(V)

                @blk.gpsimd
                def _(g, hs=hs, tci=tci, idx=idx):
                    wait(g, hdone[idx])
                    hpark[idx] = mark("dHB%d" % (tci % 4), g.dma_start(
                        out=h_park[tci][:, hs * 512:(hs + 1) * 512],
                        in_=hblk[tci % 4][:, :]), 16)

            if tci == NTT - 1 and hs + 3 < HSN:
                p6_tern(hs + 3)

    if True:
        @blk.vector
        def _(v):
            v = CH(v, V)
            wait(v, hdone[HSN * NTT - 1])
            v.tensor_scalar_max(shc[:, :], runmax[:, :], 1e-5)
            v.reciprocal(shc[:, :], shc[:, :])
            v.tensor_scalar_mul(shc[:, :], shc[:, :], 127.0)
            v.reciprocal(c3c[:, :], shc[:, :])
            v.tensor_single_scalar(c3c[:, :], c3c[:, :], wconst[:, 2:3], OP.mult)
    hscale_done = tok(V)

    free_sc("p6")
    free_sc("zq")
    HQRT = 3 * 512
    hqT = enter_sc("hq", nc.sbuf_tensor("hqT", [128, HCN * TC], BF16))
    hrow = [enter_sc("p7a", nc.sbuf_tensor("hrow%d" % i, [128, HQRT], F32))
            for i in range(3)]
    ht_t = [enter_sc("p7a", nc.sbuf_tensor("ht_t%d" % i, [128, HQRT], F32))
            for i in range(3)]
    it7 = [enter_sc("p7a", nc.sbuf_tensor("it7%d" % i, [128, HQRT], I32))
           for i in range(3)]
    ft7 = [enter_sc("p7a", nc.sbuf_tensor("ft7%d" % i, [128, HQRT], F32))
           for i in range(3)]
    hq_sb = [enter_sc("p7a", nc.sbuf_tensor("hq_sb%d" % i, [128, HID], BF16))
             for i in range(2)]

    p7a_done = []
    piece_done = []
    v1_tok = {}
    sc_tok = {}
    P7PC = [(0, 3), (3, 6), (6, 9), (9, HSN)]
    PIECES = [(tci, hs0, hs1) for tci in range(NTT) for (hs0, hs1) in P7PC]
    NP7 = len(PIECES)

    def p7a_sync(k):
        tci, hs0, hs1 = PIECES[k]
        sz, hp = (hs1 - hs0) * 512, k % 3

        @blk.sync
        def _(sy, tci=tci, hs0=hs0, hs1=hs1, sz=sz, hp=hp, k=k):
            if k == 0:
                for j in range(4):
                    wait(sy, tok("dHB%d" % j))
            if k >= 3:
                wait(sy, piece_done[k - 3])
            mark("dHR%d" % hp, sy.dma_start(
                out=hrow[hp][:, 0:sz],
                in_=h_park[tci][:, hs0 * 512:hs1 * 512]), 16)

    def p7a_v1(k):
        tci = PIECES[k][0]
        sz, hp = (PIECES[k][2] - PIECES[k][1]) * 512, k % 3

        @blk.vector
        def _(v, tci=tci, sz=sz, hp=hp):
            v = CH(v, V)
            wait(v, tok("dHR%d" % hp), hscale_done)
            v.tensor_single_scalar(ht_t[hp][:, 0:sz], hrow[hp][:, 0:sz],
                                   shc[:, tci:tci + 1], OP.mult)
        v1_tok[k] = tok(V)

    def p7a_sc(k):
        sz, hp = (PIECES[k][2] - PIECES[k][1]) * 512, k % 3

        @blk.scalar
        def _(s, sz=sz, hp=hp, k=k):
            s = CH(s, SC)
            wait(s, v1_tok[k])
            s.copy(it7[hp][:, 0:sz], ht_t[hp][:, 0:sz])
            s.copy(ft7[hp][:, 0:sz], it7[hp][:, 0:sz])
        sc_tok[k] = tok(SC)

    def p7a_v2(k):
        tci, hs0, hs1 = PIECES[k]
        off, sz, hp = hs0 * 512, (hs1 - hs0) * 512, k % 3

        @blk.vector
        def _(v, tci=tci, off=off, sz=sz, hp=hp, k=k):
            v = CH(v, V)
            wait(v, sc_tok[k])
            if len(p7a_done) >= 2:
                wait(v, p7a_done[-2])
            v.tensor_tensor(hrow[hp][:, 0:sz], ht_t[hp][:, 0:sz],
                            ft7[hp][:, 0:sz], OP.subtract)
            v.scalar_tensor_tensor(ft7[hp][:, 0:sz], hrow[hp][:, 0:sz],
                                   0.5, ft7[hp][:, 0:sz],
                                   OP.is_gt, OP.add)
            v.scalar_tensor_tensor(hq_sb[tci % 2][:, off:off + sz],
                                   hrow[hp][:, 0:sz], -0.5,
                                   ft7[hp][:, 0:sz],
                                   OP.is_lt, OP.subtract)
            piece_done.append(tok(V))

        if k % 4 == 3:
            @blk.sync
            def _(sy, tci=tci):
                wait(sy, piece_done[-1])
                h3 = hqT[:, :].rearrange("q (c t) -> q c t", t=TC)[
                    :, :, tci * 128:(tci + 1) * 128]
                p7a_done.append(mark("dTR2" if tci % 2 else "dTR",
                                     sy.dma_start_transpose(h3,
                                                            hq_sb[tci % 2][:, :]),
                                     16))

    p7a_sync(0)
    p7a_sync(1)
    p7a_sync(2)
    p7a_v1(0)
    p7a_v1(1)
    p7a_sc(0)
    for i in range(1, NP7):
        if i + 1 < NP7:
            p7a_v1(i + 1)
        p7a_v2(i - 1)
        if i + 2 < NP7:
            p7a_sync(i + 2)
        p7a_sc(i)
    p7a_v2(NP7 - 1)

    free_sc("p7a")
    w3fh = [enter_sc("p7b", nc.sbuf_tensor("w3fh%d" % i, [128, HID // 2], F32))
            for i in range(2)]
    s3bh = [enter_sc("p7b", nc.sbuf_tensor("s3bh%d" % i, [128, HID // 2], BF16))
            for i in range(2)]
    w3q = [enter_sc("p7b", nc.sbuf_tensor("w3q%d" % i, [128, HID], BF16))
           for i in range(3)]
    b3raw = [enter_sc("p7b", nc.sbuf_tensor("b3raw%d" % i, [128, TC], F32))
             for i in range(2)]
    b3strip = [enter_sc("p7b", nc.sbuf_tensor("b3strip%d" % i, [128, TC], F32))
               for i in range(2)]

    tern3_done = [None] * G
    p7b_piece_done = [None, None]
    b3mm_done = [None] * G
    b3ev_done = [None] * G
    b3tr_done = [None] * G
    b3st_done = [None] * G
    b3pk_done = [None] * G
    H2 = HID // 2

    def p7b_tern(j):
        q3 = j % 3
        for half in range(2):
            wpar = (2 * j + half) % 2
            if True:
                @blk.sync
                def _(sy, j=j, half=half, wpar=wpar):
                    if j == 0 and half == 0:
                        wait(sy, tok(V), tok("dTR"), tok("dTR2"))
                    if p7b_piece_done[wpar] is not None:
                        wait(sy, p7b_piece_done[wpar])
                    if half == 0 and j >= 3:
                        wait(sy, b3mm_done[j - 3])
                    mark("dW%d" % wpar, sy.dma_start(
                        out=w3fh[wpar][:, :],
                        in_=w3_t[j][:, half * H2:(half + 1) * H2]), 16)

                @blk.vector
                def _(v, j=j, q3=q3, half=half, wpar=wpar):
                    v = CH(v, V)
                    wait(v, tok("dW%d" % wpar))
                    v.tensor_single_scalar(s3bh[wpar][:, :], w3fh[wpar][:, :],
                                           wconst[:, 5:6], OP.is_gt)
                    v.scalar_tensor_tensor(w3q[q3][:, half * H2:(half + 1) * H2],
                                           w3fh[wpar][:, :], wconst[:, 8:9],
                                           s3bh[wpar][:, :],
                                           OP.is_lt, OP.subtract)
                    p7b_piece_done[wpar] = tok(V)
        tern3_done[j] = p7b_piece_done[(2 * j + 1) % 2]

    for j in range(3):
        p7b_tern(j)

    for dc in range(G):
        p = dc % 2
        q3 = dc % 3
        if True:
            @blk.tensor
            def _(t, dc=dc, p=p, q3=q3):
                wait(t, tern3_done[dc])
                if dc == 0:
                    wait(t, p7a_done[-1])
                if dc >= 2:
                    wait(t, b3ev_done[dc - 2])
                te_group_wait(t)
                for half in range(2):
                    ps = pb[2 * p + half]
                    for hcc in range(HCN):
                        inst = t.matmul(
                            ps[:, :],
                            w3q[q3][:, hcc * 128:(hcc + 1) * 128],
                            hqT[:, hcc * TC + half * 512:hcc * TC + half * 512 + 512],
                            start=(hcc == 0), stop=(hcc == HCN - 1))
                    mark(TE, inst, 1)
                b3mm_done[dc] = tok(TE)

        if dc + 3 < G:
            p7b_tern(dc + 3)

        if True:
            @blk.vector
            def _(v, dc=dc, p=p):
                v = CH(v, V)
                wait(v, b3mm_done[dc])
                if dc >= 2:
                    wait(v, b3tr_done[dc - 2])
                for half in range(2):
                    v.tensor_copy(b3raw[p][:, half * 512:half * 512 + 512],
                                  pb[2 * p + half][:, :])
                b3ev_done[dc] = tok(V)

            @blk.tensor
            def _(t, dc=dc, p=p):
                wait(t, b3ev_done[dc])
                if dc >= 2:
                    wait(t, b3st_done[dc - 2])
                te_group_wait(t)
                for tb in range(NTT):
                    inst = t.transpose(
                        pb[4 + 2 * p + tb // 4][:, (tb % 4) * 128:(tb % 4) * 128 + 128],
                        b3raw[p][:, tb * 128:(tb + 1) * 128], ident[:, :])
                    mark(TE, inst, 1)
                b3tr_done[dc] = tok(TE)

            @blk.vector
            def _(v, dc=dc):
                v = CH(v, V)
                wait(v, b3tr_done[dc])
                if dc >= 2:
                    wait(v, b3pk_done[dc - 2])
                for b in range(2):
                    v.tensor_copy(b3strip[p][:, b * 512:b * 512 + 512],
                                  pb[4 + 2 * p + b][:, :])
                b3st_done[dc] = tok(V)

            @blk.sync
            def _(sy, dc=dc, p=p):
                wait(sy, b3st_done[dc])
                b3pk_done[dc] = mark("dPK%d" % p, sy.dma_start(
                    out=b3_park[dc], in_=b3strip[p][:, :]), 16)

    free_sc("p7b")
    free_sc("hq")
    xt = [enter(nc.sbuf_tensor("xtp8_%d" % i, [128, D], F32)) for i in range(2)]
    ws["xn32"] = enter(nc.sbuf_tensor("xn32c", [128, D], F32))
    b3t = [enter(nc.sbuf_tensor("b3t%d" % i, [128, D], F32)) for i in range(2)]
    ot8 = [ws["xn32"], enter(nc.sbuf_tensor("xn32c2", [128, D], F32))]
    colb5b = [colb, colb2]

    p8_out = [None] * NTT
    p8_front_v = [None] * NTT
    p8_back_v = [None] * NTT

    def p8_front(tci):
        p = tci % 2
        cb = colb5b[p]
        if True:
            @blk.sync
            def _(sy, tci=tci, p=p):
                if tci == 0:
                    wait(sy, tok("dPK"), tok("dPK0"), tok("dPK1"))
                if tci >= 2:
                    wait(sy, p8_back_v[tci - 2])
                mark("dX%d" % p, sy.dma_start(
                    out=b3t[p][:, :],
                    in_=b3_park[:, :, tci * 128:(tci + 1) * 128].rearrange(
                        "dc p q -> p dc q")), 16)
                mark("dX%d" % p, sy.dma_start(out=xt[p][:, :], in_=x1_park[tci]), 16)

            def pre8(v, tci=tci, p=p):
                wait(v._e, tok("dX%d" % p))
                v.tensor_single_scalar(b3t[p][:, :], b3t[p][:, :],
                                       c3c[:, tci:tci + 1], OP.mult)

            p8_front_v[tci] = ln_stats(blk, b3t[p], 0, cb=cb, pre=pre8)

    def p8_back(tci):
        p = tci % 2
        cb = colb5b[p]
        if True:
            @blk.vector
            def _(v, tci=tci, p=p, cb=cb):
                v = CH(v, V)
                wait(v, p8_front_v[tci])
                if tci >= 2:
                    wait(v, p8_out[tci - 2])
                v.tensor_scalar(ot8[p][:, :], b3t[p][:, :], cb[:, 4:5],
                                cb[:, 6:7], OP.mult, OP.add)
                v.tensor_tensor(ot8[p][:, :], ot8[p][:, :], xt[p][:, :], OP.add)
                p8_back_v[tci] = tok(V)

            @blk.sync
            def _(sy, tci=tci, p=p):
                wait(sy, p8_back_v[tci])
                p8_out[tci] = mark("dOUT2" if p else "dOUT", sy.dma_start(
                    out=out_dram[tci * 128:(tci + 1) * 128, :],
                    in_=ot8[p][:, :]), 16)

    p8_front(0)
    for tci in range(1, NTT):
        p8_front(tci)
        p8_back(tci - 1)
    p8_back(NTT - 1)

    if True:
        @blk.gpsimd
        def _(g):
            wait(g, tok("dOUT"), tok("dOUT2"))

    blk.__exit__(None, None, None)
    for h in reversed(stack):
        h.__exit__(None, None, None)
    return nc


def _prep(inputs):
    f = np.ascontiguousarray
    x = np.asarray(inputs["L0_attn_x"], dtype=np.float32).reshape(S, D)
    Wa = np.asarray(inputs["L0_attn_Wa"], dtype=np.float32)
    Wx = np.asarray(inputs["L0_attn_Wx"], dtype=np.float32)
    ba = np.asarray(inputs["L0_attn_ba"], dtype=np.float32)
    bx = np.asarray(inputs["L0_attn_bx"], dtype=np.float32)
    lam = np.asarray(inputs["L0_attn_Lam"], dtype=np.float32)
    w1 = np.asarray(inputs["L0_ffn_w1"], dtype=np.float32)
    w2 = np.asarray(inputs["L0_ffn_w2"], dtype=np.float32)
    w3 = np.asarray(inputs["L0_ffn_w3"], dtype=np.float32)

    wa_t = f(Wa.T.reshape(G, 128, G, 128).transpose(2, 1, 0, 3)
             .reshape(G, 128, G * 128))
    wx_t = f(Wx.T.reshape(G, 128, G, 128).transpose(2, 1, 0, 3)
             .reshape(G, 128, G * 128))
    w12 = f(np.stack([
        w1.T.reshape(G, 128, HSN, 512).transpose(2, 1, 0, 3).reshape(HSN, 128, G * 512),
        w2.T.reshape(G, 128, HSN, 512).transpose(2, 1, 0, 3).reshape(HSN, 128, G * 512),
    ]))
    w3_tiled = f(w3.T.reshape(HCN, 128, G, 128).transpose(2, 1, 0, 3)
                 .reshape(G, 128, HCN * 128))

    def col128(vec):
        return f(vec.reshape(G, 128).T)

    lam_c, ba_c, bx_c = col128(lam), col128(ba), col128(bx)
    ident = np.eye(128, dtype=np.float32)

    def slices(w):
        flat = f(w).reshape(NCORE, 128, NSL)
        return [f(flat[c]) for c in range(NCORE)]

    w1sl, w2sl, w3sl = slices(w1), slices(w2), slices(w3)

    in_maps = []
    for c in range(NCORE):
        mask = np.zeros((128, 2 * NCORE), dtype=np.float32)
        mask[:, 0:NCORE] = (np.arange(NCORE) < c).astype(np.float32)[None, :]
        mask[:, NCORE:] = 1.0 - mask[:, 0:NCORE]
        in_maps.append({
            "x": f(x[c * TC:(c + 1) * TC]),
            "wa_t": wa_t, "wx_t": wx_t, "w12_t": w12, "w3_t": w3_tiled,
            "lam": lam_c, "ba": ba_c, "bx": bx_c,
            "mask": mask, "ident": ident,
            "onesc": np.ones((128, 1), dtype=np.float32),
            "onesr": np.ones((1, 128), dtype=np.float32),
            "w1s": w1sl[c], "w2s": w2sl[c], "w3s": w3sl[c],
        })
    return in_maps


def kernel(**inputs):
    in_maps = _prep(inputs)
    nc = build_nc()
    res = run_bass_kernel_spmd(nc, in_maps, core_ids=list(range(NCORE)))
    out = np.concatenate([np.asarray(r["out"]) for r in res.results], axis=0)
    return out.reshape(1, S, D).astype(np.float32)


# version 18
# speedup vs baseline: 1.0482x; 1.0482x over previous
import numpy as np
import concourse.bass as bass
import concourse.mybir as mybir
from concourse.bass_utils import run_bass_kernel_spmd

F32 = mybir.dt.float32
BF16 = mybir.dt.bfloat16
I32 = mybir.dt.int32
AF = mybir.ActivationFunctionType
OP = mybir.AluOpType
AX = mybir.AxisListType.X

NCORE = 8
S, D, HID = 8192, 2048, 5632
TC = S // NCORE
NTT = TC // 128
G = D // 128
HCN = HID // 128
HSN = HID // 512
NSL = HID * D // NCORE // 128
EPS = float(np.float32(1e-5))

_CHAIN_OPS = frozenset([
    "tensor_tensor", "tensor_add", "tensor_sub", "tensor_mul", "tensor_max",
    "tensor_scalar", "tensor_single_scalar", "tensor_scalar_add",
    "tensor_scalar_sub", "tensor_scalar_mul", "tensor_scalar_max",
    "tensor_scalar_min", "tensor_copy", "tensor_relu", "tensor_reduce",
    "reduce_max", "reduce_sum", "scalar_tensor_tensor", "tensor_tensor_scan",
    "memset", "reciprocal", "select", "copy_predicated", "bn_stats",
    "activation", "copy", "add", "mul",
    "partition_all_reduce", "partition_broadcast",
])


class _Chained:

    def __init__(self, eng, sem, cnt, name):
        self._e = eng
        self._sem = sem
        self._cnt = cnt
        self._name = name

    def __getattr__(self, attr):
        real = getattr(self._e, attr)
        if attr not in _CHAIN_OPS:
            return real
        cnt, name = self._cnt, self._name

        def wrapped(*a, **k):
            if cnt[name] > 0:
                self._e.wait_ge(self._sem, cnt[name])
            inst = real(*a, **k)
            inst.then_inc(self._sem, 1)
            cnt[name] += 1
            return inst
        return wrapped


def build_nc():
    nc = bass.Bass("TRN2", target_bir_lowering=False)

    x_in = nc.dram_tensor("x", [TC, D], F32, kind="ExternalInput").ap()
    wa_t = nc.dram_tensor("wa_t", [G, 128, G * 128], F32, kind="ExternalInput").ap()
    wx_t = nc.dram_tensor("wx_t", [G, 128, G * 128], F32, kind="ExternalInput").ap()
    w12_t = nc.dram_tensor("w12_t", [2, HSN, 128, G * 512], F32,
                           kind="ExternalInput").ap()
    w3_t = nc.dram_tensor("w3_t", [G, 128, HCN * 128], F32, kind="ExternalInput").ap()
    lam_in = nc.dram_tensor("lam", [128, G], F32, kind="ExternalInput").ap()
    ba_in = nc.dram_tensor("ba", [128, G], F32, kind="ExternalInput").ap()
    bx_in = nc.dram_tensor("bx", [128, G], F32, kind="ExternalInput").ap()
    mask_in = nc.dram_tensor("mask", [128, 2 * NCORE], F32, kind="ExternalInput").ap()
    ident_in = nc.dram_tensor("ident", [128, 128], F32, kind="ExternalInput").ap()
    onesc_in = nc.dram_tensor("onesc", [128, 1], F32, kind="ExternalInput").ap()
    onesr_in = nc.dram_tensor("onesr", [1, 128], F32, kind="ExternalInput").ap()
    w1s_in = nc.dram_tensor("w1s", [128, NSL], F32, kind="ExternalInput").ap()
    w2s_in = nc.dram_tensor("w2s", [128, NSL], F32, kind="ExternalInput").ap()
    w3s_in = nc.dram_tensor("w3s", [128, NSL], F32, kind="ExternalInput").ap()

    out_dram = nc.dram_tensor("out", [TC, D], F32, kind="ExternalOutput").ap()

    a_spill = nc.dram_tensor("a_spill", [G, 128, TC], F32).ap()
    u_spill = nc.dram_tensor("u_spill", [G, 128, TC], F32).ap()
    h_park = nc.dram_tensor("h_park", [NTT, 128, HSN * 512], F32).ap()
    x1_park = nc.dram_tensor("x1_park", [NTT, 128, D], F32).ap()
    b3_park = nc.dram_tensor("b3_park", [G, 128, TC], F32).ap()
    cc_sum_in = nc.dram_tensor("cc_sum_in", [4, 1], F32).ap()
    cc_sum_out = nc.dram_tensor("cc_sum_out", [4, 1], F32).ap()
    cc_ag_in = nc.dram_tensor("cc_ag_in", [128, 2 * G], F32).ap()
    cc_ag_out = nc.dram_tensor("cc_ag_out", [NCORE * 128, 2 * G], F32).ap()

    stack = []

    def enter(h):
        t = h.__enter__()
        stack.append(h)
        return t

    scopes = {}

    def enter_sc(name, h):
        t = h.__enter__()
        scopes.setdefault(name, []).append(h)
        return t

    def free_sc(name):
        for h in reversed(scopes.pop(name, [])):
            h.__exit__(None, None, None)

    sems, cnt = {}, {}
    for n in ["sV", "sS", "sT", "sG",
              "dPRE", "dW0", "dW1", "dX0", "dX1", "dSP0", "dSP1",
              "dRL0", "dRL1", "dCC", "dHP0", "dHP1", "dHR0", "dHR1", "dHR2",
              "dHB0", "dHB1", "dHB2", "dHB3", "dHB4", "dHB5", "dHB6", "dHB7",
              "dPK", "dPK0", "dPK1", "dOUT", "dOUT2", "dTR", "dTR2", "cc"]:
        sems[n] = enter(nc.semaphore(n))
        cnt[n] = 0

    def mark(name, inst, step):
        inst.then_inc(sems[name], step)
        cnt[name] += step
        return (name, cnt[name])

    def tok(name):
        return (name, cnt[name])

    def wait(eng, *tokens):
        for t_ in tokens:
            if t_ is None:
                continue
            name, v = t_
            if v > 0:
                eng.wait_ge(sems[name], v)

    V, SC, TE, GP = "sV", "sS", "sT", "sG"

    def CH(eng, name):
        return _Chained(eng, sems[name], cnt, name)

    def te_group_wait(t):
        if cnt[TE] > 0:
            t.wait_ge(sems[TE], cnt[TE])

    ident = enter(nc.sbuf_tensor("identt", [128, 128], F32))
    lam = enter(nc.sbuf_tensor("lamt", [128, G], F32))
    bav = enter(nc.sbuf_tensor("bav", [128, G], F32))
    bxv = enter(nc.sbuf_tensor("bxv", [128, G], F32))
    coef = enter(nc.sbuf_tensor("coef", [128, G], F32))
    smallA = enter(nc.sbuf_tensor("smallA", [128, G], F32))
    maskt = enter(nc.sbuf_tensor("maskt", [128, 2 * NCORE], F32))
    wsums = enter(nc.sbuf_tensor("wsums", [128, 4], F32))
    wsall = enter(nc.sbuf_tensor("wsall", [128, 4], F32))
    onesc = enter(nc.sbuf_tensor("onesct", [128, 1], F32))
    onesr = enter(nc.sbuf_tensor("onesrt", [1, 128], F32))
    wconst = enter(nc.sbuf_tensor("wconst", [128, 16], F32))
    colAH = enter(nc.sbuf_tensor("colAH", [128, 2 * G], F32))
    gat = enter(nc.sbuf_tensor("gat", [128, NCORE * 2 * G], F32))
    Hcol = enter(nc.sbuf_tensor("Hcol", [128, G], F32))
    tmpG = enter(nc.sbuf_tensor("tmpG", [128, G], F32))
    tmpS = enter(nc.sbuf_tensor("tmpS", [128, G], F32))
    colb = enter(nc.sbuf_tensor("colb", [128, 16], F32))
    colb2 = enter(nc.sbuf_tensor("colb2", [128, 16], F32))
    bstb = enter(nc.sbuf_tensor("bstb", [128, 32], F32))
    bstb2 = enter(nc.sbuf_tensor("bstb2", [128, 32], F32))
    sinvz = enter(nc.sbuf_tensor("sinvz", [128, NTT], F32))
    c1c = enter(nc.sbuf_tensor("c1c", [128, NTT], F32))
    c2c = enter(nc.sbuf_tensor("c2c", [128, NTT], F32))
    c3c = enter(nc.sbuf_tensor("c3c", [128, NTT], F32))
    runmax = enter(nc.sbuf_tensor("runmax", [128, NTT], F32))
    shc = enter(nc.sbuf_tensor("shc", [128, NTT], F32))
    zbuf = enter(nc.sbuf_tensor("zbuf", [128, TC], F32))
    ws = {}
    wsl = enter_sc("p0", nc.sbuf_tensor("wsl", [128, NSL], F32))

    pb = [enter(nc.psum_tensor("pb%d" % k, [128, 512], F32)) for k in range(8)]

    blk = nc.Block("ph0"); blk.__enter__()
    if True:
        @blk.sync
        def _(sy):
            for dst, src in [(ident, ident_in), (lam, lam_in), (bav, ba_in),
                             (bxv, bx_in), (maskt, mask_in),
                             (onesc, onesc_in), (onesr, onesr_in)]:
                mark("dPRE", sy.dma_start(out=dst[:, :], in_=src), 16)
            mark("dPRE", sy.dma_start(out=wsl[:, :], in_=w1s_in), 16)

        @blk.scalar
        def _(s):
            s = CH(s, SC)
            wait(s, tok("dPRE"))
            s.activation(out=smallA[:, :], in_=lam[:, :], func=AF.Exp, scale=-1.0)

        @blk.vector
        def _(v):
            v = CH(v, V)
            wait(v, tok(SC), tok("dPRE"))
            v.memset(zbuf[:, :], 0.0)
            v.memset(Hcol[:, :], 0.0)
            v.memset(runmax[:, :], 0.0)
            e, t_ = smallA, coef
            v.tensor_scalar(t_[:, :], e[:, :], -0.2, 0.25, OP.mult, OP.add)
            v.scalar_tensor_tensor(t_[:, :], e[:, :], -1.0, t_[:, :], OP.mult, OP.mult)
            v.tensor_scalar_add(t_[:, :], t_[:, :], float(np.float32(1.0 / 3)))
            v.scalar_tensor_tensor(t_[:, :], e[:, :], -1.0, t_[:, :], OP.mult, OP.mult)
            v.tensor_scalar_add(t_[:, :], t_[:, :], 0.5)
            v.scalar_tensor_tensor(t_[:, :], e[:, :], -1.0, t_[:, :], OP.mult, OP.mult)
            v.tensor_scalar_add(t_[:, :], t_[:, :], 1.0)
            v.tensor_tensor(t_[:, :], e[:, :], t_[:, :], OP.mult)
            v.tensor_scalar_mul(t_[:, :], t_[:, :], -8.0)
            v.reduce_sum(wsums[:, 0:1], wsl[:, :], axis=AX, apply_absolute_value=True)

    for j, src in [(1, w2s_in), (2, w3s_in)]:
        if True:
            @blk.sync
            def _(sy, src=src):
                wait(sy, tok(V))
                mark("dPRE", sy.dma_start(out=wsl[:, :], in_=src), 16)

            @blk.vector
            def _(v, j=j):
                v = CH(v, V)
                wait(v, tok("dPRE"))
                v.reduce_sum(wsums[:, j:j + 1], wsl[:, :], axis=AX,
                             apply_absolute_value=True)

    if True:
        @blk.vector
        def _(v):
            v = CH(v, V)
            v.memset(wsums[:, 3:4], 0.0)

        @blk.tensor
        def _(t):
            wait(t, tok(V), tok("dPRE"))
            te_group_wait(t)
            mark(TE, t.matmul(pb[0][0:4, 0:1], wsums[:, 0:4], onesc[:, 0:1],
                              start=True, stop=True), 1)

        @blk.vector
        def _(v):
            v = CH(v, V)
            wait(v, tok(TE))
            v.tensor_copy(wsall[0:4, 0:1], pb[0][0:4, 0:1])

        @blk.gpsimd
        def _(g):
            wait(g, tok(V))
            mark("dCC", g.dma_start(out=cc_sum_in, in_=wsall[0:4, 0:1]), 16)
            wait(g, tok("dCC"))
            mark("cc", g.collective_compute(
                "AllReduce", OP.add, replica_groups=[list(range(NCORE))],
                ins=[cc_sum_in.opt()], outs=[cc_sum_out.opt()]), 1)
            wait(g, tok("cc"))
            mark("dCC", g.dma_start(out=wconst[0:1, 12:16],
                                    in_=cc_sum_out.rearrange("p f -> f p")), 16)

        @blk.vector
        def _(v):
            v = CH(v, V)
            wait(v, tok("dCC"))
            r0 = wconst[0:1, :]
            v.tensor_scalar(r0[:, 0:3], r0[:, 12:15], float(np.float32(1.0 / (HID * D))),
                            1e-5, OP.mult, OP.max)
            v.tensor_scalar_mul(r0[:, 3:6], r0[:, 0:3], 0.5)
            v.tensor_scalar_mul(r0[:, 6:9], r0[:, 3:6], -1.0)

        @blk.tensor
        def _(t):
            wait(t, tok(V))
            te_group_wait(t)
            mark(TE, t.matmul(pb[1][:, 0:9], onesr[0:1, :], wconst[0:1, 0:9],
                              start=True, stop=True), 1)

        @blk.vector
        def _(v):
            v = CH(v, V)
            wait(v, tok(TE))
            v.tensor_copy(wconst[:, 0:9], pb[1][:, 0:9])
    wconst_done = tok(V)

    def ln_stats(blk, src_t, c0=0, cb=None, pre=None):
        if cb is None:
            cb = colb
        bst = bstb if cb is colb else bstb2

        @blk.vector
        def _(v):
            v = CH(v, V)
            if pre is not None:
                pre(v)
            for qq in range(4):
                v.bn_stats(bst[:, 6 * qq:6 * qq + 6],
                           src_t[:, c0 + 512 * qq:c0 + 512 * qq + 512])
            v.tensor_tensor(bst[:, 24:32], bst[:, 1:24:3], bst[:, 1:24:3],
                            OP.mult)
            v.reduce_sum(cb[:, 7:8], bst[:, 1:24:3], axis=AX)
            v.reduce_sum(cb[:, 8:9], bst[:, 24:32], axis=AX)
            v.reduce_sum(cb[:, 9:10], bst[:, 2:24:3], axis=AX)
            v.tensor_scalar_mul(cb[:, 5:6], cb[:, 7:8], -0.125)
            v.tensor_tensor(cb[:, 12:13], cb[:, 5:6], cb[:, 5:6], OP.mult)
            v.scalar_tensor_tensor(cb[:, 11:12], cb[:, 8:9], 0.125,
                                   cb[:, 12:13], OP.mult, OP.subtract)
            v.tensor_scalar(cb[:, 12:13], cb[:, 9:10],
                            float(np.float32(1.0 / D)), EPS, OP.mult, OP.add)
            v.tensor_tensor(cb[:, 11:12], cb[:, 11:12], cb[:, 12:13], OP.add)
            v.reciprocal(cb[:, 2:3], cb[:, 11:12])

        @blk.scalar
        def _(s):
            s = CH(s, SC)
            wait(s, tok(V))
            s.activation(out=cb[:, 4:5], in_=cb[:, 2:3], func=AF.Sqrt)
            s.activation(out=cb[:, 6:7], in_=cb[:, 5:6], func=AF.Identity,
                         scale=cb[:, 4:5])
        return tok(SC)

    free_sc("p0")
    xnT_hi = enter_sc("p12", nc.sbuf_tensor("xnT_hi", [128, G * TC], BF16))
    xnT_lo = enter_sc("p12", nc.sbuf_tensor("xnT_lo", [128, G * TC], BF16))
    wafh = enter_sc("p12", nc.sbuf_tensor("wafh", [128, D], F32))
    wahi = [enter_sc("p12", nc.sbuf_tensor("wahi%d" % i, [128, D], BF16)) for i in range(3)]
    walo = [enter_sc("p12", nc.sbuf_tensor("walo%d" % i, [128, D], BF16)) for i in range(3)]
    wxhi = [enter_sc("p12", nc.sbuf_tensor("wxhi%d" % i, [128, D], BF16)) for i in range(3)]
    wxlo = [enter_sc("p12", nc.sbuf_tensor("wxlo%d" % i, [128, D], BF16)) for i in range(3)]

    split_done = [None] * G
    mm_done = [None] * G

    def p2_prep(j):
        q = j % 3
        for srcw, dsth, dstl in [(wa_t, wahi, walo), (wx_t, wxhi, wxlo)]:
            if True:
                @blk.sync
                def _(sy, j=j, q=q, srcw=srcw):
                    if j >= 1 and srcw is wa_t:
                        wait(sy, split_done[j - 1])
                    elif srcw is wx_t:
                        wait(sy, tok(V))
                    if j >= 3 and srcw is wa_t:
                        wait(sy, mm_done[j - 3])
                    mark("dW%d" % (j % 2), sy.dma_start(out=wafh[:, :],
                                                        in_=srcw[j]), 16)

                @blk.scalar
                def _(s, j=j, q=q, dsth=dsth):
                    s = CH(s, SC)
                    wait(s, tok("dW%d" % (j % 2)))
                    s.activation(out=dsth[q][:, :], in_=wafh[:, :], func=AF.Copy)

                @blk.vector
                def _(v, j=j, q=q, dsth=dsth, dstl=dstl):
                    v = CH(v, V)
                    wait(v, tok(SC))
                    v.tensor_tensor(dstl[q][:, :], wafh[:, :], dsth[q][:, :],
                                    OP.subtract)
        split_done[j] = tok(V)

    hi_t = enter_sc("p1x", nc.sbuf_tensor("hi_t", [128, D], BF16))
    lo_t = enter_sc("p1x", nc.sbuf_tensor("lo_t", [128, D], BF16))
    xt1 = enter_sc("p1x", nc.sbuf_tensor("xt1", [128, D], F32))
    ws["xn32"] = enter_sc("p1x", nc.sbuf_tensor("xn32a", [128, D], F32))

    xt1b = enter_sc("p1x", nc.sbuf_tensor("xt1b", [128, D], F32))
    xn1 = [ws["xn32"],
           enter_sc("p1x", nc.sbuf_tensor("xn32a2", [128, D], F32))]
    hi2 = [hi_t, enter_sc("p1x", nc.sbuf_tensor("hi_t2", [128, D], BF16))]
    lo2 = [lo_t, enter_sc("p1x", nc.sbuf_tensor("lo_t2", [128, D], BF16))]
    xtb = [xt1, xt1b]
    colb5 = [colb, colb2]

    p1_xbar_done = []
    p1_back_s = [None] * NTT
    p1_front_v = [None] * NTT

    def p1_front(tci):
        p = tci % 2
        cb = colb5[p]
        if True:
            @blk.sync
            def _(sy, tci=tci, p=p):
                if tci >= 2:
                    wait(sy, p1_back_s[tci - 2])
                mark("dX%d" % p, sy.dma_start(
                    out=xtb[p][:, :], in_=x_in[tci * 128:(tci + 1) * 128, :]), 16)

            def pre(v, p=p):
                wait(v._e, tok("dX%d" % p))

            ln_stats(blk, xtb[p], 0, cb=cb, pre=pre)
            p1_front_v[tci] = tok(V)

    def p1_back(tci):
        p = tci % 2
        cb = colb5[p]
        if True:
            @blk.scalar
            def _(s, tci=tci, p=p, cb=cb):
                s = CH(s, SC)
                wait(s, p1_front_v[tci])
                if tci >= 2:
                    wait(s, p1_xbar_done[tci - 2])
                s.activation(out=xn1[p][:, :], in_=xtb[p][:, :], func=AF.Identity,
                             bias=cb[:, 6:7], scale=cb[:, 4:5])
                p1_back_s[tci] = tok(SC)

            @blk.vector
            def _(v, tci=tci, p=p):
                v = CH(v, V)
                wait(v, tok(SC))
                v.tensor_copy(hi2[p][:, :], xn1[p][:, :])
                v.tensor_tensor(lo2[p][:, :], xn1[p][:, :], hi2[p][:, :],
                                OP.subtract)

            @blk.sync
            def _(sy, tci=tci, p=p):
                wait(sy, tok(V))
                hi3 = xnT_hi[:, :].rearrange("q (c t) -> q c t", t=TC)[
                    :, :, tci * 128:(tci + 1) * 128]
                lo3 = xnT_lo[:, :].rearrange("q (c t) -> q c t", t=TC)[
                    :, :, tci * 128:(tci + 1) * 128]
                mark("dTR2" if tci % 2 else "dTR",
                     sy.dma_start_transpose(hi3, hi2[p][:, :]), 16)
                p1_xbar_done.append(
                    mark("dTR2" if tci % 2 else "dTR",
                         sy.dma_start_transpose(lo3, lo2[p][:, :]), 16))

    p1_front(0)
    for tci in range(1, NTT):
        p1_front(tci)
        p1_back(tci - 1)
    p1_back(NTT - 1)

    for j in range(3):
        p2_prep(j)

    free_sc("p1x")
    rgt = enter_sc("p12", nc.sbuf_tensor("rgt", [128, TC], F32))
    igt = enter_sc("p12", nc.sbuf_tensor("igt", [128, TC], F32))
    agt = [enter_sc("p12", nc.sbuf_tensor("agt%d" % i, [128, TC], F32)) for i in range(2)]
    ugt = [enter_sc("p12", nc.sbuf_tensor("ugt%d" % i, [128, TC], F32)) for i in range(2)]
    xnr = enter_sc("p12", nc.sbuf_tensor("xnr", [128, TC], F32))
    gsc = enter_sc("p12", nc.sbuf_tensor("gsc", [128, TC], F32))
    y0b = enter_sc("p12", nc.sbuf_tensor("y0b", [128, TC], F32))
    afl = enter_sc("p12", nc.sbuf_tensor("afl", [128, TC], F32))

    gate_evac = [None, None]
    spill_done = [None, None]
    scan1_done = [None] * G

    for hc in range(G):
        p = hc % 2
        q = hc % 3
        if True:
            @blk.tensor
            def _(t, hc=hc, p=p, q=q):
                wait(t, split_done[hc])
                if hc == 0:
                    wait(t, tok("dTR"), tok("dTR2"), wconst_done)
                if hc >= 2:
                    wait(t, gate_evac[p])
                te_group_wait(t)
                for bank, whi, wlo in [(0, wahi[q], walo[q]),
                                       (2, wxhi[q], wxlo[q])]:
                    for half in range(2):
                        ps = pb[bank + half + 4 * p]
                        first = True
                        for wtile, xtile in [(whi, xnT_hi), (whi, xnT_lo),
                                             (wlo, xnT_hi)]:
                            for dc in range(G):
                                inst = t.matmul(
                                    ps[:, :],
                                    wtile[:, dc * 128:(dc + 1) * 128],
                                    xtile[:, dc * TC + half * 512:
                                          dc * TC + half * 512 + 512],
                                    start=first,
                                    stop=(wtile is wlo and dc == G - 1))
                                first = False
                        mark(TE, inst, 1)
                mm_done[hc] = tok(TE)

        if hc + 3 < G:
            p2_prep(hc + 3)

        if True:
            @blk.scalar
            def _(s, hc=hc, p=p):
                s = CH(s, SC)
                wait(s, mm_done[hc])
                if spill_done[p] is not None:
                    wait(s, spill_done[p])
                for half in range(2):
                    s.activation(out=rgt[:, half * 512:half * 512 + 512],
                                 in_=pb[0 + half + 4 * p][:, :], func=AF.Sigmoid,
                                 bias=bav[:, hc:hc + 1])
                for half in range(2):
                    s.activation(out=igt[:, half * 512:half * 512 + 512],
                                 in_=pb[2 + half + 4 * p][:, :], func=AF.Sigmoid,
                                 bias=bxv[:, hc:hc + 1])
                s.activation(out=agt[p][:, :], in_=rgt[:, :], func=AF.Exp,
                             scale=coef[:, hc:hc + 1])
                gate_evac[p] = tok(SC)

            @blk.vector
            def _(v, hc=hc, p=p):
                v = CH(v, V)
                wait(v, gate_evac[p])
                v.tensor_tensor(xnr[:, :],
                                xnT_hi[:, hc * TC:(hc + 1) * TC],
                                xnT_lo[:, hc * TC:(hc + 1) * TC], OP.add)
                v.tensor_tensor(xnr[:, :], igt[:, :], xnr[:, :], OP.mult)
                v.tensor_tensor(gsc[:, :], agt[p][:, :], agt[p][:, :], OP.mult)
                v.tensor_scalar(gsc[:, :], gsc[:, :], -1.0, 1.0, OP.mult, OP.add)

            @blk.scalar
            def _(s, hc=hc):
                s = CH(s, SC)
                wait(s, tok(V))
                s.activation(out=gsc[:, :], in_=gsc[:, :], func=AF.Sqrt)

            @blk.vector
            def _(v, hc=hc, p=p):
                v = CH(v, V)
                wait(v, tok(SC))
                if spill_done[p] is not None:
                    wait(v, spill_done[p])
                v.tensor_tensor(ugt[p][:, :], gsc[:, :], xnr[:, :], OP.mult)
                v.tensor_tensor_scan(y0b[:, :], agt[p][:, :], ugt[p][:, :],
                                     0.0, OP.mult, OP.add)
                v.tensor_copy(colAH[:, G + hc:G + hc + 1], y0b[:, TC - 1:TC])
                v.tensor_tensor_scan(afl[:, :], agt[p][:, :], zbuf[:, :],
                                     1.0, OP.mult, OP.add)
                v.tensor_copy(colAH[:, hc:hc + 1], afl[:, TC - 1:TC])
                scan1_done[hc] = tok(V)

            @blk.gpsimd
            def _(g, hc=hc, p=p):
                wait(g, scan1_done[hc])
                mark("dSP%d" % p, g.dma_start(out=a_spill[hc], in_=agt[p][:, :]), 16)
                spill_done[p] = mark("dSP%d" % p,
                                     g.dma_start(out=u_spill[hc], in_=ugt[p][:, :]),
                                     16)

    if True:
        @blk.gpsimd
        def _(g):
            wait(g, scan1_done[G - 2], scan1_done[G - 1])
            mark("dCC", g.dma_start(out=cc_ag_in, in_=colAH[:, :]), 16)
            wait(g, tok("dCC"))
            mark("cc", g.collective_compute(
                "AllGather", OP.bypass, replica_groups=[list(range(NCORE))],
                ins=[cc_ag_in.opt()], outs=[cc_ag_out.opt()]), 1)
            wait(g, tok("cc"))
            mark("dCC", g.dma_start(
                out=gat[:, :],
                in_=cc_ag_out.rearrange("(r p) j -> p r j", p=128)), 16)

        @blk.vector
        def _(v):
            v = CH(v, V)
            wait(v, tok("dCC"))
            for k in range(NCORE - 1):
                base = k * 2 * G
                v.tensor_scalar(tmpG[:, :], gat[:, base:base + G],
                                maskt[:, k:k + 1],
                                maskt[:, NCORE + k:NCORE + k + 1],
                                OP.mult, OP.add)
                v.tensor_single_scalar(tmpS[:, :], gat[:, base + G:base + 2 * G],
                                       maskt[:, k:k + 1], OP.mult)
                v.tensor_tensor(Hcol[:, :], Hcol[:, :], tmpG[:, :], OP.mult)
                v.tensor_tensor(Hcol[:, :], Hcol[:, :], tmpS[:, :], OP.add)
    fold_done = tok(V)

    free_sc("p12")
    zqT = enter_sc("zq", nc.sbuf_tensor("zqT", [128, G * TC], BF16))
    yT = enter_sc("yt", nc.sbuf_tensor("yT", [128, NTT * D], F32))
    arl = [enter_sc("p4", nc.sbuf_tensor("arl%d" % i, [128, TC], F32)) for i in range(2)]
    url = [enter_sc("p4", nc.sbuf_tensor("url%d" % i, [128, TC], F32)) for i in range(2)]
    ybuf = [enter_sc("p4", nc.sbuf_tensor("ybuf%d" % i, [128, TC], F32)) for i in range(2)]

    scan2_done = [None, None]
    tr_done = [None, None]
    evac_done = [None, None]
    for hc in range(G):
        p = hc % 2
        if True:
            @blk.sync
            def _(sy, hc=hc, p=p):
                if hc == 0:
                    wait(sy, tok("dSP0"), tok("dSP1"))
                if hc >= 2:
                    wait(sy, scan2_done[p])
                mark("dRL%d" % p, sy.dma_start(out=arl[p][:, :], in_=a_spill[hc]), 16)
                mark("dRL%d" % p, sy.dma_start(out=url[p][:, :], in_=u_spill[hc]), 16)

            @blk.vector
            def _(v, hc=hc, p=p):
                v = CH(v, V)
                wait(v, tok("dRL%d" % p), fold_done)
                if hc >= 2:
                    wait(v, tr_done[p])
                v.tensor_tensor_scan(ybuf[p][:, :], arl[p][:, :], url[p][:, :],
                                     Hcol[:, hc:hc + 1], OP.mult, OP.add)
                scan2_done[p] = tok(V)

            @blk.tensor
            def _(t, hc=hc, p=p):
                wait(t, scan2_done[p])
                if hc >= 2:
                    wait(t, evac_done[p])
                te_group_wait(t)
                for tb in range(NTT):
                    inst = t.transpose(
                        pb[4 * p + tb // 4][:, (tb % 4) * 128:(tb % 4) * 128 + 128],
                        ybuf[p][:, tb * 128:(tb + 1) * 128], ident[:, :])
                    mark(TE, inst, 1)
                tr_done[p] = tok(TE)

            @blk.vector
            def _(v, hc=hc, p=p):
                v = CH(v, V)
                wait(v, tr_done[p])
                for b in range(2):
                    out3 = yT[:, :].rearrange("p (tb d) -> p tb d", d=D)[
                        :, 4 * b:4 * b + 4, hc * 128:(hc + 1) * 128]
                    v.tensor_copy(out3, pb[4 * p + b][:, :].rearrange(
                        "p (tb q) -> p tb q", q=128))
                evac_done[p] = tok(V)

    free_sc("p4")
    xt = [enter_sc("p5", nc.sbuf_tensor("xtp5_%d" % i, [128, D], F32))
          for i in range(2)]
    cent = [enter_sc("p5", nc.sbuf_tensor("cent%d" % i, [128, D], F32))
            for i in range(2)]
    x15 = [enter_sc("p5", nc.sbuf_tensor("x15_%d" % i, [128, D], F32))
           for i in range(2)]
    it2 = enter_sc("p5", nc.sbuf_tensor("it2", [128, D], I32))
    ft2 = [enter_sc("p5", nc.sbuf_tensor("ft2_%d" % i, [128, D], F32))
           for i in range(2)]
    ds2 = enter_sc("p5", nc.sbuf_tensor("ds2", [128, D], F32))
    zqt_sb = [enter_sc("p5", nc.sbuf_tensor("zqt_sb%d" % i, [128, D], BF16))
              for i in range(2)]
    colb5 = [colb, colb2]

    p5_done = []
    p5_back_s = [None] * NTT
    p5_park = [None] * NTT
    p5_vb = [None] * NTT
    p5_vc = [None] * NTT
    p5_sb = [None] * NTT
    p5_vd = [None] * NTT
    p5_bb = [None] * NTT

    def p5_sy0(t):
        p = t % 2

        @blk.sync
        def _(sy, t=t, p=p):
            if t < 2:
                wait(sy, scan2_done[0], scan2_done[1],
                     tr_done[0], tr_done[1])
            if t >= 2:
                wait(sy, p5_done[t - 2])
            mark("dX%d" % p, sy.dma_start(
                out=xt[p][:, :], in_=x_in[t * 128:(t + 1) * 128, :]), 16)

    def p5_va(t):
        p = t % 2

        def pre5(v, t=t, p=p):
            if t == 0:
                wait(v._e, evac_done[0], evac_done[1])
            if t >= 2 and p5_back_s[t - 2] is not None:
                wait(v._e, p5_back_s[t - 2])
        return ln_stats(blk, yT, t * D, cb=colb5[p], pre=pre5)

    def p5_vb_f(t, lnA_s):
        p = t % 2
        cb = colb5[p]

        @blk.vector
        def _(v, t=t, p=p, cb=cb, lnA_s=lnA_s):
            v = CH(v, V)
            wait(v, tok("dX%d" % p), lnA_s)
            v.tensor_scalar(x15[p][:, :], yT[:, t * D:(t + 1) * D],
                            cb[:, 4:5], cb[:, 6:7], OP.mult, OP.add)
            v.tensor_tensor(x15[p][:, :], x15[p][:, :], xt[p][:, :], OP.add)
        p5_vb[t] = tok(V)

    def p5_sy1(t):
        p = t % 2

        @blk.sync
        def _(sy, t=t, p=p):
            wait(sy, p5_vb[t])
            p5_park[t] = mark("dPK%d" % p, sy.dma_start(
                out=x1_park[t], in_=x15[p][:, :]), 16)

    def p5_vc_f(t):
        p = t % 2
        cb = colb5[p]
        bst = bstb if cb is colb else bstb2

        @blk.vector
        def _(v, t=t, p=p, cb=cb, bst=bst):
            v = CH(v, V)
            for qq in range(4):
                v.bn_stats(bst[:, 6 * qq:6 * qq + 6],
                           x15[p][:, 512 * qq:512 * qq + 512])
            v.tensor_tensor(bst[:, 24:32], bst[:, 1:24:3], bst[:, 1:24:3],
                            OP.mult)
            v.reduce_sum(cb[:, 0:1], bst[:, 1:24:3], axis=AX)
            v.reduce_sum(cb[:, 1:2], bst[:, 24:32], axis=AX)
            v.reduce_sum(cb[:, 3:4], bst[:, 2:24:3], axis=AX)
            v.tensor_scalar_mul(cb[:, 5:6], cb[:, 0:1], -0.125)
            v.tensor_tensor(cb[:, 12:13], cb[:, 5:6], cb[:, 5:6], OP.mult)
            v.scalar_tensor_tensor(cb[:, 11:12], cb[:, 1:2], 0.125,
                                   cb[:, 12:13], OP.mult, OP.subtract)
            v.tensor_scalar(cb[:, 13:14], cb[:, 3:4],
                            float(np.float32(1.0 / D)), EPS, OP.mult, OP.add)
            v.tensor_tensor(cb[:, 11:12], cb[:, 11:12], cb[:, 13:14], OP.add)
            v.reciprocal(cb[:, 2:3], cb[:, 11:12])
            v.tensor_single_scalar(cent[p][:, :], x15[p][:, :], cb[:, 5:6],
                                   OP.add)
            v.reduce_max(cb[:, 7:8], cent[p][:, :], axis=AX,
                         apply_absolute_value=True)
        p5_vc[t] = tok(V)

    def p5_sb_f(t):
        cb = colb5[t % 2]

        @blk.scalar
        def _(s, t=t, cb=cb):
            s = CH(s, SC)
            wait(s, p5_vc[t])
            s.activation(out=cb[:, 4:5], in_=cb[:, 2:3], func=AF.Sqrt)
        p5_sb[t] = tok(SC)

    def p5_vd_f(t):
        cb = colb5[t % 2]

        @blk.vector
        def _(v, t=t, cb=cb):
            v = CH(v, V)
            wait(v, p5_sb[t])
            v.tensor_tensor(cb[:, 7:8], cb[:, 7:8], cb[:, 4:5], OP.mult)
            v.tensor_scalar_max(cb[:, 7:8], cb[:, 7:8], 1e-5)
            v.reciprocal(cb[:, 8:9], cb[:, 7:8])
            v.tensor_scalar_mul(cb[:, 8:9], cb[:, 8:9], 127.0)
            v.reciprocal(sinvz[:, t:t + 1], cb[:, 8:9])
            v.tensor_tensor(cb[:, 9:10], cb[:, 4:5], cb[:, 8:9], OP.mult)
            v.scalar_tensor_tensor(c1c[:, t:t + 1], sinvz[:, t:t + 1], -1.0,
                                   wconst[:, 0:1], OP.mult, OP.mult)
            v.scalar_tensor_tensor(c2c[:, t:t + 1], sinvz[:, t:t + 1], -1.0,
                                   wconst[:, 1:2], OP.mult, OP.mult)
        p5_vd[t] = tok(V)

    def p5_ba(t):
        p = t % 2
        cb = colb5[p]

        @blk.scalar
        def _(s, t=t, p=p, cb=cb):
            s = CH(s, SC)
            wait(s, p5_vd[t], p5_park[t])
            s.activation(out=x15[p][:, :], in_=cent[p][:, :], func=AF.Identity,
                         scale=cb[:, 9:10])
            s.copy(it2[:, :], x15[p][:, :])
            s.copy(ft2[p][:, :], it2[:, :])
        p5_back_s[t] = tok(SC)

    def p5_bb_f(t):
        p = t % 2

        @blk.vector
        def _(v, t=t, p=p):
            v = CH(v, V)
            wait(v, p5_back_s[t])
            if t >= 2:
                wait(v, p5_done[t - 2])
            v.tensor_tensor(ds2[:, :], x15[p][:, :], ft2[p][:, :], OP.subtract)
            v.scalar_tensor_tensor(ft2[p][:, :], ds2[:, :], 0.5, ft2[p][:, :],
                                   OP.is_gt, OP.add)
            v.scalar_tensor_tensor(zqt_sb[p][:, :], ds2[:, :], -0.5,
                                   ft2[p][:, :], OP.is_lt, OP.subtract)
        p5_bb[t] = tok(V)

    def p5_sy2(t):
        p = t % 2

        @blk.sync
        def _(sy, t=t, p=p):
            wait(sy, p5_bb[t])
            z3 = zqT[:, :].rearrange("q (c t) -> q c t", t=TC)[
                :, :, t * 128:(t + 1) * 128]
            p5_done.append(mark("dTR2" if t % 2 else "dTR",
                                sy.dma_start_transpose(z3, zqt_sb[p][:, :]),
                                16))

    p5_sy0(0)
    p5_sy0(1)
    p5_vb_f(0, p5_va(0))
    p5_sy1(0)
    p5_vc_f(0)
    p5_sb_f(0)
    p5_vd_f(0)
    for i in range(NTT):
        if i + 1 < NTT:
            p5_vb_f(i + 1, p5_va(i + 1))
            p5_ba(i)
            p5_sy1(i + 1)
            p5_vc_f(i + 1)
            p5_sb_f(i + 1)
            p5_bb_f(i)
            p5_sy2(i)
            if i + 2 < NTT:
                p5_sy0(i + 2)
            p5_vd_f(i + 1)
        else:
            p5_ba(i)
            p5_bb_f(i)
            p5_sy2(i)

    free_sc("p5")
    free_sc("yt")
    w12fh = [enter_sc("p6", nc.sbuf_tensor("w12fh%d" % i, [128, G * 256], F32))
             for i in range(2)]
    s1bh = [enter_sc("p6", nc.sbuf_tensor("s1bh%d" % i, [128, G * 256], BF16))
            for i in range(2)]
    w1q = [enter_sc("p6", nc.sbuf_tensor("w1q%d" % i, [128, G * 512], BF16))
           for i in range(3)]
    w2q = [enter_sc("p6", nc.sbuf_tensor("w2q%d" % i, [128, G * 512], BF16))
           for i in range(3)]
    silu_b = enter_sc("p6", nc.sbuf_tensor("silu_b", [128, 512], F32))
    hblk = [enter_sc("p6", nc.sbuf_tensor("hblk%d" % i, [128, 512], F32))
            for i in range(4)]

    tern_done = [None] * HSN
    p6_piece_done = [None, None]
    b12_done = {}
    silu_done = {}
    hdone = {}
    hpark = {}
    HW2 = G * 256

    def p6_tern(j):
        q3 = j % 3
        for pc in range(4):
            wi, half = pc // 2, pc % 2
            wpar = (4 * j + pc) % 2
            dstq = [w1q, w2q][wi]
            nthc = [wconst[:, 6:7], wconst[:, 7:8]][wi]
            thc_ = [wconst[:, 3:4], wconst[:, 4:5]][wi]
            if True:
                @blk.sync
                def _(sy, j=j, pc=pc, wi=wi, half=half, wpar=wpar):
                    if j == 0 and pc == 0:
                        wait(sy, tok("dTR"), tok("dTR2"), tok(V))
                    if p6_piece_done[wpar] is not None:
                        wait(sy, p6_piece_done[wpar])
                    if pc == 0 and j >= 3:
                        wait(sy, b12_done[(j - 3, NTT - 1)])
                    mark("dW%d" % wpar, sy.dma_start(
                        out=w12fh[wpar][:, :],
                        in_=w12_t[wi, j][:, half * HW2:(half + 1) * HW2]), 16)

                @blk.vector
                def _(v, j=j, q3=q3, pc=pc, wi=wi, half=half, wpar=wpar,
                      dstq=dstq, thc_=thc_, nthc=nthc):
                    v = CH(v, V)
                    wait(v, tok("dW%d" % wpar))
                    v.tensor_single_scalar(s1bh[wpar][:, :], w12fh[wpar][:, :],
                                           thc_, OP.is_gt)
                    v.tensor_single_scalar(
                        dstq[q3][:, half * HW2:(half + 1) * HW2],
                        w12fh[wpar][:, :], nthc, OP.is_lt)
                    v.tensor_tensor(dstq[q3][:, half * HW2:(half + 1) * HW2],
                                    s1bh[wpar][:, :],
                                    dstq[q3][:, half * HW2:(half + 1) * HW2],
                                    OP.subtract)
                    p6_piece_done[wpar] = tok(V)
        tern_done[j] = p6_piece_done[(4 * j + 3) % 2]

    for j in range(3):
        p6_tern(j)

    for hs in range(HSN):
        q3 = hs % 3
        for tci in range(NTT):
            idx = hs * NTT + tci
            bsel = 2 * (tci % 4)
            if True:
                @blk.tensor
                def _(t, hs=hs, q3=q3, tci=tci, idx=idx, bsel=bsel):
                    wait(t, tern_done[hs])
                    if hs == 0:
                        wait(t, p5_done[tci])
                    if idx >= 4:
                        wait(t, hdone[idx - 4])
                    te_group_wait(t)
                    for bank, wq in [(bsel + 0, w1q[q3]), (bsel + 1, w2q[q3])]:
                        for dc in range(G):
                            inst = t.matmul(
                                pb[bank][:, :],
                                zqT[:, dc * TC + tci * 128:dc * TC + (tci + 1) * 128],
                                wq[:, dc * 512:(dc + 1) * 512],
                                start=(dc == 0), stop=(dc == G - 1))
                        mark(TE, inst, 1)
                    b12_done[(hs, tci)] = tok(TE)

                @blk.scalar
                def _(s, hs=hs, tci=tci, idx=idx, bsel=bsel):
                    s = CH(s, SC)
                    wait(s, b12_done[(hs, tci)])
                    if idx >= 1:
                        wait(s, hdone[idx - 1])
                    s.activation(out=silu_b[:, :], in_=pb[bsel][:, :], func=AF.Silu,
                                 scale=c1c[:, tci:tci + 1])
                    silu_done[idx] = tok(SC)

                @blk.vector
                def _(v, hs=hs, tci=tci, idx=idx, bsel=bsel):
                    v = CH(v, V)
                    wait(v, silu_done[idx])
                    if idx >= 4:
                        wait(v, hpark[idx - 4])
                    hb = hblk[tci % 4]
                    v.scalar_tensor_tensor(hb[:, :], pb[bsel + 1][:, :],
                                           c2c[:, tci:tci + 1], silu_b[:, :],
                                           OP.mult, OP.mult)
                    v.reduce_max(colb[:, 10:11], hb[:, :], axis=AX,
                                 apply_absolute_value=True)
                    v.tensor_tensor(runmax[:, tci:tci + 1], runmax[:, tci:tci + 1],
                                    colb[:, 10:11], OP.max)
                    hdone[idx] = tok(V)

                @blk.gpsimd
                def _(g, hs=hs, tci=tci, idx=idx):
                    wait(g, hdone[idx])
                    hpark[idx] = mark("dHB%d" % (tci % 4), g.dma_start(
                        out=h_park[tci][:, hs * 512:(hs + 1) * 512],
                        in_=hblk[tci % 4][:, :]), 16)

            if tci == NTT - 1 and hs + 3 < HSN:
                p6_tern(hs + 3)

    if True:
        @blk.vector
        def _(v):
            v = CH(v, V)
            wait(v, hdone[HSN * NTT - 1])
            v.tensor_scalar_max(shc[:, :], runmax[:, :], 1e-5)
            v.reciprocal(shc[:, :], shc[:, :])
            v.tensor_scalar_mul(shc[:, :], shc[:, :], 127.0)
            v.reciprocal(c3c[:, :], shc[:, :])
            v.tensor_single_scalar(c3c[:, :], c3c[:, :], wconst[:, 2:3], OP.mult)
    hscale_done = tok(V)

    free_sc("p6")
    free_sc("zq")
    HQRT = 3 * 512
    hqT = enter_sc("hq", nc.sbuf_tensor("hqT", [128, HCN * TC], BF16))
    hrow = [enter_sc("p7a", nc.sbuf_tensor("hrow%d" % i, [128, HQRT], F32))
            for i in range(3)]
    ht_t = [enter_sc("p7a", nc.sbuf_tensor("ht_t%d" % i, [128, HQRT], F32))
            for i in range(3)]
    it7 = [enter_sc("p7a", nc.sbuf_tensor("it7%d" % i, [128, HQRT], I32))
           for i in range(3)]
    ft7 = [enter_sc("p7a", nc.sbuf_tensor("ft7%d" % i, [128, HQRT], F32))
           for i in range(3)]
    hq_sb = [enter_sc("p7a", nc.sbuf_tensor("hq_sb%d" % i, [128, HID], BF16))
             for i in range(2)]

    p7a_done = []
    piece_done = []
    v1_tok = {}
    sc_tok = {}
    P7PC = [(0, 3), (3, 6), (6, 9), (9, HSN)]
    PIECES = [(tci, hs0, hs1) for tci in range(NTT) for (hs0, hs1) in P7PC]
    NP7 = len(PIECES)

    def p7a_sync(k):
        tci, hs0, hs1 = PIECES[k]
        sz, hp = (hs1 - hs0) * 512, k % 3

        @blk.sync
        def _(sy, tci=tci, hs0=hs0, hs1=hs1, sz=sz, hp=hp, k=k):
            if k == 0:
                for j in range(4):
                    wait(sy, tok("dHB%d" % j))
            if k >= 3:
                wait(sy, piece_done[k - 3])
            mark("dHR%d" % hp, sy.dma_start(
                out=hrow[hp][:, 0:sz],
                in_=h_park[tci][:, hs0 * 512:hs1 * 512]), 16)

    def p7a_v1(k):
        tci = PIECES[k][0]
        sz, hp = (PIECES[k][2] - PIECES[k][1]) * 512, k % 3

        @blk.vector
        def _(v, tci=tci, sz=sz, hp=hp):
            v = CH(v, V)
            wait(v, tok("dHR%d" % hp), hscale_done)
            v.tensor_single_scalar(ht_t[hp][:, 0:sz], hrow[hp][:, 0:sz],
                                   shc[:, tci:tci + 1], OP.mult)
        v1_tok[k] = tok(V)

    def p7a_sc(k):
        sz, hp = (PIECES[k][2] - PIECES[k][1]) * 512, k % 3

        @blk.scalar
        def _(s, sz=sz, hp=hp, k=k):
            s = CH(s, SC)
            wait(s, v1_tok[k])
            s.copy(it7[hp][:, 0:sz], ht_t[hp][:, 0:sz])
            s.copy(ft7[hp][:, 0:sz], it7[hp][:, 0:sz])
        sc_tok[k] = tok(SC)

    def p7a_v2(k):
        tci, hs0, hs1 = PIECES[k]
        off, sz, hp = hs0 * 512, (hs1 - hs0) * 512, k % 3

        @blk.vector
        def _(v, tci=tci, off=off, sz=sz, hp=hp, k=k):
            v = CH(v, V)
            wait(v, sc_tok[k])
            if len(p7a_done) >= 2:
                wait(v, p7a_done[-2])
            v.tensor_tensor(hrow[hp][:, 0:sz], ht_t[hp][:, 0:sz],
                            ft7[hp][:, 0:sz], OP.subtract)
            v.scalar_tensor_tensor(ft7[hp][:, 0:sz], hrow[hp][:, 0:sz],
                                   0.5, ft7[hp][:, 0:sz],
                                   OP.is_gt, OP.add)
            v.scalar_tensor_tensor(hq_sb[tci % 2][:, off:off + sz],
                                   hrow[hp][:, 0:sz], -0.5,
                                   ft7[hp][:, 0:sz],
                                   OP.is_lt, OP.subtract)
            piece_done.append(tok(V))

        if k % 4 == 3:
            @blk.sync
            def _(sy, tci=tci):
                wait(sy, piece_done[-1])
                h3 = hqT[:, :].rearrange("q (c t) -> q c t", t=TC)[
                    :, :, tci * 128:(tci + 1) * 128]
                p7a_done.append(mark("dTR2" if tci % 2 else "dTR",
                                     sy.dma_start_transpose(h3,
                                                            hq_sb[tci % 2][:, :]),
                                     16))

    p7a_sync(0)
    p7a_sync(1)
    p7a_sync(2)
    p7a_v1(0)
    p7a_v1(1)
    p7a_sc(0)
    for i in range(1, NP7):
        if i + 1 < NP7:
            p7a_v1(i + 1)
        p7a_v2(i - 1)
        if i + 2 < NP7:
            p7a_sync(i + 2)
        p7a_sc(i)
    p7a_v2(NP7 - 1)

    free_sc("p7a")
    w3fh = [enter_sc("p7b", nc.sbuf_tensor("w3fh%d" % i, [128, HID // 2], F32))
            for i in range(2)]
    s3bh = [enter_sc("p7b", nc.sbuf_tensor("s3bh%d" % i, [128, HID // 2], BF16))
            for i in range(2)]
    w3q = [enter_sc("p7b", nc.sbuf_tensor("w3q%d" % i, [128, HID], BF16))
           for i in range(3)]
    b3raw = [enter_sc("p7b", nc.sbuf_tensor("b3raw%d" % i, [128, TC], F32))
             for i in range(2)]
    b3strip = [enter_sc("p7b", nc.sbuf_tensor("b3strip%d" % i, [128, TC], F32))
               for i in range(2)]

    tern3_done = [None] * G
    p7b_piece_done = [None, None]
    b3mm_done = [None] * G
    b3ev_done = [None] * G
    b3tr_done = [None] * G
    b3st_done = [None] * G
    b3pk_done = [None] * G
    H2 = HID // 2

    def p7b_tern(j):
        q3 = j % 3
        for half in range(2):
            wpar = (2 * j + half) % 2
            if True:
                @blk.sync
                def _(sy, j=j, half=half, wpar=wpar):
                    if j == 0 and half == 0:
                        wait(sy, tok(V), tok("dTR"), tok("dTR2"))
                    if p7b_piece_done[wpar] is not None:
                        wait(sy, p7b_piece_done[wpar])
                    if half == 0 and j >= 3:
                        wait(sy, b3mm_done[j - 3])
                    mark("dW%d" % wpar, sy.dma_start(
                        out=w3fh[wpar][:, :],
                        in_=w3_t[j][:, half * H2:(half + 1) * H2]), 16)

                @blk.vector
                def _(v, j=j, q3=q3, half=half, wpar=wpar):
                    v = CH(v, V)
                    wait(v, tok("dW%d" % wpar))
                    v.tensor_single_scalar(s3bh[wpar][:, :], w3fh[wpar][:, :],
                                           wconst[:, 5:6], OP.is_gt)
                    v.scalar_tensor_tensor(w3q[q3][:, half * H2:(half + 1) * H2],
                                           w3fh[wpar][:, :], wconst[:, 8:9],
                                           s3bh[wpar][:, :],
                                           OP.is_lt, OP.subtract)
                    p7b_piece_done[wpar] = tok(V)
        tern3_done[j] = p7b_piece_done[(2 * j + 1) % 2]

    for j in range(3):
        p7b_tern(j)

    for dc in range(G):
        p = dc % 2
        q3 = dc % 3
        if True:
            @blk.tensor
            def _(t, dc=dc, p=p, q3=q3):
                wait(t, tern3_done[dc])
                if dc == 0:
                    wait(t, p7a_done[-1])
                if dc >= 2:
                    wait(t, b3ev_done[dc - 2])
                te_group_wait(t)
                for half in range(2):
                    ps = pb[2 * p + half]
                    for hcc in range(HCN):
                        inst = t.matmul(
                            ps[:, :],
                            w3q[q3][:, hcc * 128:(hcc + 1) * 128],
                            hqT[:, hcc * TC + half * 512:hcc * TC + half * 512 + 512],
                            start=(hcc == 0), stop=(hcc == HCN - 1))
                    mark(TE, inst, 1)
                b3mm_done[dc] = tok(TE)

        if dc + 3 < G:
            p7b_tern(dc + 3)

        if True:
            @blk.vector
            def _(v, dc=dc, p=p):
                v = CH(v, V)
                wait(v, b3mm_done[dc])
                if dc >= 2:
                    wait(v, b3tr_done[dc - 2])
                for half in range(2):
                    v.tensor_copy(b3raw[p][:, half * 512:half * 512 + 512],
                                  pb[2 * p + half][:, :])
                b3ev_done[dc] = tok(V)

            @blk.tensor
            def _(t, dc=dc, p=p):
                wait(t, b3ev_done[dc])
                if dc >= 2:
                    wait(t, b3st_done[dc - 2])
                te_group_wait(t)
                for tb in range(NTT):
                    inst = t.transpose(
                        pb[4 + 2 * p + tb // 4][:, (tb % 4) * 128:(tb % 4) * 128 + 128],
                        b3raw[p][:, tb * 128:(tb + 1) * 128], ident[:, :])
                    mark(TE, inst, 1)
                b3tr_done[dc] = tok(TE)

            @blk.vector
            def _(v, dc=dc):
                v = CH(v, V)
                wait(v, b3tr_done[dc])
                if dc >= 2:
                    wait(v, b3pk_done[dc - 2])
                for b in range(2):
                    v.tensor_copy(b3strip[p][:, b * 512:b * 512 + 512],
                                  pb[4 + 2 * p + b][:, :])
                b3st_done[dc] = tok(V)

            @blk.sync
            def _(sy, dc=dc, p=p):
                wait(sy, b3st_done[dc])
                b3pk_done[dc] = mark("dPK%d" % p, sy.dma_start(
                    out=b3_park[dc], in_=b3strip[p][:, :]), 16)

    free_sc("p7b")
    free_sc("hq")
    xt = [enter(nc.sbuf_tensor("xtp8_%d" % i, [128, D], F32)) for i in range(2)]
    ws["xn32"] = enter(nc.sbuf_tensor("xn32c", [128, D], F32))
    b3t = [enter(nc.sbuf_tensor("b3t%d" % i, [128, D], F32)) for i in range(2)]
    ot8 = [ws["xn32"], enter(nc.sbuf_tensor("xn32c2", [128, D], F32))]
    colb5b = [colb, colb2]

    p8_out = [None] * NTT
    p8_front_v = [None] * NTT
    p8_back_v = [None] * NTT

    def p8_front(tci):
        p = tci % 2
        cb = colb5b[p]
        if True:
            @blk.sync
            def _(sy, tci=tci, p=p):
                if tci == 0:
                    wait(sy, tok("dPK"), tok("dPK0"), tok("dPK1"))
                if tci >= 2:
                    wait(sy, p8_back_v[tci - 2])
                mark("dX%d" % p, sy.dma_start(
                    out=b3t[p][:, :],
                    in_=b3_park[:, :, tci * 128:(tci + 1) * 128].rearrange(
                        "dc p q -> p dc q")), 16)
                mark("dX%d" % p, sy.dma_start(out=xt[p][:, :], in_=x1_park[tci]), 16)

            def pre8(v, tci=tci, p=p):
                wait(v._e, tok("dX%d" % p))
                v.tensor_single_scalar(b3t[p][:, :], b3t[p][:, :],
                                       c3c[:, tci:tci + 1], OP.mult)

            p8_front_v[tci] = ln_stats(blk, b3t[p], 0, cb=cb, pre=pre8)

    def p8_back(tci):
        p = tci % 2
        cb = colb5b[p]
        if True:
            @blk.vector
            def _(v, tci=tci, p=p, cb=cb):
                v = CH(v, V)
                wait(v, p8_front_v[tci])
                if tci >= 2:
                    wait(v, p8_out[tci - 2])
                v.tensor_scalar(ot8[p][:, :], b3t[p][:, :], cb[:, 4:5],
                                cb[:, 6:7], OP.mult, OP.add)
                v.tensor_tensor(ot8[p][:, :], ot8[p][:, :], xt[p][:, :], OP.add)
                p8_back_v[tci] = tok(V)

            @blk.sync
            def _(sy, tci=tci, p=p):
                wait(sy, p8_back_v[tci])
                p8_out[tci] = mark("dOUT2" if p else "dOUT", sy.dma_start(
                    out=out_dram[tci * 128:(tci + 1) * 128, :],
                    in_=ot8[p][:, :]), 16)

    p8_front(0)
    for tci in range(1, NTT):
        p8_front(tci)
        p8_back(tci - 1)
    p8_back(NTT - 1)

    if True:
        @blk.gpsimd
        def _(g):
            wait(g, tok("dOUT"), tok("dOUT2"))

    blk.__exit__(None, None, None)
    for h in reversed(stack):
        h.__exit__(None, None, None)
    return nc


def _prep(inputs):
    f = np.ascontiguousarray
    x = np.asarray(inputs["L0_attn_x"], dtype=np.float32).reshape(S, D)
    Wa = np.asarray(inputs["L0_attn_Wa"], dtype=np.float32)
    Wx = np.asarray(inputs["L0_attn_Wx"], dtype=np.float32)
    ba = np.asarray(inputs["L0_attn_ba"], dtype=np.float32)
    bx = np.asarray(inputs["L0_attn_bx"], dtype=np.float32)
    lam = np.asarray(inputs["L0_attn_Lam"], dtype=np.float32)
    w1 = np.asarray(inputs["L0_ffn_w1"], dtype=np.float32)
    w2 = np.asarray(inputs["L0_ffn_w2"], dtype=np.float32)
    w3 = np.asarray(inputs["L0_ffn_w3"], dtype=np.float32)

    wa_t = f(Wa.T.reshape(G, 128, G, 128).transpose(2, 1, 0, 3)
             .reshape(G, 128, G * 128))
    wx_t = f(Wx.T.reshape(G, 128, G, 128).transpose(2, 1, 0, 3)
             .reshape(G, 128, G * 128))
    w12 = f(np.stack([
        w1.T.reshape(G, 128, HSN, 512).transpose(2, 1, 0, 3).reshape(HSN, 128, G * 512),
        w2.T.reshape(G, 128, HSN, 512).transpose(2, 1, 0, 3).reshape(HSN, 128, G * 512),
    ]))
    w3_tiled = f(w3.T.reshape(HCN, 128, G, 128).transpose(2, 1, 0, 3)
                 .reshape(G, 128, HCN * 128))

    def col128(vec):
        return f(vec.reshape(G, 128).T)

    lam_c, ba_c, bx_c = col128(lam), col128(ba), col128(bx)
    ident = np.eye(128, dtype=np.float32)

    def slices(w):
        flat = f(w).reshape(NCORE, 128, NSL)
        return [f(flat[c]) for c in range(NCORE)]

    w1sl, w2sl, w3sl = slices(w1), slices(w2), slices(w3)

    in_maps = []
    for c in range(NCORE):
        mask = np.zeros((128, 2 * NCORE), dtype=np.float32)
        mask[:, 0:NCORE] = (np.arange(NCORE) < c).astype(np.float32)[None, :]
        mask[:, NCORE:] = 1.0 - mask[:, 0:NCORE]
        in_maps.append({
            "x": f(x[c * TC:(c + 1) * TC]),
            "wa_t": wa_t, "wx_t": wx_t, "w12_t": w12, "w3_t": w3_tiled,
            "lam": lam_c, "ba": ba_c, "bx": bx_c,
            "mask": mask, "ident": ident,
            "onesc": np.ones((128, 1), dtype=np.float32),
            "onesr": np.ones((1, 128), dtype=np.float32),
            "w1s": w1sl[c], "w2s": w2sl[c], "w3s": w3sl[c],
        })
    return in_maps


def kernel(**inputs):
    in_maps = _prep(inputs)
    nc = build_nc()
    res = run_bass_kernel_spmd(nc, in_maps, core_ids=list(range(NCORE)))
    out = np.concatenate([np.asarray(r["out"]) for r in res.results], axis=0)
    return out.reshape(1, S, D).astype(np.float32)


# version 19
# speedup vs baseline: 1.0763x; 1.0268x over previous
import numpy as np
import concourse.bass as bass
import concourse.mybir as mybir
from concourse.bass_utils import run_bass_kernel_spmd

F32 = mybir.dt.float32
BF16 = mybir.dt.bfloat16
I32 = mybir.dt.int32
AF = mybir.ActivationFunctionType
OP = mybir.AluOpType
AX = mybir.AxisListType.X

NCORE = 8
S, D, HID = 8192, 2048, 5632
TC = S // NCORE
NTT = TC // 128
G = D // 128
HCN = HID // 128
HSN = HID // 512
NSL = HID * D // NCORE // 128
EPS = float(np.float32(1e-5))

_CHAIN_OPS = frozenset([
    "tensor_tensor", "tensor_add", "tensor_sub", "tensor_mul", "tensor_max",
    "tensor_scalar", "tensor_single_scalar", "tensor_scalar_add",
    "tensor_scalar_sub", "tensor_scalar_mul", "tensor_scalar_max",
    "tensor_scalar_min", "tensor_copy", "tensor_relu", "tensor_reduce",
    "reduce_max", "reduce_sum", "scalar_tensor_tensor", "tensor_tensor_scan",
    "memset", "reciprocal", "select", "copy_predicated", "bn_stats",
    "activation", "copy", "add", "mul",
    "partition_all_reduce", "partition_broadcast",
])


class _Chained:

    def __init__(self, eng, sem, cnt, name):
        self._e = eng
        self._sem = sem
        self._cnt = cnt
        self._name = name

    def __getattr__(self, attr):
        real = getattr(self._e, attr)
        if attr not in _CHAIN_OPS:
            return real
        cnt, name = self._cnt, self._name

        def wrapped(*a, **k):
            if cnt[name] > 0:
                self._e.wait_ge(self._sem, cnt[name])
            inst = real(*a, **k)
            inst.then_inc(self._sem, 1)
            cnt[name] += 1
            return inst
        return wrapped


def build_nc():
    nc = bass.Bass("TRN2", target_bir_lowering=False)

    x_in = nc.dram_tensor("x", [TC, D], F32, kind="ExternalInput").ap()
    wa_t = nc.dram_tensor("wa_t", [G, 128, G * 128], F32, kind="ExternalInput").ap()
    wx_t = nc.dram_tensor("wx_t", [G, 128, G * 128], F32, kind="ExternalInput").ap()
    w12_t = nc.dram_tensor("w12_t", [2, HSN, 128, G * 512], F32,
                           kind="ExternalInput").ap()
    w3_t = nc.dram_tensor("w3_t", [G, 128, HCN * 128], F32, kind="ExternalInput").ap()
    lam_in = nc.dram_tensor("lam", [128, G], F32, kind="ExternalInput").ap()
    ba_in = nc.dram_tensor("ba", [128, G], F32, kind="ExternalInput").ap()
    bx_in = nc.dram_tensor("bx", [128, G], F32, kind="ExternalInput").ap()
    mask_in = nc.dram_tensor("mask", [128, 2 * NCORE], F32, kind="ExternalInput").ap()
    ident_in = nc.dram_tensor("ident", [128, 128], F32, kind="ExternalInput").ap()
    onesc_in = nc.dram_tensor("onesc", [128, 1], F32, kind="ExternalInput").ap()
    onesr_in = nc.dram_tensor("onesr", [1, 128], F32, kind="ExternalInput").ap()
    w1s_in = nc.dram_tensor("w1s", [128, NSL], F32, kind="ExternalInput").ap()
    w2s_in = nc.dram_tensor("w2s", [128, NSL], F32, kind="ExternalInput").ap()
    w3s_in = nc.dram_tensor("w3s", [128, NSL], F32, kind="ExternalInput").ap()

    out_dram = nc.dram_tensor("out", [TC, D], F32, kind="ExternalOutput").ap()

    a_spill = nc.dram_tensor("a_spill", [G, 128, TC], F32).ap()
    u_spill = nc.dram_tensor("u_spill", [G, 128, TC], F32).ap()
    h_park = nc.dram_tensor("h_park", [NTT, 128, HSN * 512], F32).ap()
    x1_park = nc.dram_tensor("x1_park", [NTT, 128, D], F32).ap()
    b3_park = nc.dram_tensor("b3_park", [G, 128, TC], F32).ap()
    cc_sum_in = nc.dram_tensor("cc_sum_in", [4, 1], F32).ap()
    cc_sum_out = nc.dram_tensor("cc_sum_out", [4, 1], F32).ap()
    cc_ag_in = nc.dram_tensor("cc_ag_in", [128, 2 * G], F32).ap()
    cc_ag_out = nc.dram_tensor("cc_ag_out", [NCORE * 128, 2 * G], F32).ap()

    stack = []

    def enter(h):
        t = h.__enter__()
        stack.append(h)
        return t

    scopes = {}

    def enter_sc(name, h):
        t = h.__enter__()
        scopes.setdefault(name, []).append(h)
        return t

    def free_sc(name):
        for h in reversed(scopes.pop(name, [])):
            h.__exit__(None, None, None)

    sems, cnt = {}, {}
    for n in ["sV", "sS", "sT", "sG",
              "dPRE", "dW0", "dW1", "dX0", "dX1", "dSP0", "dSP1",
              "dRL0", "dRL1", "dCC", "dHP0", "dHP1",
              "dHR0", "dHR1", "dHR2", "dHR3",
              "dHB0", "dHB1", "dHB2", "dHB3", "dHB4", "dHB5", "dHB6", "dHB7",
              "dPK", "dPK0", "dPK1", "dOUT", "dOUT2", "dTR", "dTR2", "cc"]:
        sems[n] = enter(nc.semaphore(n))
        cnt[n] = 0

    def mark(name, inst, step):
        inst.then_inc(sems[name], step)
        cnt[name] += step
        return (name, cnt[name])

    def tok(name):
        return (name, cnt[name])

    def wait(eng, *tokens):
        for t_ in tokens:
            if t_ is None:
                continue
            name, v = t_
            if v > 0:
                eng.wait_ge(sems[name], v)

    V, SC, TE, GP = "sV", "sS", "sT", "sG"

    def CH(eng, name):
        return _Chained(eng, sems[name], cnt, name)

    def te_group_wait(t):
        if cnt[TE] > 0:
            t.wait_ge(sems[TE], cnt[TE])

    ident = enter(nc.sbuf_tensor("identt", [128, 128], F32))
    lam = enter(nc.sbuf_tensor("lamt", [128, G], F32))
    bav = enter(nc.sbuf_tensor("bav", [128, G], F32))
    bxv = enter(nc.sbuf_tensor("bxv", [128, G], F32))
    coef = enter(nc.sbuf_tensor("coef", [128, G], F32))
    smallA = enter(nc.sbuf_tensor("smallA", [128, G], F32))
    maskt = enter(nc.sbuf_tensor("maskt", [128, 2 * NCORE], F32))
    wsums = enter(nc.sbuf_tensor("wsums", [128, 4], F32))
    wsall = enter(nc.sbuf_tensor("wsall", [128, 4], F32))
    onesc = enter(nc.sbuf_tensor("onesct", [128, 1], F32))
    onesr = enter(nc.sbuf_tensor("onesrt", [1, 128], F32))
    wconst = enter(nc.sbuf_tensor("wconst", [128, 16], F32))
    colAH = enter(nc.sbuf_tensor("colAH", [128, 2 * G], F32))
    gat = enter(nc.sbuf_tensor("gat", [128, NCORE * 2 * G], F32))
    Hcol = enter(nc.sbuf_tensor("Hcol", [128, G], F32))
    tmpG = enter(nc.sbuf_tensor("tmpG", [128, G], F32))
    tmpS = enter(nc.sbuf_tensor("tmpS", [128, G], F32))
    colb = enter(nc.sbuf_tensor("colb", [128, 16], F32))
    colb2 = enter(nc.sbuf_tensor("colb2", [128, 16], F32))
    bstb = enter(nc.sbuf_tensor("bstb", [128, 32], F32))
    bstb2 = enter(nc.sbuf_tensor("bstb2", [128, 32], F32))
    sinvz = enter(nc.sbuf_tensor("sinvz", [128, NTT], F32))
    c1c = enter(nc.sbuf_tensor("c1c", [128, NTT], F32))
    c2c = enter(nc.sbuf_tensor("c2c", [128, NTT], F32))
    c3c = enter(nc.sbuf_tensor("c3c", [128, NTT], F32))
    runmax = enter(nc.sbuf_tensor("runmax", [128, NTT], F32))
    shc = enter(nc.sbuf_tensor("shc", [128, NTT], F32))
    zbuf = enter(nc.sbuf_tensor("zbuf", [128, TC], F32))
    ws = {}
    wsl = enter_sc("p0", nc.sbuf_tensor("wsl", [128, NSL], F32))

    pb = [enter(nc.psum_tensor("pb%d" % k, [128, 512], F32)) for k in range(8)]

    blk = nc.Block("ph0"); blk.__enter__()
    if True:
        @blk.sync
        def _(sy):
            for dst, src in [(ident, ident_in), (lam, lam_in), (bav, ba_in),
                             (bxv, bx_in), (maskt, mask_in),
                             (onesc, onesc_in), (onesr, onesr_in)]:
                mark("dPRE", sy.dma_start(out=dst[:, :], in_=src), 16)
            mark("dPRE", sy.dma_start(out=wsl[:, :], in_=w1s_in), 16)

        @blk.scalar
        def _(s):
            s = CH(s, SC)
            wait(s, tok("dPRE"))
            s.activation(out=smallA[:, :], in_=lam[:, :], func=AF.Exp, scale=-1.0)

        @blk.vector
        def _(v):
            v = CH(v, V)
            wait(v, tok(SC), tok("dPRE"))
            v.memset(zbuf[:, :], 0.0)
            v.memset(Hcol[:, :], 0.0)
            v.memset(runmax[:, :], 0.0)
            e, t_ = smallA, coef
            v.tensor_scalar(t_[:, :], e[:, :], -0.2, 0.25, OP.mult, OP.add)
            v.scalar_tensor_tensor(t_[:, :], e[:, :], -1.0, t_[:, :], OP.mult, OP.mult)
            v.tensor_scalar_add(t_[:, :], t_[:, :], float(np.float32(1.0 / 3)))
            v.scalar_tensor_tensor(t_[:, :], e[:, :], -1.0, t_[:, :], OP.mult, OP.mult)
            v.tensor_scalar_add(t_[:, :], t_[:, :], 0.5)
            v.scalar_tensor_tensor(t_[:, :], e[:, :], -1.0, t_[:, :], OP.mult, OP.mult)
            v.tensor_scalar_add(t_[:, :], t_[:, :], 1.0)
            v.tensor_tensor(t_[:, :], e[:, :], t_[:, :], OP.mult)
            v.tensor_scalar_mul(t_[:, :], t_[:, :], -8.0)
            v.reduce_sum(wsums[:, 0:1], wsl[:, :], axis=AX, apply_absolute_value=True)

    for j, src in [(1, w2s_in), (2, w3s_in)]:
        if True:
            @blk.sync
            def _(sy, src=src):
                wait(sy, tok(V))
                mark("dPRE", sy.dma_start(out=wsl[:, :], in_=src), 16)

            @blk.vector
            def _(v, j=j):
                v = CH(v, V)
                wait(v, tok("dPRE"))
                v.reduce_sum(wsums[:, j:j + 1], wsl[:, :], axis=AX,
                             apply_absolute_value=True)

    if True:
        @blk.vector
        def _(v):
            v = CH(v, V)
            v.memset(wsums[:, 3:4], 0.0)

        @blk.tensor
        def _(t):
            wait(t, tok(V), tok("dPRE"))
            te_group_wait(t)
            mark(TE, t.matmul(pb[0][0:4, 0:1], wsums[:, 0:4], onesc[:, 0:1],
                              start=True, stop=True), 1)

        @blk.vector
        def _(v):
            v = CH(v, V)
            wait(v, tok(TE))
            v.tensor_copy(wsall[0:4, 0:1], pb[0][0:4, 0:1])

        @blk.gpsimd
        def _(g):
            wait(g, tok(V))
            mark("dCC", g.dma_start(out=cc_sum_in, in_=wsall[0:4, 0:1]), 16)
            wait(g, tok("dCC"))
            mark("cc", g.collective_compute(
                "AllReduce", OP.add, replica_groups=[list(range(NCORE))],
                ins=[cc_sum_in.opt()], outs=[cc_sum_out.opt()]), 1)
            wait(g, tok("cc"))
            mark("dCC", g.dma_start(out=wconst[0:1, 12:16],
                                    in_=cc_sum_out.rearrange("p f -> f p")), 16)

        @blk.vector
        def _(v):
            v = CH(v, V)
            wait(v, tok("dCC"))
            r0 = wconst[0:1, :]
            v.tensor_scalar(r0[:, 0:3], r0[:, 12:15], float(np.float32(1.0 / (HID * D))),
                            1e-5, OP.mult, OP.max)
            v.tensor_scalar_mul(r0[:, 3:6], r0[:, 0:3], 0.5)
            v.tensor_scalar_mul(r0[:, 6:9], r0[:, 3:6], -1.0)

        @blk.tensor
        def _(t):
            wait(t, tok(V))
            te_group_wait(t)
            mark(TE, t.matmul(pb[1][:, 0:9], onesr[0:1, :], wconst[0:1, 0:9],
                              start=True, stop=True), 1)

        @blk.vector
        def _(v):
            v = CH(v, V)
            wait(v, tok(TE))
            v.tensor_copy(wconst[:, 0:9], pb[1][:, 0:9])
    wconst_done = tok(V)

    def ln_stats(blk, src_t, c0=0, cb=None, pre=None):
        if cb is None:
            cb = colb
        bst = bstb if cb is colb else bstb2

        @blk.vector
        def _(v):
            v = CH(v, V)
            if pre is not None:
                pre(v)
            for qq in range(4):
                v.bn_stats(bst[:, 6 * qq:6 * qq + 6],
                           src_t[:, c0 + 512 * qq:c0 + 512 * qq + 512])
            v.tensor_tensor(bst[:, 24:32], bst[:, 1:24:3], bst[:, 1:24:3],
                            OP.mult)
            v.reduce_sum(cb[:, 7:8], bst[:, 1:24:3], axis=AX)
            v.reduce_sum(cb[:, 8:9], bst[:, 24:32], axis=AX)
            v.reduce_sum(cb[:, 9:10], bst[:, 2:24:3], axis=AX)
            v.tensor_scalar_mul(cb[:, 5:6], cb[:, 7:8], -0.125)
            v.tensor_tensor(cb[:, 12:13], cb[:, 5:6], cb[:, 5:6], OP.mult)
            v.scalar_tensor_tensor(cb[:, 11:12], cb[:, 8:9], 0.125,
                                   cb[:, 12:13], OP.mult, OP.subtract)
            v.tensor_scalar(cb[:, 12:13], cb[:, 9:10],
                            float(np.float32(1.0 / D)), EPS, OP.mult, OP.add)
            v.tensor_tensor(cb[:, 11:12], cb[:, 11:12], cb[:, 12:13], OP.add)
            v.reciprocal(cb[:, 2:3], cb[:, 11:12])

        @blk.scalar
        def _(s):
            s = CH(s, SC)
            wait(s, tok(V))
            s.activation(out=cb[:, 4:5], in_=cb[:, 2:3], func=AF.Sqrt)
            s.activation(out=cb[:, 6:7], in_=cb[:, 5:6], func=AF.Identity,
                         scale=cb[:, 4:5])
        return tok(SC)

    free_sc("p0")
    xnT_hi = enter_sc("p12", nc.sbuf_tensor("xnT_hi", [128, G * TC], BF16))
    xnT_lo = enter_sc("p12", nc.sbuf_tensor("xnT_lo", [128, G * TC], BF16))
    wafh = enter_sc("p12", nc.sbuf_tensor("wafh", [128, D], F32))
    wahi = [enter_sc("p12", nc.sbuf_tensor("wahi%d" % i, [128, D], BF16)) for i in range(3)]
    walo = [enter_sc("p12", nc.sbuf_tensor("walo%d" % i, [128, D], BF16)) for i in range(3)]
    wxhi = [enter_sc("p12", nc.sbuf_tensor("wxhi%d" % i, [128, D], BF16)) for i in range(3)]
    wxlo = [enter_sc("p12", nc.sbuf_tensor("wxlo%d" % i, [128, D], BF16)) for i in range(3)]

    split_done = [None] * G
    mm_done = [None] * G

    def p2_prep(j):
        q = j % 3
        for srcw, dsth, dstl in [(wa_t, wahi, walo), (wx_t, wxhi, wxlo)]:
            if True:
                @blk.sync
                def _(sy, j=j, q=q, srcw=srcw):
                    if j >= 1 and srcw is wa_t:
                        wait(sy, split_done[j - 1])
                    elif srcw is wx_t:
                        wait(sy, tok(V))
                    if j >= 3 and srcw is wa_t:
                        wait(sy, mm_done[j - 3])
                    mark("dW%d" % (j % 2), sy.dma_start(out=wafh[:, :],
                                                        in_=srcw[j]), 16)

                @blk.scalar
                def _(s, j=j, q=q, dsth=dsth):
                    s = CH(s, SC)
                    wait(s, tok("dW%d" % (j % 2)))
                    s.activation(out=dsth[q][:, :], in_=wafh[:, :], func=AF.Copy)

                @blk.vector
                def _(v, j=j, q=q, dsth=dsth, dstl=dstl):
                    v = CH(v, V)
                    wait(v, tok(SC))
                    v.tensor_tensor(dstl[q][:, :], wafh[:, :], dsth[q][:, :],
                                    OP.subtract)
        split_done[j] = tok(V)

    hi_t = enter_sc("p1x", nc.sbuf_tensor("hi_t", [128, D], BF16))
    lo_t = enter_sc("p1x", nc.sbuf_tensor("lo_t", [128, D], BF16))
    xt1 = enter_sc("p1x", nc.sbuf_tensor("xt1", [128, D], F32))
    ws["xn32"] = enter_sc("p1x", nc.sbuf_tensor("xn32a", [128, D], F32))

    xt1b = enter_sc("p1x", nc.sbuf_tensor("xt1b", [128, D], F32))
    xn1 = [ws["xn32"],
           enter_sc("p1x", nc.sbuf_tensor("xn32a2", [128, D], F32))]
    hi2 = [hi_t, enter_sc("p1x", nc.sbuf_tensor("hi_t2", [128, D], BF16))]
    lo2 = [lo_t, enter_sc("p1x", nc.sbuf_tensor("lo_t2", [128, D], BF16))]
    xtb = [xt1, xt1b]
    colb5 = [colb, colb2]

    p1_xbar_done = []
    p1_back_s = [None] * NTT
    p1_front_v = [None] * NTT

    def p1_front(tci):
        p = tci % 2
        cb = colb5[p]
        if True:
            @blk.sync
            def _(sy, tci=tci, p=p):
                if tci >= 2:
                    wait(sy, p1_back_s[tci - 2])
                mark("dX%d" % p, sy.dma_start(
                    out=xtb[p][:, :], in_=x_in[tci * 128:(tci + 1) * 128, :]), 16)

            def pre(v, p=p):
                wait(v._e, tok("dX%d" % p))

            ln_stats(blk, xtb[p], 0, cb=cb, pre=pre)
            p1_front_v[tci] = tok(V)

    def p1_back(tci):
        p = tci % 2
        cb = colb5[p]
        if True:
            @blk.scalar
            def _(s, tci=tci, p=p, cb=cb):
                s = CH(s, SC)
                wait(s, p1_front_v[tci])
                if tci >= 2:
                    wait(s, p1_xbar_done[tci - 2])
                s.activation(out=xn1[p][:, :], in_=xtb[p][:, :], func=AF.Identity,
                             bias=cb[:, 6:7], scale=cb[:, 4:5])
                p1_back_s[tci] = tok(SC)

            @blk.vector
            def _(v, tci=tci, p=p):
                v = CH(v, V)
                wait(v, tok(SC))
                v.tensor_copy(hi2[p][:, :], xn1[p][:, :])
                v.tensor_tensor(lo2[p][:, :], xn1[p][:, :], hi2[p][:, :],
                                OP.subtract)

            @blk.sync
            def _(sy, tci=tci, p=p):
                wait(sy, tok(V))
                hi3 = xnT_hi[:, :].rearrange("q (c t) -> q c t", t=TC)[
                    :, :, tci * 128:(tci + 1) * 128]
                lo3 = xnT_lo[:, :].rearrange("q (c t) -> q c t", t=TC)[
                    :, :, tci * 128:(tci + 1) * 128]
                mark("dTR2" if tci % 2 else "dTR",
                     sy.dma_start_transpose(hi3, hi2[p][:, :]), 16)
                p1_xbar_done.append(
                    mark("dTR2" if tci % 2 else "dTR",
                         sy.dma_start_transpose(lo3, lo2[p][:, :]), 16))

    p1_front(0)
    for tci in range(1, NTT):
        p1_front(tci)
        p1_back(tci - 1)
    p1_back(NTT - 1)

    for j in range(3):
        p2_prep(j)

    free_sc("p1x")
    rgt = enter_sc("p12", nc.sbuf_tensor("rgt", [128, TC], F32))
    igt = enter_sc("p12", nc.sbuf_tensor("igt", [128, TC], F32))
    agt = [enter_sc("p12", nc.sbuf_tensor("agt%d" % i, [128, TC], F32)) for i in range(2)]
    ugt = [enter_sc("p12", nc.sbuf_tensor("ugt%d" % i, [128, TC], F32)) for i in range(2)]
    xnr = enter_sc("p12", nc.sbuf_tensor("xnr", [128, TC], F32))
    gsc = enter_sc("p12", nc.sbuf_tensor("gsc", [128, TC], F32))
    y0b = enter_sc("p12", nc.sbuf_tensor("y0b", [128, TC], F32))
    afl = enter_sc("p12", nc.sbuf_tensor("afl", [128, TC], F32))

    gate_evac = [None, None]
    spill_done = [None, None]
    scan1_done = [None] * G

    for hc in range(G):
        p = hc % 2
        q = hc % 3
        if True:
            @blk.tensor
            def _(t, hc=hc, p=p, q=q):
                wait(t, split_done[hc])
                if hc == 0:
                    wait(t, tok("dTR"), tok("dTR2"), wconst_done)
                if hc >= 2:
                    wait(t, gate_evac[p])
                te_group_wait(t)
                for bank, whi, wlo in [(0, wahi[q], walo[q]),
                                       (2, wxhi[q], wxlo[q])]:
                    for half in range(2):
                        ps = pb[bank + half + 4 * p]
                        first = True
                        for wtile, xtile in [(whi, xnT_hi), (whi, xnT_lo),
                                             (wlo, xnT_hi)]:
                            for dc in range(G):
                                inst = t.matmul(
                                    ps[:, :],
                                    wtile[:, dc * 128:(dc + 1) * 128],
                                    xtile[:, dc * TC + half * 512:
                                          dc * TC + half * 512 + 512],
                                    start=first,
                                    stop=(wtile is wlo and dc == G - 1))
                                first = False
                        mark(TE, inst, 1)
                mm_done[hc] = tok(TE)

        if hc + 3 < G:
            p2_prep(hc + 3)

        if True:
            @blk.scalar
            def _(s, hc=hc, p=p):
                s = CH(s, SC)
                wait(s, mm_done[hc])
                if spill_done[p] is not None:
                    wait(s, spill_done[p])
                for half in range(2):
                    s.activation(out=rgt[:, half * 512:half * 512 + 512],
                                 in_=pb[0 + half + 4 * p][:, :], func=AF.Sigmoid,
                                 bias=bav[:, hc:hc + 1])
                for half in range(2):
                    s.activation(out=igt[:, half * 512:half * 512 + 512],
                                 in_=pb[2 + half + 4 * p][:, :], func=AF.Sigmoid,
                                 bias=bxv[:, hc:hc + 1])
                s.activation(out=agt[p][:, :], in_=rgt[:, :], func=AF.Exp,
                             scale=coef[:, hc:hc + 1])
                gate_evac[p] = tok(SC)

            @blk.vector
            def _(v, hc=hc, p=p):
                v = CH(v, V)
                wait(v, gate_evac[p])
                v.tensor_tensor(xnr[:, :],
                                xnT_hi[:, hc * TC:(hc + 1) * TC],
                                xnT_lo[:, hc * TC:(hc + 1) * TC], OP.add)
                v.tensor_tensor(xnr[:, :], igt[:, :], xnr[:, :], OP.mult)
                v.tensor_tensor(gsc[:, :], agt[p][:, :], agt[p][:, :], OP.mult)
                v.tensor_scalar(gsc[:, :], gsc[:, :], -1.0, 1.0, OP.mult, OP.add)

            @blk.scalar
            def _(s, hc=hc):
                s = CH(s, SC)
                wait(s, tok(V))
                s.activation(out=gsc[:, :], in_=gsc[:, :], func=AF.Sqrt)

            @blk.vector
            def _(v, hc=hc, p=p):
                v = CH(v, V)
                wait(v, tok(SC))
                if spill_done[p] is not None:
                    wait(v, spill_done[p])
                v.tensor_tensor(ugt[p][:, :], gsc[:, :], xnr[:, :], OP.mult)
                v.tensor_tensor_scan(y0b[:, :], agt[p][:, :], ugt[p][:, :],
                                     0.0, OP.mult, OP.add)
                v.tensor_copy(colAH[:, G + hc:G + hc + 1], y0b[:, TC - 1:TC])
                v.tensor_tensor_scan(afl[:, :], agt[p][:, :], zbuf[:, :],
                                     1.0, OP.mult, OP.add)
                v.tensor_copy(colAH[:, hc:hc + 1], afl[:, TC - 1:TC])
                scan1_done[hc] = tok(V)

            @blk.gpsimd
            def _(g, hc=hc, p=p):
                wait(g, scan1_done[hc])
                mark("dSP%d" % p, g.dma_start(out=a_spill[hc], in_=agt[p][:, :]), 16)
                spill_done[p] = mark("dSP%d" % p,
                                     g.dma_start(out=u_spill[hc], in_=ugt[p][:, :]),
                                     16)

    if True:
        @blk.gpsimd
        def _(g):
            wait(g, scan1_done[G - 2], scan1_done[G - 1])
            mark("dCC", g.dma_start(out=cc_ag_in, in_=colAH[:, :]), 16)
            wait(g, tok("dCC"))
            mark("cc", g.collective_compute(
                "AllGather", OP.bypass, replica_groups=[list(range(NCORE))],
                ins=[cc_ag_in.opt()], outs=[cc_ag_out.opt()]), 1)
            wait(g, tok("cc"))
            mark("dCC", g.dma_start(
                out=gat[:, :],
                in_=cc_ag_out.rearrange("(r p) j -> p r j", p=128)), 16)

        @blk.vector
        def _(v):
            v = CH(v, V)
            wait(v, tok("dCC"))
            for k in range(NCORE - 1):
                base = k * 2 * G
                v.tensor_scalar(tmpG[:, :], gat[:, base:base + G],
                                maskt[:, k:k + 1],
                                maskt[:, NCORE + k:NCORE + k + 1],
                                OP.mult, OP.add)
                v.tensor_single_scalar(tmpS[:, :], gat[:, base + G:base + 2 * G],
                                       maskt[:, k:k + 1], OP.mult)
                v.tensor_tensor(Hcol[:, :], Hcol[:, :], tmpG[:, :], OP.mult)
                v.tensor_tensor(Hcol[:, :], Hcol[:, :], tmpS[:, :], OP.add)
    fold_done = tok(V)

    free_sc("p12")
    zqT = enter_sc("zq", nc.sbuf_tensor("zqT", [128, G * TC], BF16))
    yT = enter_sc("yt", nc.sbuf_tensor("yT", [128, NTT * D], F32))
    arl = [enter_sc("p4", nc.sbuf_tensor("arl%d" % i, [128, TC], F32)) for i in range(2)]
    url = [enter_sc("p4", nc.sbuf_tensor("url%d" % i, [128, TC], F32)) for i in range(2)]
    ybuf = [enter_sc("p4", nc.sbuf_tensor("ybuf%d" % i, [128, TC], F32)) for i in range(2)]

    scan2_done = [None, None]
    tr_done = [None, None]
    evac_done = [None, None]
    for hc in range(G):
        p = hc % 2
        if True:
            @blk.sync
            def _(sy, hc=hc, p=p):
                if hc == 0:
                    wait(sy, tok("dSP0"), tok("dSP1"))
                if hc >= 2:
                    wait(sy, scan2_done[p])
                mark("dRL%d" % p, sy.dma_start(out=arl[p][:, :], in_=a_spill[hc]), 16)
                mark("dRL%d" % p, sy.dma_start(out=url[p][:, :], in_=u_spill[hc]), 16)

            @blk.vector
            def _(v, hc=hc, p=p):
                v = CH(v, V)
                wait(v, tok("dRL%d" % p), fold_done)
                if hc >= 2:
                    wait(v, tr_done[p])
                v.tensor_tensor_scan(ybuf[p][:, :], arl[p][:, :], url[p][:, :],
                                     Hcol[:, hc:hc + 1], OP.mult, OP.add)
                scan2_done[p] = tok(V)

            @blk.tensor
            def _(t, hc=hc, p=p):
                wait(t, scan2_done[p])
                if hc >= 2:
                    wait(t, evac_done[p])
                te_group_wait(t)
                for tb in range(NTT):
                    inst = t.transpose(
                        pb[4 * p + tb // 4][:, (tb % 4) * 128:(tb % 4) * 128 + 128],
                        ybuf[p][:, tb * 128:(tb + 1) * 128], ident[:, :])
                    mark(TE, inst, 1)
                tr_done[p] = tok(TE)

            @blk.vector
            def _(v, hc=hc, p=p):
                v = CH(v, V)
                wait(v, tr_done[p])
                for b in range(2):
                    out3 = yT[:, :].rearrange("p (tb d) -> p tb d", d=D)[
                        :, 4 * b:4 * b + 4, hc * 128:(hc + 1) * 128]
                    v.tensor_copy(out3, pb[4 * p + b][:, :].rearrange(
                        "p (tb q) -> p tb q", q=128))
                evac_done[p] = tok(V)

    free_sc("p4")
    xt = [enter_sc("p5", nc.sbuf_tensor("xtp5_%d" % i, [128, D], F32))
          for i in range(2)]
    cent = [enter_sc("p5", nc.sbuf_tensor("cent%d" % i, [128, D], F32))
            for i in range(2)]
    x15 = [enter_sc("p5", nc.sbuf_tensor("x15_%d" % i, [128, D], F32))
           for i in range(2)]
    it2 = enter_sc("p5", nc.sbuf_tensor("it2", [128, D], I32))
    ft2 = [enter_sc("p5", nc.sbuf_tensor("ft2_%d" % i, [128, D], F32))
           for i in range(2)]
    ds2 = enter_sc("p5", nc.sbuf_tensor("ds2", [128, D], F32))
    zqt_sb = [enter_sc("p5", nc.sbuf_tensor("zqt_sb%d" % i, [128, D], BF16))
              for i in range(2)]
    colb5 = [colb, colb2]

    p5_done = []
    p5_back_s = [None] * NTT
    p5_park = [None] * NTT
    p5_vb = [None] * NTT
    p5_vc = [None] * NTT
    p5_sb = [None] * NTT
    p5_vd = [None] * NTT
    p5_bb = [None] * NTT

    def p5_sy0(t):
        p = t % 2

        @blk.sync
        def _(sy, t=t, p=p):
            if t < 2:
                wait(sy, scan2_done[0], scan2_done[1],
                     tr_done[0], tr_done[1])
            if t >= 2:
                wait(sy, p5_done[t - 2])
            mark("dX%d" % p, sy.dma_start(
                out=xt[p][:, :], in_=x_in[t * 128:(t + 1) * 128, :]), 16)

    def p5_va(t):
        p = t % 2

        def pre5(v, t=t, p=p):
            if t == 0:
                wait(v._e, evac_done[0], evac_done[1])
            if t >= 2 and p5_back_s[t - 2] is not None:
                wait(v._e, p5_back_s[t - 2])
        return ln_stats(blk, yT, t * D, cb=colb5[p], pre=pre5)

    def p5_vb_f(t, lnA_s):
        p = t % 2
        cb = colb5[p]

        @blk.vector
        def _(v, t=t, p=p, cb=cb, lnA_s=lnA_s):
            v = CH(v, V)
            wait(v, tok("dX%d" % p), lnA_s)
            v.tensor_scalar(x15[p][:, :], yT[:, t * D:(t + 1) * D],
                            cb[:, 4:5], cb[:, 6:7], OP.mult, OP.add)
            v.tensor_tensor(x15[p][:, :], x15[p][:, :], xt[p][:, :], OP.add)
        p5_vb[t] = tok(V)

    def p5_sy1(t):
        p = t % 2

        @blk.sync
        def _(sy, t=t, p=p):
            wait(sy, p5_vb[t])
            p5_park[t] = mark("dPK%d" % p, sy.dma_start(
                out=x1_park[t], in_=x15[p][:, :]), 16)

    def p5_vc_f(t):
        p = t % 2
        cb = colb5[p]
        bst = bstb if cb is colb else bstb2

        @blk.vector
        def _(v, t=t, p=p, cb=cb, bst=bst):
            v = CH(v, V)
            for qq in range(4):
                v.bn_stats(bst[:, 6 * qq:6 * qq + 6],
                           x15[p][:, 512 * qq:512 * qq + 512])
            v.tensor_tensor(bst[:, 24:32], bst[:, 1:24:3], bst[:, 1:24:3],
                            OP.mult)
            v.reduce_sum(cb[:, 0:1], bst[:, 1:24:3], axis=AX)
            v.reduce_sum(cb[:, 1:2], bst[:, 24:32], axis=AX)
            v.reduce_sum(cb[:, 3:4], bst[:, 2:24:3], axis=AX)
            v.tensor_scalar_mul(cb[:, 5:6], cb[:, 0:1], -0.125)
            v.tensor_tensor(cb[:, 12:13], cb[:, 5:6], cb[:, 5:6], OP.mult)
            v.scalar_tensor_tensor(cb[:, 11:12], cb[:, 1:2], 0.125,
                                   cb[:, 12:13], OP.mult, OP.subtract)
            v.tensor_scalar(cb[:, 13:14], cb[:, 3:4],
                            float(np.float32(1.0 / D)), EPS, OP.mult, OP.add)
            v.tensor_tensor(cb[:, 11:12], cb[:, 11:12], cb[:, 13:14], OP.add)
            v.reciprocal(cb[:, 2:3], cb[:, 11:12])
            v.tensor_single_scalar(cent[p][:, :], x15[p][:, :], cb[:, 5:6],
                                   OP.add)
            v.reduce_max(cb[:, 7:8], cent[p][:, :], axis=AX,
                         apply_absolute_value=True)
        p5_vc[t] = tok(V)

    def p5_sb_f(t):
        cb = colb5[t % 2]

        @blk.scalar
        def _(s, t=t, cb=cb):
            s = CH(s, SC)
            wait(s, p5_vc[t])
            s.activation(out=cb[:, 4:5], in_=cb[:, 2:3], func=AF.Sqrt)
        p5_sb[t] = tok(SC)

    def p5_vd_f(t):
        cb = colb5[t % 2]

        @blk.vector
        def _(v, t=t, cb=cb):
            v = CH(v, V)
            wait(v, p5_sb[t])
            v.tensor_tensor(cb[:, 7:8], cb[:, 7:8], cb[:, 4:5], OP.mult)
            v.tensor_scalar_max(cb[:, 7:8], cb[:, 7:8], 1e-5)
            v.reciprocal(cb[:, 8:9], cb[:, 7:8])
            v.tensor_scalar_mul(cb[:, 8:9], cb[:, 8:9], 127.0)
            v.reciprocal(sinvz[:, t:t + 1], cb[:, 8:9])
            v.tensor_tensor(cb[:, 9:10], cb[:, 4:5], cb[:, 8:9], OP.mult)
            v.scalar_tensor_tensor(c1c[:, t:t + 1], sinvz[:, t:t + 1], -1.0,
                                   wconst[:, 0:1], OP.mult, OP.mult)
            v.scalar_tensor_tensor(c2c[:, t:t + 1], sinvz[:, t:t + 1], -1.0,
                                   wconst[:, 1:2], OP.mult, OP.mult)
        p5_vd[t] = tok(V)

    def p5_ba(t):
        p = t % 2
        cb = colb5[p]

        @blk.scalar
        def _(s, t=t, p=p, cb=cb):
            s = CH(s, SC)
            wait(s, p5_vd[t], p5_park[t])
            s.activation(out=x15[p][:, :], in_=cent[p][:, :], func=AF.Identity,
                         scale=cb[:, 9:10])
            s.copy(it2[:, :], x15[p][:, :])
            s.copy(ft2[p][:, :], it2[:, :])
        p5_back_s[t] = tok(SC)

    def p5_bb_f(t):
        p = t % 2

        @blk.vector
        def _(v, t=t, p=p):
            v = CH(v, V)
            wait(v, p5_back_s[t])
            if t >= 2:
                wait(v, p5_done[t - 2])
            v.tensor_tensor(ds2[:, :], x15[p][:, :], ft2[p][:, :], OP.subtract)
            v.scalar_tensor_tensor(ft2[p][:, :], ds2[:, :], 0.5, ft2[p][:, :],
                                   OP.is_gt, OP.add)
            v.scalar_tensor_tensor(zqt_sb[p][:, :], ds2[:, :], -0.5,
                                   ft2[p][:, :], OP.is_lt, OP.subtract)
        p5_bb[t] = tok(V)

    def p5_sy2(t):
        p = t % 2

        @blk.sync
        def _(sy, t=t, p=p):
            wait(sy, p5_bb[t])
            z3 = zqT[:, :].rearrange("q (c t) -> q c t", t=TC)[
                :, :, t * 128:(t + 1) * 128]
            p5_done.append(mark("dTR2" if t % 2 else "dTR",
                                sy.dma_start_transpose(z3, zqt_sb[p][:, :]),
                                16))

    p5_sy0(0)
    p5_sy0(1)
    p5_vb_f(0, p5_va(0))
    p5_sy1(0)
    p5_vc_f(0)
    p5_sb_f(0)
    p5_vd_f(0)
    for i in range(NTT):
        if i + 1 < NTT:
            p5_ba(i)
            p5_vb_f(i + 1, p5_va(i + 1))
            p5_sy1(i + 1)
            p5_vc_f(i + 1)
            p5_sb_f(i + 1)
            p5_bb_f(i)
            p5_sy2(i)
            if i + 2 < NTT:
                p5_sy0(i + 2)
            p5_vd_f(i + 1)
        else:
            p5_ba(i)
            p5_bb_f(i)
            p5_sy2(i)

    free_sc("p5")
    free_sc("yt")
    w12fh = [enter_sc("p6", nc.sbuf_tensor("w12fh%d" % i, [128, G * 256], F32))
             for i in range(2)]
    s1bh = [enter_sc("p6", nc.sbuf_tensor("s1bh%d" % i, [128, G * 256], BF16))
            for i in range(2)]
    w1q = [enter_sc("p6", nc.sbuf_tensor("w1q%d" % i, [128, G * 512], BF16))
           for i in range(3)]
    w2q = [enter_sc("p6", nc.sbuf_tensor("w2q%d" % i, [128, G * 512], BF16))
           for i in range(3)]
    silu_b = enter_sc("p6", nc.sbuf_tensor("silu_b", [128, 512], F32))
    hblk = [enter_sc("p6", nc.sbuf_tensor("hblk%d" % i, [128, 512], F32))
            for i in range(4)]

    tern_done = [None] * HSN
    p6_piece_done = [None, None]
    b12_done = {}
    silu_done = {}
    hdone = {}
    hpark = {}
    HW2 = G * 256

    def p6_tern(j):
        q3 = j % 3
        for pc in range(4):
            wi, half = pc // 2, pc % 2
            wpar = (4 * j + pc) % 2
            dstq = [w1q, w2q][wi]
            nthc = [wconst[:, 6:7], wconst[:, 7:8]][wi]
            thc_ = [wconst[:, 3:4], wconst[:, 4:5]][wi]
            if True:
                @blk.sync
                def _(sy, j=j, pc=pc, wi=wi, half=half, wpar=wpar):
                    if j == 0 and pc == 0:
                        wait(sy, tok("dTR"), tok("dTR2"), tok(V))
                    if p6_piece_done[wpar] is not None:
                        wait(sy, p6_piece_done[wpar])
                    if pc == 0 and j >= 3:
                        wait(sy, b12_done[(j - 3, NTT - 1)])
                    mark("dW%d" % wpar, sy.dma_start(
                        out=w12fh[wpar][:, :],
                        in_=w12_t[wi, j][:, half * HW2:(half + 1) * HW2]), 16)

                @blk.vector
                def _(v, j=j, q3=q3, pc=pc, wi=wi, half=half, wpar=wpar,
                      dstq=dstq, thc_=thc_, nthc=nthc):
                    v = CH(v, V)
                    wait(v, tok("dW%d" % wpar))
                    v.tensor_single_scalar(s1bh[wpar][:, :], w12fh[wpar][:, :],
                                           thc_, OP.is_gt)
                    v.tensor_single_scalar(
                        dstq[q3][:, half * HW2:(half + 1) * HW2],
                        w12fh[wpar][:, :], nthc, OP.is_lt)
                    v.tensor_tensor(dstq[q3][:, half * HW2:(half + 1) * HW2],
                                    s1bh[wpar][:, :],
                                    dstq[q3][:, half * HW2:(half + 1) * HW2],
                                    OP.subtract)
                    p6_piece_done[wpar] = tok(V)
        tern_done[j] = p6_piece_done[(4 * j + 3) % 2]

    for j in range(3):
        p6_tern(j)

    for hs in range(HSN):
        q3 = hs % 3
        for tci in range(NTT):
            idx = hs * NTT + tci
            bsel = 2 * (tci % 4)
            if True:
                @blk.tensor
                def _(t, hs=hs, q3=q3, tci=tci, idx=idx, bsel=bsel):
                    wait(t, tern_done[hs])
                    if hs == 0:
                        wait(t, p5_done[tci])
                    if idx >= 4:
                        wait(t, hdone[idx - 4])
                    te_group_wait(t)
                    for bank, wq in [(bsel + 0, w1q[q3]), (bsel + 1, w2q[q3])]:
                        for dc in range(G):
                            inst = t.matmul(
                                pb[bank][:, :],
                                zqT[:, dc * TC + tci * 128:dc * TC + (tci + 1) * 128],
                                wq[:, dc * 512:(dc + 1) * 512],
                                start=(dc == 0), stop=(dc == G - 1))
                        mark(TE, inst, 1)
                    b12_done[(hs, tci)] = tok(TE)

                @blk.scalar
                def _(s, hs=hs, tci=tci, idx=idx, bsel=bsel):
                    s = CH(s, SC)
                    wait(s, b12_done[(hs, tci)])
                    if idx >= 1:
                        wait(s, hdone[idx - 1])
                    s.activation(out=silu_b[:, :], in_=pb[bsel][:, :], func=AF.Silu,
                                 scale=c1c[:, tci:tci + 1])
                    silu_done[idx] = tok(SC)

                @blk.vector
                def _(v, hs=hs, tci=tci, idx=idx, bsel=bsel):
                    v = CH(v, V)
                    wait(v, silu_done[idx])
                    if idx >= 4:
                        wait(v, hpark[idx - 4])
                    hb = hblk[tci % 4]
                    v.scalar_tensor_tensor(hb[:, :], pb[bsel + 1][:, :],
                                           c2c[:, tci:tci + 1], silu_b[:, :],
                                           OP.mult, OP.mult)
                    v.reduce_max(colb[:, 10:11], hb[:, :], axis=AX,
                                 apply_absolute_value=True)
                    v.tensor_tensor(runmax[:, tci:tci + 1], runmax[:, tci:tci + 1],
                                    colb[:, 10:11], OP.max)
                    hdone[idx] = tok(V)

                @blk.gpsimd
                def _(g, hs=hs, tci=tci, idx=idx):
                    wait(g, hdone[idx])
                    hpark[idx] = mark("dHB%d" % (tci % 4), g.dma_start(
                        out=h_park[tci][:, hs * 512:(hs + 1) * 512],
                        in_=hblk[tci % 4][:, :]), 16)

            if tci == NTT - 1 and hs + 3 < HSN:
                p6_tern(hs + 3)

    if True:
        @blk.vector
        def _(v):
            v = CH(v, V)
            wait(v, hdone[HSN * NTT - 1])
            v.tensor_scalar_max(shc[:, :], runmax[:, :], 1e-5)
            v.reciprocal(shc[:, :], shc[:, :])
            v.tensor_scalar_mul(shc[:, :], shc[:, :], 127.0)
            v.reciprocal(c3c[:, :], shc[:, :])
            v.tensor_single_scalar(c3c[:, :], c3c[:, :], wconst[:, 2:3], OP.mult)
    hscale_done = tok(V)

    free_sc("p6")
    free_sc("zq")
    HQRT = 3 * 512
    hqT = enter_sc("hq", nc.sbuf_tensor("hqT", [128, HCN * TC], BF16))
    hrow = [enter_sc("p7a", nc.sbuf_tensor("hrow%d" % i, [128, HQRT], F32))
            for i in range(4)]
    ht_t = [enter_sc("p7a", nc.sbuf_tensor("ht_t%d" % i, [128, HQRT], F32))
            for i in range(3)]
    it7 = [enter_sc("p7a", nc.sbuf_tensor("it7%d" % i, [128, HQRT], I32))
           for i in range(3)]
    ft7 = [enter_sc("p7a", nc.sbuf_tensor("ft7%d" % i, [128, HQRT], F32))
           for i in range(3)]
    hq_sb = [enter_sc("p7a", nc.sbuf_tensor("hq_sb%d" % i, [128, HID], BF16))
             for i in range(2)]

    p7a_done = []
    piece_done = []
    v1_tok = {}
    sc_tok = {}
    P7PC = [(0, 3), (3, 6), (6, 9), (9, HSN)]
    PIECES = [(tci, hs0, hs1) for tci in range(NTT) for (hs0, hs1) in P7PC]
    NP7 = len(PIECES)

    def p7a_sync(k):
        tci, hs0, hs1 = PIECES[k]
        sz, hp4 = (hs1 - hs0) * 512, k % 4

        @blk.gpsimd
        def _(g, tci=tci, hs0=hs0, hs1=hs1, sz=sz, hp4=hp4, k=k):
            if k == 0:
                for j in range(4):
                    wait(g, tok("dHB%d" % j))
            if k >= 4:
                wait(g, piece_done[k - 4])
            mark("dHR%d" % hp4, g.dma_start(
                out=hrow[hp4][:, 0:sz],
                in_=h_park[tci][:, hs0 * 512:hs1 * 512]), 16)

    def p7a_v1(k):
        tci = PIECES[k][0]
        sz, hp, hp4 = (PIECES[k][2] - PIECES[k][1]) * 512, k % 3, k % 4

        @blk.vector
        def _(v, tci=tci, sz=sz, hp=hp, hp4=hp4):
            v = CH(v, V)
            wait(v, tok("dHR%d" % hp4), hscale_done)
            v.tensor_single_scalar(ht_t[hp][:, 0:sz], hrow[hp4][:, 0:sz],
                                   shc[:, tci:tci + 1], OP.mult)
        v1_tok[k] = tok(V)

    def p7a_sc(k):
        sz, hp = (PIECES[k][2] - PIECES[k][1]) * 512, k % 3

        @blk.scalar
        def _(s, sz=sz, hp=hp, k=k):
            s = CH(s, SC)
            wait(s, v1_tok[k])
            s.copy(it7[hp][:, 0:sz], ht_t[hp][:, 0:sz])
            s.copy(ft7[hp][:, 0:sz], it7[hp][:, 0:sz])
        sc_tok[k] = tok(SC)

    def p7a_v2(k):
        tci, hs0, hs1 = PIECES[k]
        off, sz, hp, hp4 = hs0 * 512, (hs1 - hs0) * 512, k % 3, k % 4

        @blk.vector
        def _(v, tci=tci, off=off, sz=sz, hp=hp, hp4=hp4, k=k):
            v = CH(v, V)
            wait(v, sc_tok[k])
            if len(p7a_done) >= 2:
                wait(v, p7a_done[-2])
            v.tensor_tensor(hrow[hp4][:, 0:sz], ht_t[hp][:, 0:sz],
                            ft7[hp][:, 0:sz], OP.subtract)
            v.scalar_tensor_tensor(ft7[hp][:, 0:sz], hrow[hp4][:, 0:sz],
                                   0.5, ft7[hp][:, 0:sz],
                                   OP.is_gt, OP.add)
            v.scalar_tensor_tensor(hq_sb[tci % 2][:, off:off + sz],
                                   hrow[hp4][:, 0:sz], -0.5,
                                   ft7[hp][:, 0:sz],
                                   OP.is_lt, OP.subtract)
            piece_done.append(tok(V))

        if k % 4 == 3:
            @blk.sync
            def _(sy, tci=tci):
                wait(sy, piece_done[-1])
                h3 = hqT[:, :].rearrange("q (c t) -> q c t", t=TC)[
                    :, :, tci * 128:(tci + 1) * 128]
                p7a_done.append(mark("dTR2" if tci % 2 else "dTR",
                                     sy.dma_start_transpose(h3,
                                                            hq_sb[tci % 2][:, :]),
                                     16))

    p7a_sync(0)
    p7a_sync(1)
    p7a_sync(2)
    p7a_sync(3)
    p7a_v1(0)
    p7a_v1(1)
    p7a_sc(0)
    for i in range(1, NP7):
        if i + 1 < NP7:
            p7a_v1(i + 1)
        p7a_v2(i - 1)
        if i + 3 < NP7:
            p7a_sync(i + 3)
        p7a_sc(i)
    p7a_v2(NP7 - 1)

    free_sc("p7a")
    w3fh = [enter_sc("p7b", nc.sbuf_tensor("w3fh%d" % i, [128, HID // 2], F32))
            for i in range(2)]
    s3bh = [enter_sc("p7b", nc.sbuf_tensor("s3bh%d" % i, [128, HID // 2], BF16))
            for i in range(2)]
    w3q = [enter_sc("p7b", nc.sbuf_tensor("w3q%d" % i, [128, HID], BF16))
           for i in range(3)]
    b3raw = [enter_sc("p7b", nc.sbuf_tensor("b3raw%d" % i, [128, TC], F32))
             for i in range(2)]
    b3strip = [enter_sc("p7b", nc.sbuf_tensor("b3strip%d" % i, [128, TC], F32))
               for i in range(2)]

    tern3_done = [None] * G
    p7b_piece_done = [None, None]
    b3mm_done = [None] * G
    b3ev_done = [None] * G
    b3tr_done = [None] * G
    b3st_done = [None] * G
    b3pk_done = [None] * G
    H2 = HID // 2

    def p7b_tern(j):
        q3 = j % 3
        for half in range(2):
            wpar = (2 * j + half) % 2
            if True:
                @blk.sync
                def _(sy, j=j, half=half, wpar=wpar):
                    if j == 0 and half == 0:
                        wait(sy, tok(V), tok("dTR"), tok("dTR2"))
                    if p7b_piece_done[wpar] is not None:
                        wait(sy, p7b_piece_done[wpar])
                    if half == 0 and j >= 3:
                        wait(sy, b3mm_done[j - 3])
                    mark("dW%d" % wpar, sy.dma_start(
                        out=w3fh[wpar][:, :],
                        in_=w3_t[j][:, half * H2:(half + 1) * H2]), 16)

                @blk.vector
                def _(v, j=j, q3=q3, half=half, wpar=wpar):
                    v = CH(v, V)
                    wait(v, tok("dW%d" % wpar))
                    v.tensor_single_scalar(s3bh[wpar][:, :], w3fh[wpar][:, :],
                                           wconst[:, 5:6], OP.is_gt)
                    v.scalar_tensor_tensor(w3q[q3][:, half * H2:(half + 1) * H2],
                                           w3fh[wpar][:, :], wconst[:, 8:9],
                                           s3bh[wpar][:, :],
                                           OP.is_lt, OP.subtract)
                    p7b_piece_done[wpar] = tok(V)
        tern3_done[j] = p7b_piece_done[(2 * j + 1) % 2]

    for j in range(3):
        p7b_tern(j)

    for dc in range(G):
        p = dc % 2
        q3 = dc % 3
        if True:
            @blk.tensor
            def _(t, dc=dc, p=p, q3=q3):
                wait(t, tern3_done[dc])
                if dc == 0:
                    wait(t, p7a_done[-1])
                if dc >= 2:
                    wait(t, b3ev_done[dc - 2])
                te_group_wait(t)
                for half in range(2):
                    ps = pb[2 * p + half]
                    for hcc in range(HCN):
                        inst = t.matmul(
                            ps[:, :],
                            w3q[q3][:, hcc * 128:(hcc + 1) * 128],
                            hqT[:, hcc * TC + half * 512:hcc * TC + half * 512 + 512],
                            start=(hcc == 0), stop=(hcc == HCN - 1))
                    mark(TE, inst, 1)
                b3mm_done[dc] = tok(TE)

        if dc + 3 < G:
            p7b_tern(dc + 3)

        if True:
            @blk.vector
            def _(v, dc=dc, p=p):
                v = CH(v, V)
                wait(v, b3mm_done[dc])
                if dc >= 2:
                    wait(v, b3tr_done[dc - 2])
                for half in range(2):
                    v.tensor_copy(b3raw[p][:, half * 512:half * 512 + 512],
                                  pb[2 * p + half][:, :])
                b3ev_done[dc] = tok(V)

            @blk.tensor
            def _(t, dc=dc, p=p):
                wait(t, b3ev_done[dc])
                if dc >= 2:
                    wait(t, b3st_done[dc - 2])
                te_group_wait(t)
                for tb in range(NTT):
                    inst = t.transpose(
                        pb[4 + 2 * p + tb // 4][:, (tb % 4) * 128:(tb % 4) * 128 + 128],
                        b3raw[p][:, tb * 128:(tb + 1) * 128], ident[:, :])
                    mark(TE, inst, 1)
                b3tr_done[dc] = tok(TE)

            @blk.vector
            def _(v, dc=dc):
                v = CH(v, V)
                wait(v, b3tr_done[dc])
                if dc >= 2:
                    wait(v, b3pk_done[dc - 2])
                for b in range(2):
                    v.tensor_copy(b3strip[p][:, b * 512:b * 512 + 512],
                                  pb[4 + 2 * p + b][:, :])
                b3st_done[dc] = tok(V)

            @blk.sync
            def _(sy, dc=dc, p=p):
                wait(sy, b3st_done[dc])
                b3pk_done[dc] = mark("dPK%d" % p, sy.dma_start(
                    out=b3_park[dc], in_=b3strip[p][:, :]), 16)

    free_sc("p7b")
    free_sc("hq")
    xt = [enter(nc.sbuf_tensor("xtp8_%d" % i, [128, D], F32)) for i in range(2)]
    ws["xn32"] = enter(nc.sbuf_tensor("xn32c", [128, D], F32))
    b3t = [enter(nc.sbuf_tensor("b3t%d" % i, [128, D], F32)) for i in range(2)]
    ot8 = [ws["xn32"], enter(nc.sbuf_tensor("xn32c2", [128, D], F32))]
    colb5b = [colb, colb2]

    p8_out = [None] * NTT
    p8_front_v = [None] * NTT
    p8_back_v = [None] * NTT

    def p8_front(tci):
        p = tci % 2
        cb = colb5b[p]
        if True:
            @blk.sync
            def _(sy, tci=tci, p=p):
                if tci == 0:
                    wait(sy, tok("dPK"), tok("dPK0"), tok("dPK1"))
                if tci >= 2:
                    wait(sy, p8_back_v[tci - 2])
                mark("dX%d" % p, sy.dma_start(
                    out=b3t[p][:, :],
                    in_=b3_park[:, :, tci * 128:(tci + 1) * 128].rearrange(
                        "dc p q -> p dc q")), 16)
                mark("dX%d" % p, sy.dma_start(out=xt[p][:, :], in_=x1_park[tci]), 16)

            def pre8(v, tci=tci, p=p):
                wait(v._e, tok("dX%d" % p))
                v.tensor_single_scalar(b3t[p][:, :], b3t[p][:, :],
                                       c3c[:, tci:tci + 1], OP.mult)

            p8_front_v[tci] = ln_stats(blk, b3t[p], 0, cb=cb, pre=pre8)

    def p8_back(tci):
        p = tci % 2
        cb = colb5b[p]
        if True:
            @blk.vector
            def _(v, tci=tci, p=p, cb=cb):
                v = CH(v, V)
                wait(v, p8_front_v[tci])
                if tci >= 2:
                    wait(v, p8_out[tci - 2])
                v.tensor_scalar(ot8[p][:, :], b3t[p][:, :], cb[:, 4:5],
                                cb[:, 6:7], OP.mult, OP.add)
                v.tensor_tensor(ot8[p][:, :], ot8[p][:, :], xt[p][:, :], OP.add)
                p8_back_v[tci] = tok(V)

            @blk.sync
            def _(sy, tci=tci, p=p):
                wait(sy, p8_back_v[tci])
                p8_out[tci] = mark("dOUT2" if p else "dOUT", sy.dma_start(
                    out=out_dram[tci * 128:(tci + 1) * 128, :],
                    in_=ot8[p][:, :]), 16)

    p8_front(0)
    for tci in range(1, NTT):
        p8_front(tci)
        p8_back(tci - 1)
    p8_back(NTT - 1)

    if True:
        @blk.gpsimd
        def _(g):
            wait(g, tok("dOUT"), tok("dOUT2"))

    blk.__exit__(None, None, None)
    for h in reversed(stack):
        h.__exit__(None, None, None)
    return nc


def _prep(inputs):
    f = np.ascontiguousarray
    x = np.asarray(inputs["L0_attn_x"], dtype=np.float32).reshape(S, D)
    Wa = np.asarray(inputs["L0_attn_Wa"], dtype=np.float32)
    Wx = np.asarray(inputs["L0_attn_Wx"], dtype=np.float32)
    ba = np.asarray(inputs["L0_attn_ba"], dtype=np.float32)
    bx = np.asarray(inputs["L0_attn_bx"], dtype=np.float32)
    lam = np.asarray(inputs["L0_attn_Lam"], dtype=np.float32)
    w1 = np.asarray(inputs["L0_ffn_w1"], dtype=np.float32)
    w2 = np.asarray(inputs["L0_ffn_w2"], dtype=np.float32)
    w3 = np.asarray(inputs["L0_ffn_w3"], dtype=np.float32)

    wa_t = f(Wa.T.reshape(G, 128, G, 128).transpose(2, 1, 0, 3)
             .reshape(G, 128, G * 128))
    wx_t = f(Wx.T.reshape(G, 128, G, 128).transpose(2, 1, 0, 3)
             .reshape(G, 128, G * 128))
    w12 = f(np.stack([
        w1.T.reshape(G, 128, HSN, 512).transpose(2, 1, 0, 3).reshape(HSN, 128, G * 512),
        w2.T.reshape(G, 128, HSN, 512).transpose(2, 1, 0, 3).reshape(HSN, 128, G * 512),
    ]))
    w3_tiled = f(w3.T.reshape(HCN, 128, G, 128).transpose(2, 1, 0, 3)
                 .reshape(G, 128, HCN * 128))

    def col128(vec):
        return f(vec.reshape(G, 128).T)

    lam_c, ba_c, bx_c = col128(lam), col128(ba), col128(bx)
    ident = np.eye(128, dtype=np.float32)

    def slices(w):
        flat = f(w).reshape(NCORE, 128, NSL)
        return [f(flat[c]) for c in range(NCORE)]

    w1sl, w2sl, w3sl = slices(w1), slices(w2), slices(w3)

    in_maps = []
    for c in range(NCORE):
        mask = np.zeros((128, 2 * NCORE), dtype=np.float32)
        mask[:, 0:NCORE] = (np.arange(NCORE) < c).astype(np.float32)[None, :]
        mask[:, NCORE:] = 1.0 - mask[:, 0:NCORE]
        in_maps.append({
            "x": f(x[c * TC:(c + 1) * TC]),
            "wa_t": wa_t, "wx_t": wx_t, "w12_t": w12, "w3_t": w3_tiled,
            "lam": lam_c, "ba": ba_c, "bx": bx_c,
            "mask": mask, "ident": ident,
            "onesc": np.ones((128, 1), dtype=np.float32),
            "onesr": np.ones((1, 128), dtype=np.float32),
            "w1s": w1sl[c], "w2s": w2sl[c], "w3s": w3sl[c],
        })
    return in_maps


def kernel(**inputs):
    in_maps = _prep(inputs)
    nc = build_nc()
    res = run_bass_kernel_spmd(nc, in_maps, core_ids=list(range(NCORE)))
    out = np.concatenate([np.asarray(r["out"]) for r in res.results], axis=0)
    return out.reshape(1, S, D).astype(np.float32)


# version 21
# speedup vs baseline: 1.1295x; 1.0494x over previous
import numpy as np
import concourse.bass as bass
import concourse.mybir as mybir
from concourse.bass_utils import run_bass_kernel_spmd

F32 = mybir.dt.float32
BF16 = mybir.dt.bfloat16
I32 = mybir.dt.int32
AF = mybir.ActivationFunctionType
OP = mybir.AluOpType
AX = mybir.AxisListType.X

NCORE = 8
S, D, HID = 8192, 2048, 5632
TC = S // NCORE
NTT = TC // 128
G = D // 128
HCN = HID // 128
HSN = HID // 512
NSL = HID * D // NCORE // 128
EPS = float(np.float32(1e-5))

_CHAIN_OPS = frozenset([
    "tensor_tensor", "tensor_add", "tensor_sub", "tensor_mul", "tensor_max",
    "tensor_scalar", "tensor_single_scalar", "tensor_scalar_add",
    "tensor_scalar_sub", "tensor_scalar_mul", "tensor_scalar_max",
    "tensor_scalar_min", "tensor_copy", "tensor_relu", "tensor_reduce",
    "reduce_max", "reduce_sum", "scalar_tensor_tensor", "tensor_tensor_scan",
    "memset", "reciprocal", "select", "copy_predicated", "bn_stats",
    "activation", "copy", "add", "mul",
    "partition_all_reduce", "partition_broadcast",
])


class _Chained:

    def __init__(self, eng, sem, cnt, name):
        self._e = eng
        self._sem = sem
        self._cnt = cnt
        self._name = name

    def __getattr__(self, attr):
        real = getattr(self._e, attr)
        if attr not in _CHAIN_OPS:
            return real
        cnt, name = self._cnt, self._name

        def wrapped(*a, **k):
            if cnt[name] > 0:
                self._e.wait_ge(self._sem, cnt[name])
            inst = real(*a, **k)
            inst.then_inc(self._sem, 1)
            cnt[name] += 1
            return inst
        return wrapped


def build_nc():
    nc = bass.Bass("TRN2", target_bir_lowering=False)

    x_in = nc.dram_tensor("x", [TC, D], F32, kind="ExternalInput").ap()
    wa_t = nc.dram_tensor("wa_t", [G, 128, G * 128], F32, kind="ExternalInput").ap()
    wx_t = nc.dram_tensor("wx_t", [G, 128, G * 128], F32, kind="ExternalInput").ap()
    w12_t = nc.dram_tensor("w12_t", [2, HSN, 128, G * 512], F32,
                           kind="ExternalInput").ap()
    w3_t = nc.dram_tensor("w3_t", [G, 128, HCN * 128], F32, kind="ExternalInput").ap()
    lam_in = nc.dram_tensor("lam", [128, G], F32, kind="ExternalInput").ap()
    ba_in = nc.dram_tensor("ba", [128, G], F32, kind="ExternalInput").ap()
    bx_in = nc.dram_tensor("bx", [128, G], F32, kind="ExternalInput").ap()
    mask_in = nc.dram_tensor("mask", [128, 2 * NCORE], F32, kind="ExternalInput").ap()
    ident_in = nc.dram_tensor("ident", [128, 128], F32, kind="ExternalInput").ap()
    onesc_in = nc.dram_tensor("onesc", [128, 1], F32, kind="ExternalInput").ap()
    onesr_in = nc.dram_tensor("onesr", [1, 128], F32, kind="ExternalInput").ap()
    w1s_in = nc.dram_tensor("w1s", [128, NSL], F32, kind="ExternalInput").ap()
    w2s_in = nc.dram_tensor("w2s", [128, NSL], F32, kind="ExternalInput").ap()
    w3s_in = nc.dram_tensor("w3s", [128, NSL], F32, kind="ExternalInput").ap()

    out_dram = nc.dram_tensor("out", [TC, D], F32, kind="ExternalOutput").ap()

    a_spill = nc.dram_tensor("a_spill", [G, 128, TC], F32).ap()
    u_spill = nc.dram_tensor("u_spill", [G, 128, TC], F32).ap()
    h_park = nc.dram_tensor("h_park", [NTT, 128, HSN * 512], F32).ap()
    x1_park = nc.dram_tensor("x1_park", [NTT, 128, D], F32).ap()
    b3_park = nc.dram_tensor("b3_park", [G, 128, TC], F32).ap()
    cc_sum_in = nc.dram_tensor("cc_sum_in", [4, 1], F32).ap()
    cc_sum_out = nc.dram_tensor("cc_sum_out", [4, 1], F32).ap()
    cc_ag_in = nc.dram_tensor("cc_ag_in", [128, 2 * G], F32).ap()
    cc_ag_out = nc.dram_tensor("cc_ag_out", [NCORE * 128, 2 * G], F32).ap()

    stack = []

    def enter(h):
        t = h.__enter__()
        stack.append(h)
        return t

    scopes = {}

    def enter_sc(name, h):
        t = h.__enter__()
        scopes.setdefault(name, []).append(h)
        return t

    def free_sc(name):
        for h in reversed(scopes.pop(name, [])):
            h.__exit__(None, None, None)

    sems, cnt = {}, {}
    for n in ["sV", "sS", "sT", "sG",
              "dPRE", "dW0", "dW1", "dX0", "dX1", "dSP0", "dSP1",
              "dRL0", "dRL1", "dCC", "dHP0", "dHP1",
              "dHR0", "dHR1", "dHR2", "dHR3", "dWA0", "dWA1",
              "dHB0", "dHB1", "dHB2", "dHB3", "dHB4", "dHB5", "dHB6", "dHB7",
              "dPK", "dPK0", "dPK1", "dOUT", "dOUT2", "dTR", "dTR2", "cc"]:
        sems[n] = enter(nc.semaphore(n))
        cnt[n] = 0

    def mark(name, inst, step):
        inst.then_inc(sems[name], step)
        cnt[name] += step
        return (name, cnt[name])

    def tok(name):
        return (name, cnt[name])

    def wait(eng, *tokens):
        for t_ in tokens:
            if t_ is None:
                continue
            name, v = t_
            if v > 0:
                eng.wait_ge(sems[name], v)

    V, SC, TE, GP = "sV", "sS", "sT", "sG"

    def CH(eng, name):
        return _Chained(eng, sems[name], cnt, name)

    def te_group_wait(t):
        if cnt[TE] > 0:
            t.wait_ge(sems[TE], cnt[TE])

    ident = enter(nc.sbuf_tensor("identt", [128, 128], F32))
    lam = enter(nc.sbuf_tensor("lamt", [128, G], F32))
    bav = enter(nc.sbuf_tensor("bav", [128, G], F32))
    bxv = enter(nc.sbuf_tensor("bxv", [128, G], F32))
    coef = enter(nc.sbuf_tensor("coef", [128, G], F32))
    smallA = enter(nc.sbuf_tensor("smallA", [128, G], F32))
    maskt = enter(nc.sbuf_tensor("maskt", [128, 2 * NCORE], F32))
    wsums = enter(nc.sbuf_tensor("wsums", [128, 4], F32))
    wsall = enter(nc.sbuf_tensor("wsall", [128, 4], F32))
    wsl2 = enter(nc.sbuf_tensor("wsl2", [128, NSL // 8], F32))
    wtmp = enter(nc.sbuf_tensor("wtmp", [128, 2], F32))
    onesc = enter(nc.sbuf_tensor("onesct", [128, 1], F32))
    onesr = enter(nc.sbuf_tensor("onesrt", [1, 128], F32))
    wconst = enter(nc.sbuf_tensor("wconst", [128, 16], F32))
    colAH = enter(nc.sbuf_tensor("colAH", [128, 2 * G], F32))
    gat = enter(nc.sbuf_tensor("gat", [128, NCORE * 2 * G], F32))
    Hcol = enter(nc.sbuf_tensor("Hcol", [128, G], F32))
    tmpG = enter(nc.sbuf_tensor("tmpG", [128, G], F32))
    tmpS = enter(nc.sbuf_tensor("tmpS", [128, G], F32))
    colb = enter(nc.sbuf_tensor("colb", [128, 16], F32))
    colb2 = enter(nc.sbuf_tensor("colb2", [128, 16], F32))
    bstb = enter(nc.sbuf_tensor("bstb", [128, 32], F32))
    bstb2 = enter(nc.sbuf_tensor("bstb2", [128, 32], F32))
    sinvz = enter(nc.sbuf_tensor("sinvz", [128, NTT], F32))
    c1c = enter(nc.sbuf_tensor("c1c", [128, NTT], F32))
    c2c = enter(nc.sbuf_tensor("c2c", [128, NTT], F32))
    c3c = enter(nc.sbuf_tensor("c3c", [128, NTT], F32))
    runmax = enter(nc.sbuf_tensor("runmax", [128, NTT], F32))
    shc = enter(nc.sbuf_tensor("shc", [128, NTT], F32))
    zbuf = enter(nc.sbuf_tensor("zbuf", [128, TC], F32))
    ws = {}
    wsl = enter_sc("p0", nc.sbuf_tensor("wsl", [128, NSL], F32))

    pb = [enter(nc.psum_tensor("pb%d" % k, [128, 512], F32)) for k in range(8)]

    blk = nc.Block("ph0"); blk.__enter__()
    if True:
        @blk.sync
        def _(sy):
            for dst, src in [(ident, ident_in), (lam, lam_in), (bav, ba_in),
                             (bxv, bx_in), (maskt, mask_in),
                             (onesc, onesc_in), (onesr, onesr_in)]:
                mark("dPRE", sy.dma_start(out=dst[:, :], in_=src), 16)

        @blk.scalar
        def _(s):
            s = CH(s, SC)
            wait(s, tok("dPRE"))
            s.activation(out=smallA[:, :], in_=lam[:, :], func=AF.Exp, scale=-1.0)

        @blk.vector
        def _(v):
            v = CH(v, V)
            wait(v, tok(SC), tok("dPRE"))
            v.memset(zbuf[:, :], 0.0)
            v.memset(Hcol[:, :], 0.0)
            v.memset(runmax[:, :], 0.0)
            e, t_ = smallA, coef
            v.tensor_scalar(t_[:, :], e[:, :], -0.2, 0.25, OP.mult, OP.add)
            v.scalar_tensor_tensor(t_[:, :], e[:, :], -1.0, t_[:, :], OP.mult, OP.mult)
            v.tensor_scalar_add(t_[:, :], t_[:, :], float(np.float32(1.0 / 3)))
            v.scalar_tensor_tensor(t_[:, :], e[:, :], -1.0, t_[:, :], OP.mult, OP.mult)
            v.tensor_scalar_add(t_[:, :], t_[:, :], 0.5)
            v.scalar_tensor_tensor(t_[:, :], e[:, :], -1.0, t_[:, :], OP.mult, OP.mult)
            v.tensor_scalar_add(t_[:, :], t_[:, :], 1.0)
            v.tensor_tensor(t_[:, :], e[:, :], t_[:, :], OP.mult)
            v.tensor_scalar_mul(t_[:, :], t_[:, :], -8.0)
            v.memset(wsums[:, 0:4], 0.0)

    wred = []
    p0t = {}
    WPC = NSL // 8

    def p0_wpiece(i):
        j, pc = i // 8, i % 8
        src = [w1s_in, w2s_in, w3s_in][j]

        @blk.sync
        def _(sy, i=i, pc=pc, src=src):
            if i >= 1:
                wait(sy, wred[i - 1])
            mark("dPRE", sy.dma_start(
                out=wsl2[:, :], in_=src[:, pc * WPC:(pc + 1) * WPC]), 16)

        @blk.vector
        def _(v, j=j):
            v = CH(v, V)
            wait(v, tok("dPRE"))
            v.reduce_sum(wtmp[:, 0:1], wsl2[:, :], axis=AX,
                         apply_absolute_value=True)
            v.tensor_tensor(wsums[:, j:j + 1], wsums[:, j:j + 1],
                            wtmp[:, 0:1], OP.add)
        wred.append(tok(V))

    def p0_wsum():
        @blk.tensor
        def _(t):
            wait(t, wred[23], gate_evac[0])
            te_group_wait(t)
            mark(TE, t.matmul(pb[0][0:4, 0:1], wsums[:, 0:4], onesc[:, 0:1],
                              start=True, stop=True), 1)

        @blk.vector
        def _(v):
            v = CH(v, V)
            wait(v, tok(TE))
            v.tensor_copy(wsall[0:4, 0:1], pb[0][0:4, 0:1])
        p0t["wsall"] = tok(V)

        @blk.gpsimd
        def _(g):
            wait(g, p0t["wsall"])
            mark("dCC", g.dma_start(out=cc_sum_in, in_=wsall[0:4, 0:1]), 16)
            wait(g, tok("dCC"))
            mark("cc", g.collective_compute(
                "AllReduce", OP.add, replica_groups=[list(range(NCORE))],
                ins=[cc_sum_in.opt()], outs=[cc_sum_out.opt()]), 1)

    def p0_wcc():
        @blk.gpsimd
        def _(g):
            wait(g, tok("cc"))
            mark("dCC", g.dma_start(out=wconst[0:1, 12:16],
                                    in_=cc_sum_out.rearrange("p f -> f p")), 16)

        @blk.vector
        def _(v):
            v = CH(v, V)
            wait(v, tok("dCC"))
            r0 = wconst[0:1, :]
            v.tensor_scalar(r0[:, 0:3], r0[:, 12:15],
                            float(np.float32(1.0 / (HID * D))),
                            1e-5, OP.mult, OP.max)
            v.tensor_scalar_mul(r0[:, 3:6], r0[:, 0:3], 0.5)
            v.tensor_scalar_mul(r0[:, 6:9], r0[:, 3:6], -1.0)
        p0t["wscale"] = tok(V)

    def p0_wbcast():
        @blk.tensor
        def _(t):
            wait(t, p0t["wscale"], gate_evac[0])
            te_group_wait(t)
            mark(TE, t.matmul(pb[1][:, 0:9], onesr[0:1, :], wconst[0:1, 0:9],
                              start=True, stop=True), 1)

        @blk.vector
        def _(v):
            v = CH(v, V)
            wait(v, tok(TE))
            v.tensor_copy(wconst[:, 0:9], pb[1][:, 0:9])
        p0t["wconst"] = tok(V)

    def ln_stats(blk, src_t, c0=0, cb=None, pre=None):
        if cb is None:
            cb = colb
        bst = bstb if cb is colb else bstb2

        @blk.vector
        def _(v):
            v = CH(v, V)
            if pre is not None:
                pre(v)
            for qq in range(4):
                v.bn_stats(bst[:, 6 * qq:6 * qq + 6],
                           src_t[:, c0 + 512 * qq:c0 + 512 * qq + 512])
            v.tensor_tensor(bst[:, 24:32], bst[:, 1:24:3], bst[:, 1:24:3],
                            OP.mult)
            v.reduce_sum(cb[:, 7:8], bst[:, 1:24:3], axis=AX)
            v.reduce_sum(cb[:, 8:9], bst[:, 24:32], axis=AX)
            v.reduce_sum(cb[:, 9:10], bst[:, 2:24:3], axis=AX)
            v.tensor_scalar_mul(cb[:, 5:6], cb[:, 7:8], -0.125)
            v.tensor_tensor(cb[:, 12:13], cb[:, 5:6], cb[:, 5:6], OP.mult)
            v.scalar_tensor_tensor(cb[:, 11:12], cb[:, 8:9], 0.125,
                                   cb[:, 12:13], OP.mult, OP.subtract)
            v.tensor_scalar(cb[:, 12:13], cb[:, 9:10],
                            float(np.float32(1.0 / D)), EPS, OP.mult, OP.add)
            v.tensor_tensor(cb[:, 11:12], cb[:, 11:12], cb[:, 12:13], OP.add)
            v.reciprocal(cb[:, 2:3], cb[:, 11:12])

        @blk.scalar
        def _(s):
            s = CH(s, SC)
            wait(s, tok(V))
            s.activation(out=cb[:, 4:5], in_=cb[:, 2:3], func=AF.Sqrt)
            s.activation(out=cb[:, 6:7], in_=cb[:, 5:6], func=AF.Identity,
                         scale=cb[:, 4:5])
        return tok(SC)

    free_sc("p0")
    xnT_hi = enter_sc("p12", nc.sbuf_tensor("xnT_hi", [128, G * TC], BF16))
    xnT_lo = enter_sc("p12", nc.sbuf_tensor("xnT_lo", [128, G * TC], BF16))
    wafh = enter_sc("p12", nc.sbuf_tensor("wafh", [128, D], F32))
    wahi = [enter_sc("p12", nc.sbuf_tensor("wahi%d" % i, [128, D], BF16)) for i in range(3)]
    walo = [enter_sc("p12", nc.sbuf_tensor("walo%d" % i, [128, D], BF16)) for i in range(3)]
    wxhi = [enter_sc("p12", nc.sbuf_tensor("wxhi%d" % i, [128, D], BF16)) for i in range(3)]
    wxlo = [enter_sc("p12", nc.sbuf_tensor("wxlo%d" % i, [128, D], BF16)) for i in range(3)]

    split_done = [None] * G
    mm_done = [None] * G

    def p2_prep(j):
        q = j % 3
        for srcw, dsth, dstl in [(wa_t, wahi, walo), (wx_t, wxhi, wxlo)]:
            if True:
                @blk.gpsimd
                def _(g, j=j, q=q, srcw=srcw):
                    if j >= 1 and srcw is wa_t:
                        wait(g, split_done[j - 1])
                    elif srcw is wx_t:
                        wait(g, tok(V))
                    if j >= 3 and srcw is wa_t:
                        wait(g, mm_done[j - 3])
                    mark("dWA%d" % (j % 2), g.dma_start(out=wafh[:, :],
                                                        in_=srcw[j]), 16)

                @blk.scalar
                def _(s, j=j, q=q, dsth=dsth):
                    s = CH(s, SC)
                    wait(s, tok("dWA%d" % (j % 2)))
                    s.activation(out=dsth[q][:, :], in_=wafh[:, :], func=AF.Copy)

                @blk.vector
                def _(v, j=j, q=q, dsth=dsth, dstl=dstl):
                    v = CH(v, V)
                    wait(v, tok(SC))
                    v.tensor_tensor(dstl[q][:, :], wafh[:, :], dsth[q][:, :],
                                    OP.subtract)
        split_done[j] = tok(V)

    hi_t = enter_sc("p1x", nc.sbuf_tensor("hi_t", [128, D], BF16))
    lo_t = enter_sc("p1x", nc.sbuf_tensor("lo_t", [128, D], BF16))
    xt1 = enter_sc("p1x", nc.sbuf_tensor("xt1", [128, D], F32))
    ws["xn32"] = enter_sc("p1x", nc.sbuf_tensor("xn32a", [128, D], F32))

    xt1b = enter_sc("p1x", nc.sbuf_tensor("xt1b", [128, D], F32))
    xn1 = [ws["xn32"],
           enter_sc("p1x", nc.sbuf_tensor("xn32a2", [128, D], F32))]
    hi2 = [hi_t, enter_sc("p1x", nc.sbuf_tensor("hi_t2", [128, D], BF16))]
    lo2 = [lo_t, enter_sc("p1x", nc.sbuf_tensor("lo_t2", [128, D], BF16))]
    xtb = [xt1, xt1b]
    colb5 = [colb, colb2]

    p1_xbar_done = []
    p1_back_s = [None] * NTT
    p1_front_v = [None] * NTT

    def p1_front(tci):
        p = tci % 2
        cb = colb5[p]
        if True:
            @blk.sync
            def _(sy, tci=tci, p=p):
                if tci >= 2:
                    wait(sy, p1_back_s[tci - 2])
                mark("dX%d" % p, sy.dma_start(
                    out=xtb[p][:, :], in_=x_in[tci * 128:(tci + 1) * 128, :]), 16)

            def pre(v, p=p):
                wait(v._e, tok("dX%d" % p))

            ln_stats(blk, xtb[p], 0, cb=cb, pre=pre)
            p1_front_v[tci] = tok(V)

    def p1_back(tci):
        p = tci % 2
        cb = colb5[p]
        if True:
            @blk.scalar
            def _(s, tci=tci, p=p, cb=cb):
                s = CH(s, SC)
                wait(s, p1_front_v[tci])
                if tci >= 2:
                    wait(s, p1_xbar_done[tci - 2])
                s.activation(out=xn1[p][:, :], in_=xtb[p][:, :], func=AF.Identity,
                             bias=cb[:, 6:7], scale=cb[:, 4:5])
                p1_back_s[tci] = tok(SC)

            @blk.vector
            def _(v, tci=tci, p=p):
                v = CH(v, V)
                wait(v, tok(SC))
                v.tensor_copy(hi2[p][:, :], xn1[p][:, :])
                v.tensor_tensor(lo2[p][:, :], xn1[p][:, :], hi2[p][:, :],
                                OP.subtract)

            @blk.sync
            def _(sy, tci=tci, p=p):
                wait(sy, tok(V))
                hi3 = xnT_hi[:, :].rearrange("q (c t) -> q c t", t=TC)[
                    :, :, tci * 128:(tci + 1) * 128]
                lo3 = xnT_lo[:, :].rearrange("q (c t) -> q c t", t=TC)[
                    :, :, tci * 128:(tci + 1) * 128]
                mark("dTR2" if tci % 2 else "dTR",
                     sy.dma_start_transpose(hi3, hi2[p][:, :]), 16)
                p1_xbar_done.append(
                    mark("dTR2" if tci % 2 else "dTR",
                         sy.dma_start_transpose(lo3, lo2[p][:, :]), 16))

    p1_front(0)
    for tci in range(1, NTT):
        p1_front(tci)
        p1_back(tci - 1)
    p1_back(NTT - 1)

    for j in range(3):
        p2_prep(j)

    free_sc("p1x")
    rgt = enter_sc("p12", nc.sbuf_tensor("rgt", [128, TC], F32))
    igt = enter_sc("p12", nc.sbuf_tensor("igt", [128, TC], F32))
    agt = [enter_sc("p12", nc.sbuf_tensor("agt%d" % i, [128, TC], F32)) for i in range(2)]
    ugt = [enter_sc("p12", nc.sbuf_tensor("ugt%d" % i, [128, TC], F32)) for i in range(2)]
    xnr = enter_sc("p12", nc.sbuf_tensor("xnr", [128, TC], F32))
    gsc = enter_sc("p12", nc.sbuf_tensor("gsc", [128, TC], F32))
    y0b = enter_sc("p12", nc.sbuf_tensor("y0b", [128, TC], F32))
    afl = enter_sc("p12", nc.sbuf_tensor("afl", [128, TC], F32))

    gate_evac = [None, None]
    spill_done = [None, None]
    scan1_done = [None] * G

    for hc in range(G):
        p = hc % 2
        q = hc % 3
        if True:
            @blk.tensor
            def _(t, hc=hc, p=p, q=q):
                wait(t, split_done[hc])
                if hc == 0:
                    wait(t, tok("dTR"), tok("dTR2"))
                if hc == 14:
                    wait(t, p0t["wsall"])
                if hc >= 2:
                    wait(t, gate_evac[p])
                te_group_wait(t)
                for bank, whi, wlo in [(0, wahi[q], walo[q]),
                                       (2, wxhi[q], wxlo[q])]:
                    for half in range(2):
                        ps = pb[bank + half + 4 * p]
                        first = True
                        for wtile, xtile in [(whi, xnT_hi), (whi, xnT_lo),
                                             (wlo, xnT_hi)]:
                            for dc in range(G):
                                inst = t.matmul(
                                    ps[:, :],
                                    wtile[:, dc * 128:(dc + 1) * 128],
                                    xtile[:, dc * TC + half * 512:
                                          dc * TC + half * 512 + 512],
                                    start=first,
                                    stop=(wtile is wlo and dc == G - 1))
                                first = False
                        mark(TE, inst, 1)
                mm_done[hc] = tok(TE)

        if hc + 3 < G:
            p2_prep(hc + 3)

        if True:
            @blk.scalar
            def _(s, hc=hc, p=p):
                s = CH(s, SC)
                wait(s, mm_done[hc])
                if spill_done[p] is not None:
                    wait(s, spill_done[p])
                for half in range(2):
                    s.activation(out=rgt[:, half * 512:half * 512 + 512],
                                 in_=pb[0 + half + 4 * p][:, :], func=AF.Sigmoid,
                                 bias=bav[:, hc:hc + 1])
                for half in range(2):
                    s.activation(out=igt[:, half * 512:half * 512 + 512],
                                 in_=pb[2 + half + 4 * p][:, :], func=AF.Sigmoid,
                                 bias=bxv[:, hc:hc + 1])
                s.activation(out=agt[p][:, :], in_=rgt[:, :], func=AF.Exp,
                             scale=coef[:, hc:hc + 1])
                gate_evac[p] = tok(SC)

            @blk.vector
            def _(v, hc=hc, p=p):
                v = CH(v, V)
                wait(v, gate_evac[p])
                v.tensor_tensor(xnr[:, :],
                                xnT_hi[:, hc * TC:(hc + 1) * TC],
                                xnT_lo[:, hc * TC:(hc + 1) * TC], OP.add)
                v.tensor_tensor(xnr[:, :], igt[:, :], xnr[:, :], OP.mult)
                v.tensor_tensor(gsc[:, :], agt[p][:, :], agt[p][:, :], OP.mult)
                v.tensor_scalar(gsc[:, :], gsc[:, :], -1.0, 1.0, OP.mult, OP.add)

            @blk.scalar
            def _(s, hc=hc):
                s = CH(s, SC)
                wait(s, tok(V))
                s.activation(out=gsc[:, :], in_=gsc[:, :], func=AF.Sqrt)

            @blk.vector
            def _(v, hc=hc, p=p):
                v = CH(v, V)
                wait(v, tok(SC))
                if spill_done[p] is not None:
                    wait(v, spill_done[p])
                v.tensor_tensor(ugt[p][:, :], gsc[:, :], xnr[:, :], OP.mult)
                v.tensor_tensor_scan(y0b[:, :], agt[p][:, :], ugt[p][:, :],
                                     0.0, OP.mult, OP.add)
                v.tensor_copy(colAH[:, G + hc:G + hc + 1], y0b[:, TC - 1:TC])
                v.tensor_tensor_scan(afl[:, :], agt[p][:, :], zbuf[:, :],
                                     1.0, OP.mult, OP.add)
                v.tensor_copy(colAH[:, hc:hc + 1], afl[:, TC - 1:TC])
                scan1_done[hc] = tok(V)

            @blk.gpsimd
            def _(g, hc=hc, p=p):
                wait(g, scan1_done[hc])
                mark("dSP%d" % p, g.dma_start(out=a_spill[hc], in_=agt[p][:, :]), 16)
                spill_done[p] = mark("dSP%d" % p,
                                     g.dma_start(out=u_spill[hc], in_=ugt[p][:, :]),
                                     16)

        if hc < 12:
            p0_wpiece(2 * hc)
            p0_wpiece(2 * hc + 1)
        elif hc == 12:
            p0_wsum()
        elif hc == 13:
            p0_wcc()
        elif hc == 14:
            p0_wbcast()

    if True:
        @blk.gpsimd
        def _(g):
            wait(g, scan1_done[G - 2], scan1_done[G - 1])
            mark("dCC", g.dma_start(out=cc_ag_in, in_=colAH[:, :]), 16)
            wait(g, tok("dCC"))
            mark("cc", g.collective_compute(
                "AllGather", OP.bypass, replica_groups=[list(range(NCORE))],
                ins=[cc_ag_in.opt()], outs=[cc_ag_out.opt()]), 1)
            wait(g, tok("cc"))
            mark("dCC", g.dma_start(
                out=gat[:, :],
                in_=cc_ag_out.rearrange("(r p) j -> p r j", p=128)), 16)

        @blk.vector
        def _(v):
            v = CH(v, V)
            wait(v, tok("dCC"))
            for k in range(NCORE - 1):
                base = k * 2 * G
                v.tensor_scalar(tmpG[:, :], gat[:, base:base + G],
                                maskt[:, k:k + 1],
                                maskt[:, NCORE + k:NCORE + k + 1],
                                OP.mult, OP.add)
                v.tensor_single_scalar(tmpS[:, :], gat[:, base + G:base + 2 * G],
                                       maskt[:, k:k + 1], OP.mult)
                v.tensor_tensor(Hcol[:, :], Hcol[:, :], tmpG[:, :], OP.mult)
                v.tensor_tensor(Hcol[:, :], Hcol[:, :], tmpS[:, :], OP.add)
    fold_done = tok(V)

    free_sc("p12")
    zqT = enter_sc("zq", nc.sbuf_tensor("zqT", [128, G * TC], BF16))
    yT = enter_sc("yt", nc.sbuf_tensor("yT", [128, NTT * D], F32))
    arl = [enter_sc("p4", nc.sbuf_tensor("arl%d" % i, [128, TC], F32)) for i in range(2)]
    url = [enter_sc("p4", nc.sbuf_tensor("url%d" % i, [128, TC], F32)) for i in range(2)]
    ybuf = [enter_sc("p4", nc.sbuf_tensor("ybuf%d" % i, [128, TC], F32)) for i in range(2)]

    scan2_done = [None, None]
    tr_done = [None, None]
    evac_done = [None, None]
    for hc in range(G):
        p = hc % 2
        if True:
            @blk.sync
            def _(sy, hc=hc, p=p):
                if hc == 0:
                    wait(sy, tok("dSP0"), tok("dSP1"))
                if hc >= 2:
                    wait(sy, scan2_done[p])
                mark("dRL%d" % p, sy.dma_start(out=arl[p][:, :], in_=a_spill[hc]), 16)
                mark("dRL%d" % p, sy.dma_start(out=url[p][:, :], in_=u_spill[hc]), 16)

            @blk.vector
            def _(v, hc=hc, p=p):
                v = CH(v, V)
                wait(v, tok("dRL%d" % p), fold_done)
                if hc >= 2:
                    wait(v, tr_done[p])
                v.tensor_tensor_scan(ybuf[p][:, :], arl[p][:, :], url[p][:, :],
                                     Hcol[:, hc:hc + 1], OP.mult, OP.add)
                scan2_done[p] = tok(V)

            @blk.tensor
            def _(t, hc=hc, p=p):
                wait(t, scan2_done[p])
                if hc == 0:
                    wait(t, p0t["wconst"])
                if hc >= 2:
                    wait(t, evac_done[p])
                te_group_wait(t)
                for tb in range(NTT):
                    inst = t.transpose(
                        pb[4 * p + tb // 4][:, (tb % 4) * 128:(tb % 4) * 128 + 128],
                        ybuf[p][:, tb * 128:(tb + 1) * 128], ident[:, :])
                    mark(TE, inst, 1)
                tr_done[p] = tok(TE)

            @blk.vector
            def _(v, hc=hc, p=p):
                v = CH(v, V)
                wait(v, tr_done[p])
                for b in range(2):
                    out3 = yT[:, :].rearrange("p (tb d) -> p tb d", d=D)[
                        :, 4 * b:4 * b + 4, hc * 128:(hc + 1) * 128]
                    v.tensor_copy(out3, pb[4 * p + b][:, :].rearrange(
                        "p (tb q) -> p tb q", q=128))
                evac_done[p] = tok(V)

    free_sc("p4")
    xt = [enter_sc("p5", nc.sbuf_tensor("xtp5_%d" % i, [128, D], F32))
          for i in range(2)]
    cent = [enter_sc("p5", nc.sbuf_tensor("cent%d" % i, [128, D], F32))
            for i in range(2)]
    x15 = [enter_sc("p5", nc.sbuf_tensor("x15_%d" % i, [128, D], F32))
           for i in range(2)]
    it2 = enter_sc("p5", nc.sbuf_tensor("it2", [128, D], I32))
    ft2 = [enter_sc("p5", nc.sbuf_tensor("ft2_%d" % i, [128, D], F32))
           for i in range(2)]
    ds2 = enter_sc("p5", nc.sbuf_tensor("ds2", [128, D], F32))
    zqt_sb = [enter_sc("p5", nc.sbuf_tensor("zqt_sb%d" % i, [128, D], BF16))
              for i in range(2)]
    colb5 = [colb, colb2]

    p5_done = []
    p5_back_s = [None] * NTT
    p5_park = [None] * NTT
    p5_vb = [None] * NTT
    p5_vc = [None] * NTT
    p5_sb = [None] * NTT
    p5_vd = [None] * NTT
    p5_bb = [None] * NTT

    def p5_sy0(t):
        p = t % 2

        @blk.sync
        def _(sy, t=t, p=p):
            if t < 2:
                wait(sy, scan2_done[0], scan2_done[1],
                     tr_done[0], tr_done[1])
            if t >= 2:
                wait(sy, p5_done[t - 2])
            mark("dX%d" % p, sy.dma_start(
                out=xt[p][:, :], in_=x_in[t * 128:(t + 1) * 128, :]), 16)

    def p5_va(t):
        p = t % 2

        def pre5(v, t=t, p=p):
            if t == 0:
                wait(v._e, evac_done[0], evac_done[1])
            if t >= 2 and p5_back_s[t - 2] is not None:
                wait(v._e, p5_back_s[t - 2])
        return ln_stats(blk, yT, t * D, cb=colb5[p], pre=pre5)

    def p5_vb_f(t, lnA_s):
        p = t % 2
        cb = colb5[p]

        @blk.vector
        def _(v, t=t, p=p, cb=cb, lnA_s=lnA_s):
            v = CH(v, V)
            wait(v, tok("dX%d" % p), lnA_s)
            v.tensor_scalar(x15[p][:, :], yT[:, t * D:(t + 1) * D],
                            cb[:, 4:5], cb[:, 6:7], OP.mult, OP.add)
            v.tensor_tensor(x15[p][:, :], x15[p][:, :], xt[p][:, :], OP.add)
        p5_vb[t] = tok(V)

    def p5_sy1(t):
        p = t % 2

        @blk.sync
        def _(sy, t=t, p=p):
            wait(sy, p5_vb[t])
            p5_park[t] = mark("dPK%d" % p, sy.dma_start(
                out=x1_park[t], in_=x15[p][:, :]), 16)

    def p5_vc_f(t):
        p = t % 2
        cb = colb5[p]
        bst = bstb if cb is colb else bstb2

        @blk.vector
        def _(v, t=t, p=p, cb=cb, bst=bst):
            v = CH(v, V)
            for qq in range(4):
                v.bn_stats(bst[:, 6 * qq:6 * qq + 6],
                           x15[p][:, 512 * qq:512 * qq + 512])
            v.tensor_tensor(bst[:, 24:32], bst[:, 1:24:3], bst[:, 1:24:3],
                            OP.mult)
            v.reduce_sum(cb[:, 0:1], bst[:, 1:24:3], axis=AX)
            v.reduce_sum(cb[:, 1:2], bst[:, 24:32], axis=AX)
            v.reduce_sum(cb[:, 3:4], bst[:, 2:24:3], axis=AX)
            v.tensor_scalar_mul(cb[:, 5:6], cb[:, 0:1], -0.125)
            v.tensor_tensor(cb[:, 12:13], cb[:, 5:6], cb[:, 5:6], OP.mult)
            v.scalar_tensor_tensor(cb[:, 11:12], cb[:, 1:2], 0.125,
                                   cb[:, 12:13], OP.mult, OP.subtract)
            v.tensor_scalar(cb[:, 13:14], cb[:, 3:4],
                            float(np.float32(1.0 / D)), EPS, OP.mult, OP.add)
            v.tensor_tensor(cb[:, 11:12], cb[:, 11:12], cb[:, 13:14], OP.add)
            v.reciprocal(cb[:, 2:3], cb[:, 11:12])
            v.tensor_single_scalar(cent[p][:, :], x15[p][:, :], cb[:, 5:6],
                                   OP.add)
            v.reduce_max(cb[:, 7:8], cent[p][:, :], axis=AX,
                         apply_absolute_value=True)
        p5_vc[t] = tok(V)

    def p5_sb_f(t):
        cb = colb5[t % 2]

        @blk.scalar
        def _(s, t=t, cb=cb):
            s = CH(s, SC)
            wait(s, p5_vc[t])
            s.activation(out=cb[:, 4:5], in_=cb[:, 2:3], func=AF.Sqrt)
        p5_sb[t] = tok(SC)

    def p5_vd_f(t):
        cb = colb5[t % 2]

        @blk.vector
        def _(v, t=t, cb=cb):
            v = CH(v, V)
            wait(v, p5_sb[t])
            v.tensor_tensor(cb[:, 7:8], cb[:, 7:8], cb[:, 4:5], OP.mult)
            v.tensor_scalar_max(cb[:, 7:8], cb[:, 7:8], 1e-5)
            v.reciprocal(cb[:, 8:9], cb[:, 7:8])
            v.tensor_scalar_mul(cb[:, 8:9], cb[:, 8:9], 127.0)
            v.reciprocal(sinvz[:, t:t + 1], cb[:, 8:9])
            v.tensor_tensor(cb[:, 9:10], cb[:, 4:5], cb[:, 8:9], OP.mult)
            v.scalar_tensor_tensor(c1c[:, t:t + 1], sinvz[:, t:t + 1], -1.0,
                                   wconst[:, 0:1], OP.mult, OP.mult)
            v.scalar_tensor_tensor(c2c[:, t:t + 1], sinvz[:, t:t + 1], -1.0,
                                   wconst[:, 1:2], OP.mult, OP.mult)
            wait(v, p5_park[t])
            v.tensor_single_scalar(x15[t % 2][:, :], cent[t % 2][:, :],
                                   cb[:, 9:10], OP.mult)
        p5_vd[t] = tok(V)

    def p5_ba(t):
        p = t % 2
        cb = colb5[p]

        @blk.scalar
        def _(s, t=t, p=p, cb=cb):
            s = CH(s, SC)
            wait(s, p5_vd[t])
            s.copy(it2[:, :], x15[p][:, :])
            s.copy(ft2[p][:, :], it2[:, :])
        p5_back_s[t] = tok(SC)

    def p5_bb_f(t):
        p = t % 2

        @blk.vector
        def _(v, t=t, p=p):
            v = CH(v, V)
            wait(v, p5_back_s[t])
            if t >= 2:
                wait(v, p5_done[t - 2])
            v.tensor_tensor(ds2[:, :], x15[p][:, :], ft2[p][:, :], OP.subtract)
            v.scalar_tensor_tensor(ft2[p][:, :], ds2[:, :], 0.5, ft2[p][:, :],
                                   OP.is_gt, OP.add)
            v.scalar_tensor_tensor(zqt_sb[p][:, :], ds2[:, :], -0.5,
                                   ft2[p][:, :], OP.is_lt, OP.subtract)
        p5_bb[t] = tok(V)

    def p5_sy2(t):
        p = t % 2

        @blk.sync
        def _(sy, t=t, p=p):
            wait(sy, p5_bb[t])
            z3 = zqT[:, :].rearrange("q (c t) -> q c t", t=TC)[
                :, :, t * 128:(t + 1) * 128]
            p5_done.append(mark("dTR2" if t % 2 else "dTR",
                                sy.dma_start_transpose(z3, zqt_sb[p][:, :]),
                                16))

    p5_sy0(0)
    p5_sy0(1)
    p5_vb_f(0, p5_va(0))
    p5_sy1(0)
    p5_vc_f(0)
    p5_sb_f(0)
    p5_vd_f(0)
    for i in range(NTT):
        if i + 1 < NTT:
            p5_ba(i)
            p5_vb_f(i + 1, p5_va(i + 1))
            p5_sy1(i + 1)
            p5_vc_f(i + 1)
            p5_sb_f(i + 1)
            p5_bb_f(i)
            p5_sy2(i)
            if i + 2 < NTT:
                p5_sy0(i + 2)
            p5_vd_f(i + 1)
        else:
            p5_ba(i)
            p5_bb_f(i)
            p5_sy2(i)

    free_sc("p5")
    free_sc("yt")
    w12fh = [enter_sc("p6", nc.sbuf_tensor("w12fh%d" % i, [128, G * 256], F32))
             for i in range(2)]
    s1bh = [enter_sc("p6", nc.sbuf_tensor("s1bh%d" % i, [128, G * 256], BF16))
            for i in range(2)]
    w1q = [enter_sc("p6", nc.sbuf_tensor("w1q%d" % i, [128, G * 512], BF16))
           for i in range(3)]
    w2q = [enter_sc("p6", nc.sbuf_tensor("w2q%d" % i, [128, G * 512], BF16))
           for i in range(3)]
    silu_b = enter_sc("p6", nc.sbuf_tensor("silu_b", [128, 512], F32))
    hblk = [enter_sc("p6", nc.sbuf_tensor("hblk%d" % i, [128, 512], F32))
            for i in range(4)]

    tern_done = [None] * HSN
    p6_piece_done = [None, None]
    b12_done = {}
    silu_done = {}
    hdone = {}
    hpark = {}
    HW2 = G * 256

    def p6_tern(j):
        q3 = j % 3
        for pc in range(4):
            wi, half = pc // 2, pc % 2
            wpar = (4 * j + pc) % 2
            dstq = [w1q, w2q][wi]
            nthc = [wconst[:, 6:7], wconst[:, 7:8]][wi]
            thc_ = [wconst[:, 3:4], wconst[:, 4:5]][wi]
            if True:
                @blk.sync
                def _(sy, j=j, pc=pc, wi=wi, half=half, wpar=wpar):
                    if j == 0 and pc == 0:
                        wait(sy, tok("dTR"), tok("dTR2"), tok(V))
                    if p6_piece_done[wpar] is not None:
                        wait(sy, p6_piece_done[wpar])
                    if pc == 0 and j >= 3:
                        wait(sy, b12_done[(j - 3, NTT - 1)])
                    mark("dW%d" % wpar, sy.dma_start(
                        out=w12fh[wpar][:, :],
                        in_=w12_t[wi, j][:, half * HW2:(half + 1) * HW2]), 16)

                @blk.vector
                def _(v, j=j, q3=q3, pc=pc, wi=wi, half=half, wpar=wpar,
                      dstq=dstq, thc_=thc_, nthc=nthc):
                    v = CH(v, V)
                    wait(v, tok("dW%d" % wpar))
                    v.tensor_single_scalar(s1bh[wpar][:, :], w12fh[wpar][:, :],
                                           thc_, OP.is_gt)
                    v.tensor_single_scalar(
                        dstq[q3][:, half * HW2:(half + 1) * HW2],
                        w12fh[wpar][:, :], nthc, OP.is_lt)
                    v.tensor_tensor(dstq[q3][:, half * HW2:(half + 1) * HW2],
                                    s1bh[wpar][:, :],
                                    dstq[q3][:, half * HW2:(half + 1) * HW2],
                                    OP.subtract)
                    p6_piece_done[wpar] = tok(V)
        tern_done[j] = p6_piece_done[(4 * j + 3) % 2]

    for j in range(3):
        p6_tern(j)

    for hs in range(HSN):
        q3 = hs % 3
        for tci in range(NTT):
            idx = hs * NTT + tci
            bsel = 2 * (tci % 4)
            if True:
                @blk.tensor
                def _(t, hs=hs, q3=q3, tci=tci, idx=idx, bsel=bsel):
                    wait(t, tern_done[hs])
                    if hs == 0:
                        wait(t, p5_done[tci])
                    if idx >= 4:
                        wait(t, hdone[idx - 4])
                    te_group_wait(t)
                    for bank, wq in [(bsel + 0, w1q[q3]), (bsel + 1, w2q[q3])]:
                        for dc in range(G):
                            inst = t.matmul(
                                pb[bank][:, :],
                                zqT[:, dc * TC + tci * 128:dc * TC + (tci + 1) * 128],
                                wq[:, dc * 512:(dc + 1) * 512],
                                start=(dc == 0), stop=(dc == G - 1))
                        mark(TE, inst, 1)
                    b12_done[(hs, tci)] = tok(TE)

                @blk.scalar
                def _(s, hs=hs, tci=tci, idx=idx, bsel=bsel):
                    s = CH(s, SC)
                    wait(s, b12_done[(hs, tci)])
                    if idx >= 1:
                        wait(s, hdone[idx - 1])
                    s.activation(out=silu_b[:, :], in_=pb[bsel][:, :], func=AF.Silu,
                                 scale=c1c[:, tci:tci + 1])
                    silu_done[idx] = tok(SC)

                @blk.vector
                def _(v, hs=hs, tci=tci, idx=idx, bsel=bsel):
                    v = CH(v, V)
                    wait(v, silu_done[idx])
                    if idx >= 4:
                        wait(v, hpark[idx - 4])
                    hb = hblk[tci % 4]
                    v.scalar_tensor_tensor(hb[:, :], pb[bsel + 1][:, :],
                                           c2c[:, tci:tci + 1], silu_b[:, :],
                                           OP.mult, OP.mult)
                    v.reduce_max(colb[:, 10:11], hb[:, :], axis=AX,
                                 apply_absolute_value=True)
                    v.tensor_tensor(runmax[:, tci:tci + 1], runmax[:, tci:tci + 1],
                                    colb[:, 10:11], OP.max)
                    hdone[idx] = tok(V)

                @blk.gpsimd
                def _(g, hs=hs, tci=tci, idx=idx):
                    wait(g, hdone[idx])
                    hpark[idx] = mark("dHB%d" % (tci % 4), g.dma_start(
                        out=h_park[tci][:, hs * 512:(hs + 1) * 512],
                        in_=hblk[tci % 4][:, :]), 16)

            if tci == NTT - 1 and hs + 3 < HSN:
                p6_tern(hs + 3)

    if True:
        @blk.vector
        def _(v):
            v = CH(v, V)
            wait(v, hdone[HSN * NTT - 1])
            v.tensor_scalar_max(shc[:, :], runmax[:, :], 1e-5)
            v.reciprocal(shc[:, :], shc[:, :])
            v.tensor_scalar_mul(shc[:, :], shc[:, :], 127.0)
            v.reciprocal(c3c[:, :], shc[:, :])
            v.tensor_single_scalar(c3c[:, :], c3c[:, :], wconst[:, 2:3], OP.mult)
    hscale_done = tok(V)

    free_sc("p6")
    free_sc("zq")
    HQRT = 3 * 512
    hqT = enter_sc("hq", nc.sbuf_tensor("hqT", [128, HCN * TC], BF16))
    hrow = [enter_sc("p7a", nc.sbuf_tensor("hrow%d" % i, [128, HQRT], F32))
            for i in range(4)]
    ht_t = [enter_sc("p7a", nc.sbuf_tensor("ht_t%d" % i, [128, HQRT], F32))
            for i in range(3)]
    it7 = [enter_sc("p7a", nc.sbuf_tensor("it7%d" % i, [128, HQRT], I32))
           for i in range(3)]
    ft7 = [enter_sc("p7a", nc.sbuf_tensor("ft7%d" % i, [128, HQRT], F32))
           for i in range(3)]
    hq_sb = [enter_sc("p7a", nc.sbuf_tensor("hq_sb%d" % i, [128, HID], BF16))
             for i in range(2)]

    p7a_done = []
    piece_done = []
    v1_tok = {}
    sc_tok = {}
    P7PC = [(0, 3), (3, 6), (6, 9), (9, HSN)]
    PIECES = [(tci, hs0, hs1) for tci in range(NTT) for (hs0, hs1) in P7PC]
    NP7 = len(PIECES)

    def p7a_sync(k):
        tci, hs0, hs1 = PIECES[k]
        sz, hp4 = (hs1 - hs0) * 512, k % 4

        @blk.gpsimd
        def _(g, tci=tci, hs0=hs0, hs1=hs1, sz=sz, hp4=hp4, k=k):
            if k == 0:
                for j in range(4):
                    wait(g, tok("dHB%d" % j))
            if k >= 4:
                wait(g, piece_done[k - 4])
            mark("dHR%d" % hp4, g.dma_start(
                out=hrow[hp4][:, 0:sz],
                in_=h_park[tci][:, hs0 * 512:hs1 * 512]), 16)

    def p7a_v1(k):
        tci = PIECES[k][0]
        sz, hp, hp4 = (PIECES[k][2] - PIECES[k][1]) * 512, k % 3, k % 4

        @blk.vector
        def _(v, tci=tci, sz=sz, hp=hp, hp4=hp4):
            v = CH(v, V)
            wait(v, tok("dHR%d" % hp4), hscale_done)
            v.tensor_single_scalar(ht_t[hp][:, 0:sz], hrow[hp4][:, 0:sz],
                                   shc[:, tci:tci + 1], OP.mult)
        v1_tok[k] = tok(V)

    def p7a_sc(k):
        sz, hp = (PIECES[k][2] - PIECES[k][1]) * 512, k % 3

        @blk.scalar
        def _(s, sz=sz, hp=hp, k=k):
            s = CH(s, SC)
            wait(s, v1_tok[k])
            s.copy(it7[hp][:, 0:sz], ht_t[hp][:, 0:sz])
            s.copy(ft7[hp][:, 0:sz], it7[hp][:, 0:sz])
        sc_tok[k] = tok(SC)

    def p7a_v2(k):
        tci, hs0, hs1 = PIECES[k]
        off, sz, hp, hp4 = hs0 * 512, (hs1 - hs0) * 512, k % 3, k % 4

        @blk.vector
        def _(v, tci=tci, off=off, sz=sz, hp=hp, hp4=hp4, k=k):
            v = CH(v, V)
            wait(v, sc_tok[k])
            if len(p7a_done) >= 2:
                wait(v, p7a_done[-2])
            v.tensor_tensor(hrow[hp4][:, 0:sz], ht_t[hp][:, 0:sz],
                            ft7[hp][:, 0:sz], OP.subtract)
            v.scalar_tensor_tensor(ft7[hp][:, 0:sz], hrow[hp4][:, 0:sz],
                                   0.5, ft7[hp][:, 0:sz],
                                   OP.is_gt, OP.add)
            v.scalar_tensor_tensor(hq_sb[tci % 2][:, off:off + sz],
                                   hrow[hp4][:, 0:sz], -0.5,
                                   ft7[hp][:, 0:sz],
                                   OP.is_lt, OP.subtract)
            piece_done.append(tok(V))

        if k % 4 == 3:
            @blk.sync
            def _(sy, tci=tci):
                wait(sy, piece_done[-1])
                h3 = hqT[:, :].rearrange("q (c t) -> q c t", t=TC)[
                    :, :, tci * 128:(tci + 1) * 128]
                p7a_done.append(mark("dTR2" if tci % 2 else "dTR",
                                     sy.dma_start_transpose(h3,
                                                            hq_sb[tci % 2][:, :]),
                                     16))

    p7a_sync(0)
    p7a_sync(1)
    p7a_sync(2)
    p7a_sync(3)
    p7a_v1(0)
    p7a_v1(1)
    p7a_sc(0)
    for i in range(1, NP7):
        if i + 1 < NP7:
            p7a_v1(i + 1)
        p7a_v2(i - 1)
        if i + 3 < NP7:
            p7a_sync(i + 3)
        p7a_sc(i)
    p7a_v2(NP7 - 1)

    free_sc("p7a")
    w3fh = [enter_sc("p7b", nc.sbuf_tensor("w3fh%d" % i, [128, HID // 2], F32))
            for i in range(2)]
    s3bh = [enter_sc("p7b", nc.sbuf_tensor("s3bh%d" % i, [128, HID // 2], BF16))
            for i in range(2)]
    w3q = [enter_sc("p7b", nc.sbuf_tensor("w3q%d" % i, [128, HID], BF16))
           for i in range(3)]
    b3raw = [enter_sc("p7b", nc.sbuf_tensor("b3raw%d" % i, [128, TC], F32))
             for i in range(2)]
    b3strip = [enter_sc("p7b", nc.sbuf_tensor("b3strip%d" % i, [128, TC], F32))
               for i in range(2)]

    tern3_done = [None] * G
    p7b_piece_done = [None, None]
    b3mm_done = [None] * G
    b3ev_done = [None] * G
    b3tr_done = [None] * G
    b3st_done = [None] * G
    b3pk_done = [None] * G
    H2 = HID // 2

    def p7b_tern(j):
        q3 = j % 3
        for half in range(2):
            wpar = (2 * j + half) % 2
            if True:
                @blk.sync
                def _(sy, j=j, half=half, wpar=wpar):
                    if j == 0 and half == 0:
                        wait(sy, tok(V), tok("dTR"), tok("dTR2"))
                    if p7b_piece_done[wpar] is not None:
                        wait(sy, p7b_piece_done[wpar])
                    if half == 0 and j >= 3:
                        wait(sy, b3mm_done[j - 3])
                    mark("dW%d" % wpar, sy.dma_start(
                        out=w3fh[wpar][:, :],
                        in_=w3_t[j][:, half * H2:(half + 1) * H2]), 16)

                @blk.vector
                def _(v, j=j, q3=q3, half=half, wpar=wpar):
                    v = CH(v, V)
                    wait(v, tok("dW%d" % wpar))
                    v.tensor_single_scalar(s3bh[wpar][:, :], w3fh[wpar][:, :],
                                           wconst[:, 5:6], OP.is_gt)
                    v.scalar_tensor_tensor(w3q[q3][:, half * H2:(half + 1) * H2],
                                           w3fh[wpar][:, :], wconst[:, 8:9],
                                           s3bh[wpar][:, :],
                                           OP.is_lt, OP.subtract)
                    p7b_piece_done[wpar] = tok(V)
        tern3_done[j] = p7b_piece_done[(2 * j + 1) % 2]

    for j in range(3):
        p7b_tern(j)

    for dc in range(G):
        p = dc % 2
        q3 = dc % 3
        if True:
            @blk.tensor
            def _(t, dc=dc, p=p, q3=q3):
                wait(t, tern3_done[dc])
                if dc == 0:
                    wait(t, p7a_done[-1])
                if dc >= 2:
                    wait(t, b3ev_done[dc - 2])
                te_group_wait(t)
                for half in range(2):
                    ps = pb[2 * p + half]
                    for hcc in range(HCN):
                        inst = t.matmul(
                            ps[:, :],
                            w3q[q3][:, hcc * 128:(hcc + 1) * 128],
                            hqT[:, hcc * TC + half * 512:hcc * TC + half * 512 + 512],
                            start=(hcc == 0), stop=(hcc == HCN - 1))
                    mark(TE, inst, 1)
                b3mm_done[dc] = tok(TE)

        if dc + 3 < G:
            p7b_tern(dc + 3)

        if True:
            @blk.vector
            def _(v, dc=dc, p=p):
                v = CH(v, V)
                wait(v, b3mm_done[dc])
                if dc >= 2:
                    wait(v, b3tr_done[dc - 2])
                for half in range(2):
                    v.tensor_copy(b3raw[p][:, half * 512:half * 512 + 512],
                                  pb[2 * p + half][:, :])
                b3ev_done[dc] = tok(V)

            @blk.tensor
            def _(t, dc=dc, p=p):
                wait(t, b3ev_done[dc])
                if dc >= 2:
                    wait(t, b3st_done[dc - 2])
                te_group_wait(t)
                for tb in range(NTT):
                    inst = t.transpose(
                        pb[4 + 2 * p + tb // 4][:, (tb % 4) * 128:(tb % 4) * 128 + 128],
                        b3raw[p][:, tb * 128:(tb + 1) * 128], ident[:, :])
                    mark(TE, inst, 1)
                b3tr_done[dc] = tok(TE)

            @blk.vector
            def _(v, dc=dc):
                v = CH(v, V)
                wait(v, b3tr_done[dc])
                if dc >= 2:
                    wait(v, b3pk_done[dc - 2])
                for b in range(2):
                    v.tensor_copy(b3strip[p][:, b * 512:b * 512 + 512],
                                  pb[4 + 2 * p + b][:, :])
                b3st_done[dc] = tok(V)

            @blk.sync
            def _(sy, dc=dc, p=p):
                wait(sy, b3st_done[dc])
                b3pk_done[dc] = mark("dPK%d" % p, sy.dma_start(
                    out=b3_park[dc], in_=b3strip[p][:, :]), 16)

    free_sc("p7b")
    free_sc("hq")
    xt = [enter(nc.sbuf_tensor("xtp8_%d" % i, [128, D], F32)) for i in range(2)]
    ws["xn32"] = enter(nc.sbuf_tensor("xn32c", [128, D], F32))
    b3t = [enter(nc.sbuf_tensor("b3t%d" % i, [128, D], F32)) for i in range(2)]
    ot8 = [ws["xn32"], enter(nc.sbuf_tensor("xn32c2", [128, D], F32))]
    colb5b = [colb, colb2]

    p8_out = [None] * NTT
    p8_front_v = [None] * NTT
    p8_back_v = [None] * NTT

    def p8_front(tci):
        p = tci % 2
        cb = colb5b[p]
        if True:
            @blk.sync
            def _(sy, tci=tci, p=p):
                if tci == 0:
                    wait(sy, tok("dPK"), tok("dPK0"), tok("dPK1"))
                if tci >= 2:
                    wait(sy, p8_back_v[tci - 2])
                mark("dX%d" % p, sy.dma_start(
                    out=b3t[p][:, :],
                    in_=b3_park[:, :, tci * 128:(tci + 1) * 128].rearrange(
                        "dc p q -> p dc q")), 16)
                mark("dX%d" % p, sy.dma_start(out=xt[p][:, :], in_=x1_park[tci]), 16)

            def pre8(v, tci=tci, p=p):
                wait(v._e, tok("dX%d" % p))
                v.tensor_single_scalar(b3t[p][:, :], b3t[p][:, :],
                                       c3c[:, tci:tci + 1], OP.mult)

            p8_front_v[tci] = ln_stats(blk, b3t[p], 0, cb=cb, pre=pre8)

    def p8_back(tci):
        p = tci % 2
        cb = colb5b[p]
        if True:
            @blk.vector
            def _(v, tci=tci, p=p, cb=cb):
                v = CH(v, V)
                wait(v, p8_front_v[tci])
                if tci >= 2:
                    wait(v, p8_out[tci - 2])
                v.tensor_scalar(ot8[p][:, :], b3t[p][:, :], cb[:, 4:5],
                                cb[:, 6:7], OP.mult, OP.add)
                v.tensor_tensor(ot8[p][:, :], ot8[p][:, :], xt[p][:, :], OP.add)
                p8_back_v[tci] = tok(V)

            @blk.sync
            def _(sy, tci=tci, p=p):
                wait(sy, p8_back_v[tci])
                p8_out[tci] = mark("dOUT2" if p else "dOUT", sy.dma_start(
                    out=out_dram[tci * 128:(tci + 1) * 128, :],
                    in_=ot8[p][:, :]), 16)

    p8_front(0)
    for tci in range(1, NTT):
        p8_front(tci)
        p8_back(tci - 1)
    p8_back(NTT - 1)

    if True:
        @blk.gpsimd
        def _(g):
            wait(g, tok("dOUT"), tok("dOUT2"))

    blk.__exit__(None, None, None)
    for h in reversed(stack):
        h.__exit__(None, None, None)
    return nc


def _prep(inputs):
    f = np.ascontiguousarray
    x = np.asarray(inputs["L0_attn_x"], dtype=np.float32).reshape(S, D)
    Wa = np.asarray(inputs["L0_attn_Wa"], dtype=np.float32)
    Wx = np.asarray(inputs["L0_attn_Wx"], dtype=np.float32)
    ba = np.asarray(inputs["L0_attn_ba"], dtype=np.float32)
    bx = np.asarray(inputs["L0_attn_bx"], dtype=np.float32)
    lam = np.asarray(inputs["L0_attn_Lam"], dtype=np.float32)
    w1 = np.asarray(inputs["L0_ffn_w1"], dtype=np.float32)
    w2 = np.asarray(inputs["L0_ffn_w2"], dtype=np.float32)
    w3 = np.asarray(inputs["L0_ffn_w3"], dtype=np.float32)

    wa_t = f(Wa.T.reshape(G, 128, G, 128).transpose(2, 1, 0, 3)
             .reshape(G, 128, G * 128))
    wx_t = f(Wx.T.reshape(G, 128, G, 128).transpose(2, 1, 0, 3)
             .reshape(G, 128, G * 128))
    w12 = f(np.stack([
        w1.T.reshape(G, 128, HSN, 512).transpose(2, 1, 0, 3).reshape(HSN, 128, G * 512),
        w2.T.reshape(G, 128, HSN, 512).transpose(2, 1, 0, 3).reshape(HSN, 128, G * 512),
    ]))
    w3_tiled = f(w3.T.reshape(HCN, 128, G, 128).transpose(2, 1, 0, 3)
                 .reshape(G, 128, HCN * 128))

    def col128(vec):
        return f(vec.reshape(G, 128).T)

    lam_c, ba_c, bx_c = col128(lam), col128(ba), col128(bx)
    ident = np.eye(128, dtype=np.float32)

    def slices(w):
        flat = f(w).reshape(NCORE, 128, NSL)
        return [f(flat[c]) for c in range(NCORE)]

    w1sl, w2sl, w3sl = slices(w1), slices(w2), slices(w3)

    in_maps = []
    for c in range(NCORE):
        mask = np.zeros((128, 2 * NCORE), dtype=np.float32)
        mask[:, 0:NCORE] = (np.arange(NCORE) < c).astype(np.float32)[None, :]
        mask[:, NCORE:] = 1.0 - mask[:, 0:NCORE]
        in_maps.append({
            "x": f(x[c * TC:(c + 1) * TC]),
            "wa_t": wa_t, "wx_t": wx_t, "w12_t": w12, "w3_t": w3_tiled,
            "lam": lam_c, "ba": ba_c, "bx": bx_c,
            "mask": mask, "ident": ident,
            "onesc": np.ones((128, 1), dtype=np.float32),
            "onesr": np.ones((1, 128), dtype=np.float32),
            "w1s": w1sl[c], "w2s": w2sl[c], "w3s": w3sl[c],
        })
    return in_maps


def kernel(**inputs):
    in_maps = _prep(inputs)
    nc = build_nc()
    res = run_bass_kernel_spmd(nc, in_maps, core_ids=list(range(NCORE)))
    out = np.concatenate([np.asarray(r["out"]) for r in res.results], axis=0)
    return out.reshape(1, S, D).astype(np.float32)


# version 22 (final)
# speedup vs baseline: 1.1376x; 1.0072x over previous
import numpy as np
import concourse.bass as bass
import concourse.mybir as mybir
from concourse.bass_utils import run_bass_kernel_spmd

F32 = mybir.dt.float32
BF16 = mybir.dt.bfloat16
I32 = mybir.dt.int32
AF = mybir.ActivationFunctionType
OP = mybir.AluOpType
AX = mybir.AxisListType.X

NCORE = 8
S, D, HID = 8192, 2048, 5632
TC = S // NCORE
NTT = TC // 128
G = D // 128
HCN = HID // 128
HSN = HID // 512
NSL = HID * D // NCORE // 128
EPS = float(np.float32(1e-5))

_CHAIN_OPS = frozenset([
    "tensor_tensor", "tensor_add", "tensor_sub", "tensor_mul", "tensor_max",
    "tensor_scalar", "tensor_single_scalar", "tensor_scalar_add",
    "tensor_scalar_sub", "tensor_scalar_mul", "tensor_scalar_max",
    "tensor_scalar_min", "tensor_copy", "tensor_relu", "tensor_reduce",
    "reduce_max", "reduce_sum", "scalar_tensor_tensor", "tensor_tensor_scan",
    "memset", "reciprocal", "select", "copy_predicated", "bn_stats",
    "activation", "copy", "add", "mul",
    "partition_all_reduce", "partition_broadcast",
])


class _Chained:

    def __init__(self, eng, sem, cnt, name):
        self._e = eng
        self._sem = sem
        self._cnt = cnt
        self._name = name

    def __getattr__(self, attr):
        real = getattr(self._e, attr)
        if attr not in _CHAIN_OPS:
            return real
        cnt, name = self._cnt, self._name

        def wrapped(*a, **k):
            if cnt[name] > 0:
                self._e.wait_ge(self._sem, cnt[name])
            inst = real(*a, **k)
            inst.then_inc(self._sem, 1)
            cnt[name] += 1
            return inst
        return wrapped


def build_nc():
    nc = bass.Bass("TRN2", target_bir_lowering=False)

    x_in = nc.dram_tensor("x", [TC, D], F32, kind="ExternalInput").ap()
    wa_t = nc.dram_tensor("wa_t", [G, 128, G * 128], F32, kind="ExternalInput").ap()
    wx_t = nc.dram_tensor("wx_t", [G, 128, G * 128], F32, kind="ExternalInput").ap()
    w12_t = nc.dram_tensor("w12_t", [2, HSN, 128, G * 512], F32,
                           kind="ExternalInput").ap()
    w3_t = nc.dram_tensor("w3_t", [G, 128, HCN * 128], F32, kind="ExternalInput").ap()
    lam_in = nc.dram_tensor("lam", [128, G], F32, kind="ExternalInput").ap()
    ba_in = nc.dram_tensor("ba", [128, G], F32, kind="ExternalInput").ap()
    bx_in = nc.dram_tensor("bx", [128, G], F32, kind="ExternalInput").ap()
    mask_in = nc.dram_tensor("mask", [128, 2 * NCORE], F32, kind="ExternalInput").ap()
    ident_in = nc.dram_tensor("ident", [128, 128], F32, kind="ExternalInput").ap()
    onesc_in = nc.dram_tensor("onesc", [128, 1], F32, kind="ExternalInput").ap()
    onesr_in = nc.dram_tensor("onesr", [1, 128], F32, kind="ExternalInput").ap()
    w1s_in = nc.dram_tensor("w1s", [128, NSL], F32, kind="ExternalInput").ap()
    w2s_in = nc.dram_tensor("w2s", [128, NSL], F32, kind="ExternalInput").ap()
    w3s_in = nc.dram_tensor("w3s", [128, NSL], F32, kind="ExternalInput").ap()

    out_dram = nc.dram_tensor("out", [TC, D], F32, kind="ExternalOutput").ap()

    a_spill = nc.dram_tensor("a_spill", [G, 128, TC], F32).ap()
    u_spill = nc.dram_tensor("u_spill", [G, 128, TC], F32).ap()
    h_park = nc.dram_tensor("h_park", [NTT, 128, HSN * 512], F32).ap()
    x1_park = nc.dram_tensor("x1_park", [NTT, 128, D], F32).ap()
    b3_park = nc.dram_tensor("b3_park", [G, 128, TC], F32).ap()
    cc_sum_in = nc.dram_tensor("cc_sum_in", [4, 1], F32).ap()
    cc_sum_out = nc.dram_tensor("cc_sum_out", [4, 1], F32).ap()
    cc_ag_in = nc.dram_tensor("cc_ag_in", [128, 2 * G], F32).ap()
    cc_ag_out = nc.dram_tensor("cc_ag_out", [NCORE * 128, 2 * G], F32).ap()

    stack = []

    def enter(h):
        t = h.__enter__()
        stack.append(h)
        return t

    scopes = {}

    def enter_sc(name, h):
        t = h.__enter__()
        scopes.setdefault(name, []).append(h)
        return t

    def free_sc(name):
        for h in reversed(scopes.pop(name, [])):
            h.__exit__(None, None, None)

    sems, cnt = {}, {}
    for n in ["sV", "sS", "sT", "sG",
              "dPRE", "dW0", "dW1", "dX0", "dX1", "dSP0", "dSP1",
              "dRL0", "dRL1", "dCC", "dHP0", "dHP1",
              "dHR0", "dHR1", "dHR2", "dHR3", "dWA0", "dWA1",
              "dHB0", "dHB1", "dHB2", "dHB3", "dHB4", "dHB5", "dHB6", "dHB7",
              "dPK", "dPK0", "dPK1", "dOUT", "dOUT2", "dTR", "dTR2", "cc"]:
        sems[n] = enter(nc.semaphore(n))
        cnt[n] = 0

    def mark(name, inst, step):
        inst.then_inc(sems[name], step)
        cnt[name] += step
        return (name, cnt[name])

    def tok(name):
        return (name, cnt[name])

    def wait(eng, *tokens):
        for t_ in tokens:
            if t_ is None:
                continue
            name, v = t_
            if v > 0:
                eng.wait_ge(sems[name], v)

    V, SC, TE, GP = "sV", "sS", "sT", "sG"

    def CH(eng, name):
        return _Chained(eng, sems[name], cnt, name)

    def te_group_wait(t):
        if cnt[TE] > 0:
            t.wait_ge(sems[TE], cnt[TE])

    ident = enter(nc.sbuf_tensor("identt", [128, 128], F32))
    lam = enter(nc.sbuf_tensor("lamt", [128, G], F32))
    bav = enter(nc.sbuf_tensor("bav", [128, G], F32))
    bxv = enter(nc.sbuf_tensor("bxv", [128, G], F32))
    coef = enter(nc.sbuf_tensor("coef", [128, G], F32))
    smallA = enter(nc.sbuf_tensor("smallA", [128, G], F32))
    maskt = enter(nc.sbuf_tensor("maskt", [128, 2 * NCORE], F32))
    wsums = enter(nc.sbuf_tensor("wsums", [128, 4], F32))
    wsall = enter(nc.sbuf_tensor("wsall", [128, 4], F32))
    wsl2 = enter(nc.sbuf_tensor("wsl2", [128, NSL // 8], F32))
    wtmp = enter(nc.sbuf_tensor("wtmp", [128, 2], F32))
    onesc = enter(nc.sbuf_tensor("onesct", [128, 1], F32))
    onesr = enter(nc.sbuf_tensor("onesrt", [1, 128], F32))
    wconst = enter(nc.sbuf_tensor("wconst", [128, 16], F32))
    colAH = enter(nc.sbuf_tensor("colAH", [128, 2 * G], F32))
    gat = enter(nc.sbuf_tensor("gat", [128, NCORE * 2 * G], F32))
    Hcol = enter(nc.sbuf_tensor("Hcol", [128, G], F32))
    tmpG = enter(nc.sbuf_tensor("tmpG", [128, G], F32))
    tmpS = enter(nc.sbuf_tensor("tmpS", [128, G], F32))
    colb = enter(nc.sbuf_tensor("colb", [128, 16], F32))
    colb2 = enter(nc.sbuf_tensor("colb2", [128, 16], F32))
    bstb = enter(nc.sbuf_tensor("bstb", [128, 32], F32))
    bstb2 = enter(nc.sbuf_tensor("bstb2", [128, 32], F32))
    sinvz = enter(nc.sbuf_tensor("sinvz", [128, NTT], F32))
    c1c = enter(nc.sbuf_tensor("c1c", [128, NTT], F32))
    c2c = enter(nc.sbuf_tensor("c2c", [128, NTT], F32))
    c3c = enter(nc.sbuf_tensor("c3c", [128, NTT], F32))
    runmax = enter(nc.sbuf_tensor("runmax", [128, NTT], F32))
    shc = enter(nc.sbuf_tensor("shc", [128, NTT], F32))
    zbuf = enter(nc.sbuf_tensor("zbuf", [128, TC], F32))
    ws = {}
    wsl = enter_sc("p0", nc.sbuf_tensor("wsl", [128, NSL], F32))

    pb = [enter(nc.psum_tensor("pb%d" % k, [128, 512], F32)) for k in range(8)]

    blk = nc.Block("ph0"); blk.__enter__()
    if True:
        @blk.sync
        def _(sy):
            for dst, src in [(ident, ident_in), (lam, lam_in), (bav, ba_in),
                             (bxv, bx_in), (maskt, mask_in),
                             (onesc, onesc_in), (onesr, onesr_in)]:
                mark("dPRE", sy.dma_start(out=dst[:, :], in_=src), 16)

        @blk.scalar
        def _(s):
            s = CH(s, SC)
            wait(s, tok("dPRE"))
            s.activation(out=smallA[:, :], in_=lam[:, :], func=AF.Exp, scale=-1.0)

        @blk.vector
        def _(v):
            v = CH(v, V)
            wait(v, tok(SC), tok("dPRE"))
            v.memset(zbuf[:, :], 0.0)
            v.memset(Hcol[:, :], 0.0)
            v.memset(runmax[:, :], 0.0)
            e, t_ = smallA, coef
            v.tensor_scalar(t_[:, :], e[:, :], -0.2, 0.25, OP.mult, OP.add)
            v.scalar_tensor_tensor(t_[:, :], e[:, :], -1.0, t_[:, :], OP.mult, OP.mult)
            v.tensor_scalar_add(t_[:, :], t_[:, :], float(np.float32(1.0 / 3)))
            v.scalar_tensor_tensor(t_[:, :], e[:, :], -1.0, t_[:, :], OP.mult, OP.mult)
            v.tensor_scalar_add(t_[:, :], t_[:, :], 0.5)
            v.scalar_tensor_tensor(t_[:, :], e[:, :], -1.0, t_[:, :], OP.mult, OP.mult)
            v.tensor_scalar_add(t_[:, :], t_[:, :], 1.0)
            v.tensor_tensor(t_[:, :], e[:, :], t_[:, :], OP.mult)
            v.tensor_scalar_mul(t_[:, :], t_[:, :], -8.0)
            v.memset(wsums[:, 0:4], 0.0)

    wred = []
    p0t = {}
    WPC = NSL // 8

    def p0_wpiece(i):
        j, pc = i // 8, i % 8
        src = [w1s_in, w2s_in, w3s_in][j]

        @blk.sync
        def _(sy, i=i, pc=pc, src=src):
            if i >= 1:
                wait(sy, wred[i - 1])
            mark("dPRE", sy.dma_start(
                out=wsl2[:, :], in_=src[:, pc * WPC:(pc + 1) * WPC]), 16)

        @blk.vector
        def _(v, j=j):
            v = CH(v, V)
            wait(v, tok("dPRE"))
            v.reduce_sum(wtmp[:, 0:1], wsl2[:, :], axis=AX,
                         apply_absolute_value=True)
            v.tensor_tensor(wsums[:, j:j + 1], wsums[:, j:j + 1],
                            wtmp[:, 0:1], OP.add)
        wred.append(tok(V))

    def p0_wsum():
        @blk.tensor
        def _(t):
            wait(t, wred[23], gate_evac[0])
            te_group_wait(t)
            mark(TE, t.matmul(pb[0][0:4, 0:1], wsums[:, 0:4], onesc[:, 0:1],
                              start=True, stop=True), 1)

        @blk.vector
        def _(v):
            v = CH(v, V)
            wait(v, tok(TE))
            v.tensor_copy(wsall[0:4, 0:1], pb[0][0:4, 0:1])
        p0t["wsall"] = tok(V)

        @blk.gpsimd
        def _(g):
            wait(g, p0t["wsall"])
            mark("dCC", g.dma_start(out=cc_sum_in, in_=wsall[0:4, 0:1]), 16)
            wait(g, tok("dCC"))
            mark("cc", g.collective_compute(
                "AllReduce", OP.add, replica_groups=[list(range(NCORE))],
                ins=[cc_sum_in.opt()], outs=[cc_sum_out.opt()]), 1)

    def p0_wcc():
        @blk.gpsimd
        def _(g):
            wait(g, tok("cc"))
            mark("dCC", g.dma_start(out=wconst[0:1, 12:16],
                                    in_=cc_sum_out.rearrange("p f -> f p")), 16)

        @blk.vector
        def _(v):
            v = CH(v, V)
            wait(v, tok("dCC"))
            r0 = wconst[0:1, :]
            v.tensor_scalar(r0[:, 0:3], r0[:, 12:15],
                            float(np.float32(1.0 / (HID * D))),
                            1e-5, OP.mult, OP.max)
            v.tensor_scalar_mul(r0[:, 3:6], r0[:, 0:3], 0.5)
            v.tensor_scalar_mul(r0[:, 6:9], r0[:, 3:6], -1.0)
        p0t["wscale"] = tok(V)

    def p0_wbcast():
        @blk.tensor
        def _(t):
            wait(t, p0t["wscale"], gate_evac[0])
            te_group_wait(t)
            mark(TE, t.matmul(pb[1][:, 0:9], onesr[0:1, :], wconst[0:1, 0:9],
                              start=True, stop=True), 1)

        @blk.vector
        def _(v):
            v = CH(v, V)
            wait(v, tok(TE))
            v.tensor_copy(wconst[:, 0:9], pb[1][:, 0:9])
        p0t["wconst"] = tok(V)

    def ln_stats(blk, src_t, c0=0, cb=None, pre=None):
        if cb is None:
            cb = colb
        bst = bstb if cb is colb else bstb2

        @blk.vector
        def _(v):
            v = CH(v, V)
            if pre is not None:
                pre(v)
            for qq in range(4):
                v.bn_stats(bst[:, 6 * qq:6 * qq + 6],
                           src_t[:, c0 + 512 * qq:c0 + 512 * qq + 512])
            v.tensor_tensor(bst[:, 24:32], bst[:, 1:24:3], bst[:, 1:24:3],
                            OP.mult)
            v.reduce_sum(cb[:, 7:8], bst[:, 1:24:3], axis=AX)
            v.reduce_sum(cb[:, 8:9], bst[:, 24:32], axis=AX)
            v.reduce_sum(cb[:, 9:10], bst[:, 2:24:3], axis=AX)
            v.tensor_scalar_mul(cb[:, 5:6], cb[:, 7:8], -0.125)
            v.tensor_tensor(cb[:, 12:13], cb[:, 5:6], cb[:, 5:6], OP.mult)
            v.scalar_tensor_tensor(cb[:, 11:12], cb[:, 8:9], 0.125,
                                   cb[:, 12:13], OP.mult, OP.subtract)
            v.tensor_scalar(cb[:, 12:13], cb[:, 9:10],
                            float(np.float32(1.0 / D)), EPS, OP.mult, OP.add)
            v.tensor_tensor(cb[:, 11:12], cb[:, 11:12], cb[:, 12:13], OP.add)
            v.reciprocal(cb[:, 2:3], cb[:, 11:12])

        @blk.scalar
        def _(s):
            s = CH(s, SC)
            wait(s, tok(V))
            s.activation(out=cb[:, 4:5], in_=cb[:, 2:3], func=AF.Sqrt)
            s.activation(out=cb[:, 6:7], in_=cb[:, 5:6], func=AF.Identity,
                         scale=cb[:, 4:5])
        return tok(SC)

    free_sc("p0")
    xnT_hi = enter_sc("p12", nc.sbuf_tensor("xnT_hi", [128, G * TC], BF16))
    xnT_lo = enter_sc("p12", nc.sbuf_tensor("xnT_lo", [128, G * TC], BF16))
    wafh = enter_sc("p12", nc.sbuf_tensor("wafh", [128, D], F32))
    wahi = [enter_sc("p12", nc.sbuf_tensor("wahi%d" % i, [128, D], BF16)) for i in range(3)]
    walo = [enter_sc("p12", nc.sbuf_tensor("walo%d" % i, [128, D], BF16)) for i in range(3)]
    wxhi = [enter_sc("p12", nc.sbuf_tensor("wxhi%d" % i, [128, D], BF16)) for i in range(3)]
    wxlo = [enter_sc("p12", nc.sbuf_tensor("wxlo%d" % i, [128, D], BF16)) for i in range(3)]

    split_done = [None] * G
    mm_done = [None] * G

    def p2_prep(j):
        q = j % 3
        for srcw, dsth, dstl in [(wa_t, wahi, walo), (wx_t, wxhi, wxlo)]:
            if True:
                @blk.gpsimd
                def _(g, j=j, q=q, srcw=srcw):
                    if j >= 1 and srcw is wa_t:
                        wait(g, split_done[j - 1])
                    elif srcw is wx_t:
                        wait(g, tok(V))
                    if j >= 3 and srcw is wa_t:
                        wait(g, mm_done[j - 3])
                    mark("dWA%d" % (j % 2), g.dma_start(out=wafh[:, :],
                                                        in_=srcw[j]), 16)

                @blk.scalar
                def _(s, j=j, q=q, dsth=dsth):
                    s = CH(s, SC)
                    wait(s, tok("dWA%d" % (j % 2)))
                    s.activation(out=dsth[q][:, :], in_=wafh[:, :], func=AF.Copy)

                @blk.vector
                def _(v, j=j, q=q, dsth=dsth, dstl=dstl):
                    v = CH(v, V)
                    wait(v, tok(SC))
                    v.tensor_tensor(dstl[q][:, :], wafh[:, :], dsth[q][:, :],
                                    OP.subtract)
        split_done[j] = tok(V)

    hi_t = enter_sc("p1x", nc.sbuf_tensor("hi_t", [128, D], BF16))
    lo_t = enter_sc("p1x", nc.sbuf_tensor("lo_t", [128, D], BF16))
    xt1 = enter_sc("p1x", nc.sbuf_tensor("xt1", [128, D], F32))
    ws["xn32"] = enter_sc("p1x", nc.sbuf_tensor("xn32a", [128, D], F32))

    xt1b = enter_sc("p1x", nc.sbuf_tensor("xt1b", [128, D], F32))
    xn1 = [ws["xn32"],
           enter_sc("p1x", nc.sbuf_tensor("xn32a2", [128, D], F32))]
    hi2 = [hi_t, enter_sc("p1x", nc.sbuf_tensor("hi_t2", [128, D], BF16))]
    lo2 = [lo_t, enter_sc("p1x", nc.sbuf_tensor("lo_t2", [128, D], BF16))]
    xtb = [xt1, xt1b]
    colb5 = [colb, colb2]

    p1_xbar_done = []
    p1_back_s = [None] * NTT
    p1_front_v = [None] * NTT

    def p1_front(tci):
        p = tci % 2
        cb = colb5[p]
        if True:
            @blk.sync
            def _(sy, tci=tci, p=p):
                if tci >= 2:
                    wait(sy, p1_back_s[tci - 2])
                mark("dX%d" % p, sy.dma_start(
                    out=xtb[p][:, :], in_=x_in[tci * 128:(tci + 1) * 128, :]), 16)

            def pre(v, p=p):
                wait(v._e, tok("dX%d" % p))

            ln_stats(blk, xtb[p], 0, cb=cb, pre=pre)
            p1_front_v[tci] = tok(V)

    def p1_back(tci):
        p = tci % 2
        cb = colb5[p]
        if True:
            @blk.scalar
            def _(s, tci=tci, p=p, cb=cb):
                s = CH(s, SC)
                wait(s, p1_front_v[tci])
                if tci >= 2:
                    wait(s, p1_xbar_done[tci - 2])
                s.activation(out=xn1[p][:, :], in_=xtb[p][:, :], func=AF.Identity,
                             bias=cb[:, 6:7], scale=cb[:, 4:5])
                p1_back_s[tci] = tok(SC)

            @blk.vector
            def _(v, tci=tci, p=p):
                v = CH(v, V)
                wait(v, tok(SC))
                v.tensor_copy(hi2[p][:, :], xn1[p][:, :])
                v.tensor_tensor(lo2[p][:, :], xn1[p][:, :], hi2[p][:, :],
                                OP.subtract)

            @blk.sync
            def _(sy, tci=tci, p=p):
                wait(sy, tok(V))
                hi3 = xnT_hi[:, :].rearrange("q (c t) -> q c t", t=TC)[
                    :, :, tci * 128:(tci + 1) * 128]
                lo3 = xnT_lo[:, :].rearrange("q (c t) -> q c t", t=TC)[
                    :, :, tci * 128:(tci + 1) * 128]
                mark("dTR2" if tci % 2 else "dTR",
                     sy.dma_start_transpose(hi3, hi2[p][:, :]), 16)
                p1_xbar_done.append(
                    mark("dTR2" if tci % 2 else "dTR",
                         sy.dma_start_transpose(lo3, lo2[p][:, :]), 16))

    p1_front(0)
    for tci in range(1, NTT):
        p1_front(tci)
        p1_back(tci - 1)
    p1_back(NTT - 1)

    for j in range(3):
        p2_prep(j)

    free_sc("p1x")
    rgt = enter_sc("p12", nc.sbuf_tensor("rgt", [128, TC], F32))
    igt = enter_sc("p12", nc.sbuf_tensor("igt", [128, TC], F32))
    agt = [enter_sc("p12", nc.sbuf_tensor("agt%d" % i, [128, TC], F32)) for i in range(2)]
    ugt = [enter_sc("p12", nc.sbuf_tensor("ugt%d" % i, [128, TC], F32)) for i in range(2)]
    xnr = enter_sc("p12", nc.sbuf_tensor("xnr", [128, TC], F32))
    gsc = enter_sc("p12", nc.sbuf_tensor("gsc", [128, TC], F32))
    y0b = enter_sc("p12", nc.sbuf_tensor("y0b", [128, TC], F32))
    afl = enter_sc("p12", nc.sbuf_tensor("afl", [128, TC], F32))

    gate_evac = [None, None]
    spill_done = [None, None]
    scan1_done = [None] * G

    for hc in range(G):
        p = hc % 2
        q = hc % 3
        if True:
            @blk.tensor
            def _(t, hc=hc, p=p, q=q):
                wait(t, split_done[hc])
                if hc == 14:
                    wait(t, p0t["wsall"])
                if hc >= 2:
                    wait(t, gate_evac[p])
                te_group_wait(t)
                for half in range(2):
                    if hc == 0:
                        wait(t, p1_xbar_done[2 + 4 * half],
                             p1_xbar_done[3 + 4 * half])
                    for bank, whi, wlo in [(0, wahi[q], walo[q]),
                                           (2, wxhi[q], wxlo[q])]:
                        ps = pb[bank + half + 4 * p]
                        first = True
                        for wtile, xtile in [(whi, xnT_hi), (whi, xnT_lo),
                                             (wlo, xnT_hi)]:
                            for dc in range(G):
                                inst = t.matmul(
                                    ps[:, :],
                                    wtile[:, dc * 128:(dc + 1) * 128],
                                    xtile[:, dc * TC + half * 512:
                                          dc * TC + half * 512 + 512],
                                    start=first,
                                    stop=(wtile is wlo and dc == G - 1))
                                first = False
                        mark(TE, inst, 1)
                mm_done[hc] = tok(TE)

        if hc + 3 < G:
            p2_prep(hc + 3)

        if True:
            @blk.scalar
            def _(s, hc=hc, p=p):
                s = CH(s, SC)
                wait(s, mm_done[hc])
                if spill_done[p] is not None:
                    wait(s, spill_done[p])
                for half in range(2):
                    s.activation(out=rgt[:, half * 512:half * 512 + 512],
                                 in_=pb[0 + half + 4 * p][:, :], func=AF.Sigmoid,
                                 bias=bav[:, hc:hc + 1])
                for half in range(2):
                    s.activation(out=igt[:, half * 512:half * 512 + 512],
                                 in_=pb[2 + half + 4 * p][:, :], func=AF.Sigmoid,
                                 bias=bxv[:, hc:hc + 1])
                s.activation(out=agt[p][:, :], in_=rgt[:, :], func=AF.Exp,
                             scale=coef[:, hc:hc + 1])
                gate_evac[p] = tok(SC)

            @blk.vector
            def _(v, hc=hc, p=p):
                v = CH(v, V)
                wait(v, gate_evac[p])
                v.tensor_tensor(xnr[:, :],
                                xnT_hi[:, hc * TC:(hc + 1) * TC],
                                xnT_lo[:, hc * TC:(hc + 1) * TC], OP.add)
                v.tensor_tensor(xnr[:, :], igt[:, :], xnr[:, :], OP.mult)
                v.tensor_tensor(gsc[:, :], agt[p][:, :], agt[p][:, :], OP.mult)
                v.tensor_scalar(gsc[:, :], gsc[:, :], -1.0, 1.0, OP.mult, OP.add)

            @blk.scalar
            def _(s, hc=hc):
                s = CH(s, SC)
                wait(s, tok(V))
                s.activation(out=gsc[:, :], in_=gsc[:, :], func=AF.Sqrt)

            @blk.vector
            def _(v, hc=hc, p=p):
                v = CH(v, V)
                wait(v, tok(SC))
                if spill_done[p] is not None:
                    wait(v, spill_done[p])
                v.tensor_tensor(ugt[p][:, :], gsc[:, :], xnr[:, :], OP.mult)
                v.tensor_tensor_scan(y0b[:, :], agt[p][:, :], ugt[p][:, :],
                                     0.0, OP.mult, OP.add)
                v.tensor_copy(colAH[:, G + hc:G + hc + 1], y0b[:, TC - 1:TC])
                v.tensor_tensor_scan(afl[:, :], agt[p][:, :], zbuf[:, :],
                                     1.0, OP.mult, OP.add)
                v.tensor_copy(colAH[:, hc:hc + 1], afl[:, TC - 1:TC])
                scan1_done[hc] = tok(V)

            @blk.gpsimd
            def _(g, hc=hc, p=p):
                wait(g, scan1_done[hc])
                mark("dSP%d" % p, g.dma_start(out=a_spill[hc], in_=agt[p][:, :]), 16)
                spill_done[p] = mark("dSP%d" % p,
                                     g.dma_start(out=u_spill[hc], in_=ugt[p][:, :]),
                                     16)

        if hc < 12:
            p0_wpiece(2 * hc)
            p0_wpiece(2 * hc + 1)
        elif hc == 12:
            p0_wsum()
        elif hc == 13:
            p0_wcc()
        elif hc == 14:
            p0_wbcast()

    if True:
        @blk.gpsimd
        def _(g):
            wait(g, scan1_done[G - 2], scan1_done[G - 1])
            mark("dCC", g.dma_start(out=cc_ag_in, in_=colAH[:, :]), 16)
            wait(g, tok("dCC"))
            mark("cc", g.collective_compute(
                "AllGather", OP.bypass, replica_groups=[list(range(NCORE))],
                ins=[cc_ag_in.opt()], outs=[cc_ag_out.opt()]), 1)
            wait(g, tok("cc"))
            mark("dCC", g.dma_start(
                out=gat[:, :],
                in_=cc_ag_out.rearrange("(r p) j -> p r j", p=128)), 16)

        @blk.vector
        def _(v):
            v = CH(v, V)
            wait(v, tok("dCC"))
            for k in range(NCORE - 1):
                base = k * 2 * G
                v.tensor_scalar(tmpG[:, :], gat[:, base:base + G],
                                maskt[:, k:k + 1],
                                maskt[:, NCORE + k:NCORE + k + 1],
                                OP.mult, OP.add)
                v.tensor_single_scalar(tmpS[:, :], gat[:, base + G:base + 2 * G],
                                       maskt[:, k:k + 1], OP.mult)
                v.tensor_tensor(Hcol[:, :], Hcol[:, :], tmpG[:, :], OP.mult)
                v.tensor_tensor(Hcol[:, :], Hcol[:, :], tmpS[:, :], OP.add)
    fold_done = tok(V)

    free_sc("p12")
    zqT = enter_sc("zq", nc.sbuf_tensor("zqT", [128, G * TC], BF16))
    yT = enter_sc("yt", nc.sbuf_tensor("yT", [128, NTT * D], F32))
    arl = [enter_sc("p4", nc.sbuf_tensor("arl%d" % i, [128, TC], F32)) for i in range(2)]
    url = [enter_sc("p4", nc.sbuf_tensor("url%d" % i, [128, TC], F32)) for i in range(2)]
    ybuf = [enter_sc("p4", nc.sbuf_tensor("ybuf%d" % i, [128, TC], F32)) for i in range(2)]

    scan2_done = [None, None]
    tr_done = [None, None]
    evac_done = [None, None]
    for hc in range(G):
        p = hc % 2
        if True:
            @blk.sync
            def _(sy, hc=hc, p=p):
                if hc == 0:
                    wait(sy, tok("dSP0"), tok("dSP1"))
                if hc >= 2:
                    wait(sy, scan2_done[p])
                mark("dRL%d" % p, sy.dma_start(out=arl[p][:, :], in_=a_spill[hc]), 16)
                mark("dRL%d" % p, sy.dma_start(out=url[p][:, :], in_=u_spill[hc]), 16)

            @blk.vector
            def _(v, hc=hc, p=p):
                v = CH(v, V)
                wait(v, tok("dRL%d" % p), fold_done)
                if hc >= 2:
                    wait(v, tr_done[p])
                v.tensor_tensor_scan(ybuf[p][:, :], arl[p][:, :], url[p][:, :],
                                     Hcol[:, hc:hc + 1], OP.mult, OP.add)
                scan2_done[p] = tok(V)

            @blk.tensor
            def _(t, hc=hc, p=p):
                wait(t, scan2_done[p])
                if hc == 0:
                    wait(t, p0t["wconst"])
                if hc >= 2:
                    wait(t, evac_done[p])
                te_group_wait(t)
                for tb in range(NTT):
                    inst = t.transpose(
                        pb[4 * p + tb // 4][:, (tb % 4) * 128:(tb % 4) * 128 + 128],
                        ybuf[p][:, tb * 128:(tb + 1) * 128], ident[:, :])
                    mark(TE, inst, 1)
                tr_done[p] = tok(TE)

            @blk.vector
            def _(v, hc=hc, p=p):
                v = CH(v, V)
                wait(v, tr_done[p])
                for b in range(2):
                    out3 = yT[:, :].rearrange("p (tb d) -> p tb d", d=D)[
                        :, 4 * b:4 * b + 4, hc * 128:(hc + 1) * 128]
                    v.tensor_copy(out3, pb[4 * p + b][:, :].rearrange(
                        "p (tb q) -> p tb q", q=128))
                evac_done[p] = tok(V)

    free_sc("p4")
    xt = [enter_sc("p5", nc.sbuf_tensor("xtp5_%d" % i, [128, D], F32))
          for i in range(2)]
    cent = [enter_sc("p5", nc.sbuf_tensor("cent%d" % i, [128, D], F32))
            for i in range(2)]
    x15 = [enter_sc("p5", nc.sbuf_tensor("x15_%d" % i, [128, D], F32))
           for i in range(2)]
    it2 = enter_sc("p5", nc.sbuf_tensor("it2", [128, D], I32))
    ft2 = [enter_sc("p5", nc.sbuf_tensor("ft2_%d" % i, [128, D], F32))
           for i in range(2)]
    ds2 = enter_sc("p5", nc.sbuf_tensor("ds2", [128, D], F32))
    zqt_sb = [enter_sc("p5", nc.sbuf_tensor("zqt_sb%d" % i, [128, D], BF16))
              for i in range(2)]
    colb5 = [colb, colb2]

    p5_done = []
    p5_back_s = [None] * NTT
    p5_park = [None] * NTT
    p5_vb = [None] * NTT
    p5_vc = [None] * NTT
    p5_sb = [None] * NTT
    p5_vd = [None] * NTT
    p5_bb = [None] * NTT

    def p5_sy0(t):
        p = t % 2

        @blk.sync
        def _(sy, t=t, p=p):
            if t < 2:
                wait(sy, scan2_done[0], scan2_done[1],
                     tr_done[0], tr_done[1])
            if t >= 2:
                wait(sy, p5_done[t - 2])
            mark("dX%d" % p, sy.dma_start(
                out=xt[p][:, :], in_=x_in[t * 128:(t + 1) * 128, :]), 16)

    def p5_va(t):
        p = t % 2

        def pre5(v, t=t, p=p):
            if t == 0:
                wait(v._e, evac_done[0], evac_done[1])
            if t >= 2 and p5_back_s[t - 2] is not None:
                wait(v._e, p5_back_s[t - 2])
        return ln_stats(blk, yT, t * D, cb=colb5[p], pre=pre5)

    def p5_vb_f(t, lnA_s):
        p = t % 2
        cb = colb5[p]

        @blk.vector
        def _(v, t=t, p=p, cb=cb, lnA_s=lnA_s):
            v = CH(v, V)
            wait(v, tok("dX%d" % p), lnA_s)
            v.tensor_scalar(x15[p][:, :], yT[:, t * D:(t + 1) * D],
                            cb[:, 4:5], cb[:, 6:7], OP.mult, OP.add)
            v.tensor_tensor(x15[p][:, :], x15[p][:, :], xt[p][:, :], OP.add)
        p5_vb[t] = tok(V)

    def p5_sy1(t):
        p = t % 2

        @blk.sync
        def _(sy, t=t, p=p):
            wait(sy, p5_vb[t])
            p5_park[t] = mark("dPK%d" % p, sy.dma_start(
                out=x1_park[t], in_=x15[p][:, :]), 16)

    def p5_vc_f(t):
        p = t % 2
        cb = colb5[p]
        bst = bstb if cb is colb else bstb2

        @blk.vector
        def _(v, t=t, p=p, cb=cb, bst=bst):
            v = CH(v, V)
            for qq in range(4):
                v.bn_stats(bst[:, 6 * qq:6 * qq + 6],
                           x15[p][:, 512 * qq:512 * qq + 512])
            v.tensor_tensor(bst[:, 24:32], bst[:, 1:24:3], bst[:, 1:24:3],
                            OP.mult)
            v.reduce_sum(cb[:, 0:1], bst[:, 1:24:3], axis=AX)
            v.reduce_sum(cb[:, 1:2], bst[:, 24:32], axis=AX)
            v.reduce_sum(cb[:, 3:4], bst[:, 2:24:3], axis=AX)
            v.tensor_scalar_mul(cb[:, 5:6], cb[:, 0:1], -0.125)
            v.tensor_tensor(cb[:, 12:13], cb[:, 5:6], cb[:, 5:6], OP.mult)
            v.scalar_tensor_tensor(cb[:, 11:12], cb[:, 1:2], 0.125,
                                   cb[:, 12:13], OP.mult, OP.subtract)
            v.tensor_scalar(cb[:, 13:14], cb[:, 3:4],
                            float(np.float32(1.0 / D)), EPS, OP.mult, OP.add)
            v.tensor_tensor(cb[:, 11:12], cb[:, 11:12], cb[:, 13:14], OP.add)
            v.reciprocal(cb[:, 2:3], cb[:, 11:12])
            v.tensor_single_scalar(cent[p][:, :], x15[p][:, :], cb[:, 5:6],
                                   OP.add)
            v.reduce_max(cb[:, 7:8], cent[p][:, :], axis=AX,
                         apply_absolute_value=True)
        p5_vc[t] = tok(V)

    def p5_sb_f(t):
        cb = colb5[t % 2]

        @blk.scalar
        def _(s, t=t, cb=cb):
            s = CH(s, SC)
            wait(s, p5_vc[t])
            s.activation(out=cb[:, 4:5], in_=cb[:, 2:3], func=AF.Sqrt)
        p5_sb[t] = tok(SC)

    def p5_vd_f(t):
        cb = colb5[t % 2]

        @blk.vector
        def _(v, t=t, cb=cb):
            v = CH(v, V)
            wait(v, p5_sb[t])
            v.tensor_tensor(cb[:, 7:8], cb[:, 7:8], cb[:, 4:5], OP.mult)
            v.tensor_scalar_max(cb[:, 7:8], cb[:, 7:8], 1e-5)
            v.reciprocal(cb[:, 8:9], cb[:, 7:8])
            v.tensor_scalar_mul(cb[:, 8:9], cb[:, 8:9], 127.0)
            v.reciprocal(sinvz[:, t:t + 1], cb[:, 8:9])
            v.tensor_tensor(cb[:, 9:10], cb[:, 4:5], cb[:, 8:9], OP.mult)
            v.scalar_tensor_tensor(c1c[:, t:t + 1], sinvz[:, t:t + 1], -1.0,
                                   wconst[:, 0:1], OP.mult, OP.mult)
            v.scalar_tensor_tensor(c2c[:, t:t + 1], sinvz[:, t:t + 1], -1.0,
                                   wconst[:, 1:2], OP.mult, OP.mult)
            wait(v, p5_park[t])
            v.tensor_single_scalar(x15[t % 2][:, :], cent[t % 2][:, :],
                                   cb[:, 9:10], OP.mult)
        p5_vd[t] = tok(V)

    def p5_ba(t):
        p = t % 2
        cb = colb5[p]

        @blk.scalar
        def _(s, t=t, p=p, cb=cb):
            s = CH(s, SC)
            wait(s, p5_vd[t])
            s.copy(it2[:, :], x15[p][:, :])
            s.copy(ft2[p][:, :], it2[:, :])
        p5_back_s[t] = tok(SC)

    def p5_bb_f(t):
        p = t % 2

        @blk.vector
        def _(v, t=t, p=p):
            v = CH(v, V)
            wait(v, p5_back_s[t])
            if t >= 2:
                wait(v, p5_done[t - 2])
            v.tensor_tensor(ds2[:, :], x15[p][:, :], ft2[p][:, :], OP.subtract)
            v.scalar_tensor_tensor(ft2[p][:, :], ds2[:, :], 0.5, ft2[p][:, :],
                                   OP.is_gt, OP.add)
            v.scalar_tensor_tensor(zqt_sb[p][:, :], ds2[:, :], -0.5,
                                   ft2[p][:, :], OP.is_lt, OP.subtract)
        p5_bb[t] = tok(V)

    def p5_sy2(t):
        p = t % 2

        @blk.sync
        def _(sy, t=t, p=p):
            wait(sy, p5_bb[t])
            z3 = zqT[:, :].rearrange("q (c t) -> q c t", t=TC)[
                :, :, t * 128:(t + 1) * 128]
            p5_done.append(mark("dTR2" if t % 2 else "dTR",
                                sy.dma_start_transpose(z3, zqt_sb[p][:, :]),
                                16))

    p5_sy0(0)
    p5_sy0(1)
    p5_vb_f(0, p5_va(0))
    p5_sy1(0)
    p5_vc_f(0)
    p5_sb_f(0)
    p5_vd_f(0)
    for i in range(NTT):
        if i + 1 < NTT:
            p5_ba(i)
            p5_vb_f(i + 1, p5_va(i + 1))
            p5_sy1(i + 1)
            p5_vc_f(i + 1)
            p5_sb_f(i + 1)
            p5_bb_f(i)
            p5_sy2(i)
            if i + 2 < NTT:
                p5_sy0(i + 2)
            p5_vd_f(i + 1)
        else:
            p5_ba(i)
            p5_bb_f(i)
            p5_sy2(i)

    free_sc("p5")
    free_sc("yt")
    w12fh = [enter_sc("p6", nc.sbuf_tensor("w12fh%d" % i, [128, G * 256], F32))
             for i in range(2)]
    s1bh = [enter_sc("p6", nc.sbuf_tensor("s1bh%d" % i, [128, G * 256], BF16))
            for i in range(2)]
    w1q = [enter_sc("p6", nc.sbuf_tensor("w1q%d" % i, [128, G * 512], BF16))
           for i in range(3)]
    w2q = [enter_sc("p6", nc.sbuf_tensor("w2q%d" % i, [128, G * 512], BF16))
           for i in range(3)]
    silu_b = enter_sc("p6", nc.sbuf_tensor("silu_b", [128, 512], F32))
    hblk = [enter_sc("p6", nc.sbuf_tensor("hblk%d" % i, [128, 512], F32))
            for i in range(4)]

    tern_done = [None] * HSN
    p6_piece_done = [None, None]
    b12_done = {}
    silu_done = {}
    hdone = {}
    hpark = {}
    HW2 = G * 256

    def p6_tern(j):
        q3 = j % 3
        for pc in range(4):
            wi, half = pc // 2, pc % 2
            wpar = (4 * j + pc) % 2
            dstq = [w1q, w2q][wi]
            nthc = [wconst[:, 6:7], wconst[:, 7:8]][wi]
            thc_ = [wconst[:, 3:4], wconst[:, 4:5]][wi]
            if True:
                @blk.sync
                def _(sy, j=j, pc=pc, wi=wi, half=half, wpar=wpar):
                    if j == 0 and pc == 0:
                        wait(sy, tok("dTR"), tok("dTR2"), tok(V))
                    if p6_piece_done[wpar] is not None:
                        wait(sy, p6_piece_done[wpar])
                    if pc == 0 and j >= 3:
                        wait(sy, b12_done[(j - 3, NTT - 1)])
                    mark("dW%d" % wpar, sy.dma_start(
                        out=w12fh[wpar][:, :],
                        in_=w12_t[wi, j][:, half * HW2:(half + 1) * HW2]), 16)

                @blk.vector
                def _(v, j=j, q3=q3, pc=pc, wi=wi, half=half, wpar=wpar,
                      dstq=dstq, thc_=thc_, nthc=nthc):
                    v = CH(v, V)
                    wait(v, tok("dW%d" % wpar))
                    v.tensor_single_scalar(s1bh[wpar][:, :], w12fh[wpar][:, :],
                                           thc_, OP.is_gt)
                    v.tensor_single_scalar(
                        dstq[q3][:, half * HW2:(half + 1) * HW2],
                        w12fh[wpar][:, :], nthc, OP.is_lt)
                    v.tensor_tensor(dstq[q3][:, half * HW2:(half + 1) * HW2],
                                    s1bh[wpar][:, :],
                                    dstq[q3][:, half * HW2:(half + 1) * HW2],
                                    OP.subtract)
                    p6_piece_done[wpar] = tok(V)
        tern_done[j] = p6_piece_done[(4 * j + 3) % 2]

    for j in range(3):
        p6_tern(j)

    for hs in range(HSN):
        q3 = hs % 3
        for tci in range(NTT):
            idx = hs * NTT + tci
            bsel = 2 * (tci % 4)
            if True:
                @blk.tensor
                def _(t, hs=hs, q3=q3, tci=tci, idx=idx, bsel=bsel):
                    wait(t, tern_done[hs])
                    if hs == 0:
                        wait(t, p5_done[tci])
                    if idx >= 4:
                        wait(t, hdone[idx - 4])
                    te_group_wait(t)
                    for bank, wq in [(bsel + 0, w1q[q3]), (bsel + 1, w2q[q3])]:
                        for dc in range(G):
                            inst = t.matmul(
                                pb[bank][:, :],
                                zqT[:, dc * TC + tci * 128:dc * TC + (tci + 1) * 128],
                                wq[:, dc * 512:(dc + 1) * 512],
                                start=(dc == 0), stop=(dc == G - 1))
                        mark(TE, inst, 1)
                    b12_done[(hs, tci)] = tok(TE)

                @blk.scalar
                def _(s, hs=hs, tci=tci, idx=idx, bsel=bsel):
                    s = CH(s, SC)
                    wait(s, b12_done[(hs, tci)])
                    if idx >= 1:
                        wait(s, hdone[idx - 1])
                    s.activation(out=silu_b[:, :], in_=pb[bsel][:, :], func=AF.Silu,
                                 scale=c1c[:, tci:tci + 1])
                    silu_done[idx] = tok(SC)

                @blk.vector
                def _(v, hs=hs, tci=tci, idx=idx, bsel=bsel):
                    v = CH(v, V)
                    wait(v, silu_done[idx])
                    if idx >= 4:
                        wait(v, hpark[idx - 4])
                    hb = hblk[tci % 4]
                    v.scalar_tensor_tensor(hb[:, :], pb[bsel + 1][:, :],
                                           c2c[:, tci:tci + 1], silu_b[:, :],
                                           OP.mult, OP.mult)
                    v.reduce_max(colb[:, 10:11], hb[:, :], axis=AX,
                                 apply_absolute_value=True)
                    v.tensor_tensor(runmax[:, tci:tci + 1], runmax[:, tci:tci + 1],
                                    colb[:, 10:11], OP.max)
                    hdone[idx] = tok(V)

                @blk.gpsimd
                def _(g, hs=hs, tci=tci, idx=idx):
                    wait(g, hdone[idx])
                    hpark[idx] = mark("dHB%d" % (tci % 4), g.dma_start(
                        out=h_park[tci][:, hs * 512:(hs + 1) * 512],
                        in_=hblk[tci % 4][:, :]), 16)

            if tci == NTT - 1 and hs + 3 < HSN:
                p6_tern(hs + 3)

    if True:
        @blk.vector
        def _(v):
            v = CH(v, V)
            wait(v, hdone[HSN * NTT - 1])
            v.tensor_scalar_max(shc[:, :], runmax[:, :], 1e-5)
            v.reciprocal(shc[:, :], shc[:, :])
            v.tensor_scalar_mul(shc[:, :], shc[:, :], 127.0)
            v.reciprocal(c3c[:, :], shc[:, :])
            v.tensor_single_scalar(c3c[:, :], c3c[:, :], wconst[:, 2:3], OP.mult)
    hscale_done = tok(V)

    free_sc("p6")
    free_sc("zq")
    HQRT = 3 * 512
    hqT = enter_sc("hq", nc.sbuf_tensor("hqT", [128, HCN * TC], BF16))
    hrow = [enter_sc("p7a", nc.sbuf_tensor("hrow%d" % i, [128, HQRT], F32))
            for i in range(4)]
    ht_t = [enter_sc("p7a", nc.sbuf_tensor("ht_t%d" % i, [128, HQRT], F32))
            for i in range(3)]
    it7 = [enter_sc("p7a", nc.sbuf_tensor("it7%d" % i, [128, HQRT], I32))
           for i in range(3)]
    ft7 = [enter_sc("p7a", nc.sbuf_tensor("ft7%d" % i, [128, HQRT], F32))
           for i in range(3)]
    hq_sb = [enter_sc("p7a", nc.sbuf_tensor("hq_sb%d" % i, [128, HID], BF16))
             for i in range(2)]

    p7a_done = []
    piece_done = []
    v1_tok = {}
    sc_tok = {}
    P7PC = [(0, 3), (3, 6), (6, 9), (9, HSN)]
    PIECES = [(tci, hs0, hs1) for tci in range(NTT) for (hs0, hs1) in P7PC]
    NP7 = len(PIECES)

    def p7a_sync(k):
        tci, hs0, hs1 = PIECES[k]
        sz, hp4 = (hs1 - hs0) * 512, k % 4

        @blk.gpsimd
        def _(g, tci=tci, hs0=hs0, hs1=hs1, sz=sz, hp4=hp4, k=k):
            if k == 0:
                for j in range(4):
                    wait(g, tok("dHB%d" % j))
            if k >= 4:
                wait(g, piece_done[k - 4])
            mark("dHR%d" % hp4, g.dma_start(
                out=hrow[hp4][:, 0:sz],
                in_=h_park[tci][:, hs0 * 512:hs1 * 512]), 16)

    def p7a_v1(k):
        tci = PIECES[k][0]
        sz, hp, hp4 = (PIECES[k][2] - PIECES[k][1]) * 512, k % 3, k % 4

        @blk.vector
        def _(v, tci=tci, sz=sz, hp=hp, hp4=hp4):
            v = CH(v, V)
            wait(v, tok("dHR%d" % hp4), hscale_done)
            v.tensor_single_scalar(ht_t[hp][:, 0:sz], hrow[hp4][:, 0:sz],
                                   shc[:, tci:tci + 1], OP.mult)
        v1_tok[k] = tok(V)

    def p7a_sc(k):
        sz, hp = (PIECES[k][2] - PIECES[k][1]) * 512, k % 3

        @blk.scalar
        def _(s, sz=sz, hp=hp, k=k):
            s = CH(s, SC)
            wait(s, v1_tok[k])
            s.copy(it7[hp][:, 0:sz], ht_t[hp][:, 0:sz])
            s.copy(ft7[hp][:, 0:sz], it7[hp][:, 0:sz])
        sc_tok[k] = tok(SC)

    def p7a_v2(k):
        tci, hs0, hs1 = PIECES[k]
        off, sz, hp, hp4 = hs0 * 512, (hs1 - hs0) * 512, k % 3, k % 4

        @blk.vector
        def _(v, tci=tci, off=off, sz=sz, hp=hp, hp4=hp4, k=k):
            v = CH(v, V)
            wait(v, sc_tok[k])
            if len(p7a_done) >= 2:
                wait(v, p7a_done[-2])
            v.tensor_tensor(hrow[hp4][:, 0:sz], ht_t[hp][:, 0:sz],
                            ft7[hp][:, 0:sz], OP.subtract)
            v.scalar_tensor_tensor(ft7[hp][:, 0:sz], hrow[hp4][:, 0:sz],
                                   0.5, ft7[hp][:, 0:sz],
                                   OP.is_gt, OP.add)
            v.scalar_tensor_tensor(hq_sb[tci % 2][:, off:off + sz],
                                   hrow[hp4][:, 0:sz], -0.5,
                                   ft7[hp][:, 0:sz],
                                   OP.is_lt, OP.subtract)
            piece_done.append(tok(V))

        if k % 4 == 3:
            @blk.sync
            def _(sy, tci=tci):
                wait(sy, piece_done[-1])
                h3 = hqT[:, :].rearrange("q (c t) -> q c t", t=TC)[
                    :, :, tci * 128:(tci + 1) * 128]
                p7a_done.append(mark("dTR2" if tci % 2 else "dTR",
                                     sy.dma_start_transpose(h3,
                                                            hq_sb[tci % 2][:, :]),
                                     16))

    p7a_sync(0)
    p7a_sync(1)
    p7a_sync(2)
    p7a_sync(3)
    p7a_v1(0)
    p7a_v1(1)
    p7a_sc(0)
    for i in range(1, NP7):
        if i + 1 < NP7:
            p7a_v1(i + 1)
        p7a_v2(i - 1)
        if i + 3 < NP7:
            p7a_sync(i + 3)
        p7a_sc(i)
    p7a_v2(NP7 - 1)

    free_sc("p7a")
    w3fh = [enter_sc("p7b", nc.sbuf_tensor("w3fh%d" % i, [128, HID // 2], F32))
            for i in range(2)]
    s3bh = [enter_sc("p7b", nc.sbuf_tensor("s3bh%d" % i, [128, HID // 2], BF16))
            for i in range(2)]
    w3q = [enter_sc("p7b", nc.sbuf_tensor("w3q%d" % i, [128, HID], BF16))
           for i in range(3)]
    b3raw = [enter_sc("p7b", nc.sbuf_tensor("b3raw%d" % i, [128, TC], F32))
             for i in range(2)]
    b3strip = [enter_sc("p7b", nc.sbuf_tensor("b3strip%d" % i, [128, TC], F32))
               for i in range(2)]

    tern3_done = [None] * G
    p7b_piece_done = [None, None]
    b3mm_done = [None] * G
    b3ev_done = [None] * G
    b3tr_done = [None] * G
    b3st_done = [None] * G
    b3pk_done = [None] * G
    H2 = HID // 2

    def p7b_tern(j):
        q3 = j % 3
        for half in range(2):
            wpar = (2 * j + half) % 2
            if True:
                @blk.sync
                def _(sy, j=j, half=half, wpar=wpar):
                    if j == 0 and half == 0:
                        wait(sy, tok(V), tok("dTR"), tok("dTR2"))
                    if p7b_piece_done[wpar] is not None:
                        wait(sy, p7b_piece_done[wpar])
                    if half == 0 and j >= 3:
                        wait(sy, b3mm_done[j - 3])
                    mark("dW%d" % wpar, sy.dma_start(
                        out=w3fh[wpar][:, :],
                        in_=w3_t[j][:, half * H2:(half + 1) * H2]), 16)

                @blk.vector
                def _(v, j=j, q3=q3, half=half, wpar=wpar):
                    v = CH(v, V)
                    wait(v, tok("dW%d" % wpar))
                    v.tensor_single_scalar(s3bh[wpar][:, :], w3fh[wpar][:, :],
                                           wconst[:, 5:6], OP.is_gt)
                    v.scalar_tensor_tensor(w3q[q3][:, half * H2:(half + 1) * H2],
                                           w3fh[wpar][:, :], wconst[:, 8:9],
                                           s3bh[wpar][:, :],
                                           OP.is_lt, OP.subtract)
                    p7b_piece_done[wpar] = tok(V)
        tern3_done[j] = p7b_piece_done[(2 * j + 1) % 2]

    for j in range(3):
        p7b_tern(j)

    for dc in range(G):
        p = dc % 2
        q3 = dc % 3
        if True:
            @blk.tensor
            def _(t, dc=dc, p=p, q3=q3):
                wait(t, tern3_done[dc])
                if dc == 0:
                    wait(t, p7a_done[-1])
                if dc >= 2:
                    wait(t, b3ev_done[dc - 2])
                te_group_wait(t)
                for half in range(2):
                    ps = pb[2 * p + half]
                    for hcc in range(HCN):
                        inst = t.matmul(
                            ps[:, :],
                            w3q[q3][:, hcc * 128:(hcc + 1) * 128],
                            hqT[:, hcc * TC + half * 512:hcc * TC + half * 512 + 512],
                            start=(hcc == 0), stop=(hcc == HCN - 1))
                    mark(TE, inst, 1)
                b3mm_done[dc] = tok(TE)

        if dc + 3 < G:
            p7b_tern(dc + 3)

        if True:
            @blk.vector
            def _(v, dc=dc, p=p):
                v = CH(v, V)
                wait(v, b3mm_done[dc])
                if dc >= 2:
                    wait(v, b3tr_done[dc - 2])
                for half in range(2):
                    v.tensor_copy(b3raw[p][:, half * 512:half * 512 + 512],
                                  pb[2 * p + half][:, :])
                b3ev_done[dc] = tok(V)

            @blk.tensor
            def _(t, dc=dc, p=p):
                wait(t, b3ev_done[dc])
                if dc >= 2:
                    wait(t, b3st_done[dc - 2])
                te_group_wait(t)
                for tb in range(NTT):
                    inst = t.transpose(
                        pb[4 + 2 * p + tb // 4][:, (tb % 4) * 128:(tb % 4) * 128 + 128],
                        b3raw[p][:, tb * 128:(tb + 1) * 128], ident[:, :])
                    mark(TE, inst, 1)
                b3tr_done[dc] = tok(TE)

            @blk.vector
            def _(v, dc=dc):
                v = CH(v, V)
                wait(v, b3tr_done[dc])
                if dc >= 2:
                    wait(v, b3pk_done[dc - 2])
                for b in range(2):
                    v.tensor_copy(b3strip[p][:, b * 512:b * 512 + 512],
                                  pb[4 + 2 * p + b][:, :])
                b3st_done[dc] = tok(V)

            @blk.sync
            def _(sy, dc=dc, p=p):
                wait(sy, b3st_done[dc])
                b3pk_done[dc] = mark("dPK%d" % p, sy.dma_start(
                    out=b3_park[dc], in_=b3strip[p][:, :]), 16)

    free_sc("p7b")
    free_sc("hq")
    xt = [enter(nc.sbuf_tensor("xtp8_%d" % i, [128, D], F32)) for i in range(2)]
    ws["xn32"] = enter(nc.sbuf_tensor("xn32c", [128, D], F32))
    b3t = [enter(nc.sbuf_tensor("b3t%d" % i, [128, D], F32)) for i in range(2)]
    ot8 = [ws["xn32"], enter(nc.sbuf_tensor("xn32c2", [128, D], F32))]
    colb5b = [colb, colb2]

    p8_out = [None] * NTT
    p8_front_v = [None] * NTT
    p8_back_v = [None] * NTT

    def p8_front(tci):
        p = tci % 2
        cb = colb5b[p]
        if True:
            @blk.sync
            def _(sy, tci=tci, p=p):
                if tci == 0:
                    wait(sy, tok("dPK"), tok("dPK0"), tok("dPK1"))
                if tci >= 2:
                    wait(sy, p8_back_v[tci - 2])
                mark("dX%d" % p, sy.dma_start(
                    out=b3t[p][:, :],
                    in_=b3_park[:, :, tci * 128:(tci + 1) * 128].rearrange(
                        "dc p q -> p dc q")), 16)
                mark("dX%d" % p, sy.dma_start(out=xt[p][:, :], in_=x1_park[tci]), 16)

            def pre8(v, tci=tci, p=p):
                wait(v._e, tok("dX%d" % p))
                v.tensor_single_scalar(b3t[p][:, :], b3t[p][:, :],
                                       c3c[:, tci:tci + 1], OP.mult)

            p8_front_v[tci] = ln_stats(blk, b3t[p], 0, cb=cb, pre=pre8)

    def p8_back(tci):
        p = tci % 2
        cb = colb5b[p]
        if True:
            @blk.vector
            def _(v, tci=tci, p=p, cb=cb):
                v = CH(v, V)
                wait(v, p8_front_v[tci])
                if tci >= 2:
                    wait(v, p8_out[tci - 2])
                v.tensor_scalar(ot8[p][:, :], b3t[p][:, :], cb[:, 4:5],
                                cb[:, 6:7], OP.mult, OP.add)
                v.tensor_tensor(ot8[p][:, :], ot8[p][:, :], xt[p][:, :], OP.add)
                p8_back_v[tci] = tok(V)

            @blk.sync
            def _(sy, tci=tci, p=p):
                wait(sy, p8_back_v[tci])
                p8_out[tci] = mark("dOUT2" if p else "dOUT", sy.dma_start(
                    out=out_dram[tci * 128:(tci + 1) * 128, :],
                    in_=ot8[p][:, :]), 16)

    p8_front(0)
    for tci in range(1, NTT):
        p8_front(tci)
        p8_back(tci - 1)
    p8_back(NTT - 1)

    if True:
        @blk.gpsimd
        def _(g):
            wait(g, tok("dOUT"), tok("dOUT2"))

    blk.__exit__(None, None, None)
    for h in reversed(stack):
        h.__exit__(None, None, None)
    return nc


def _prep(inputs):
    f = np.ascontiguousarray
    x = np.asarray(inputs["L0_attn_x"], dtype=np.float32).reshape(S, D)
    Wa = np.asarray(inputs["L0_attn_Wa"], dtype=np.float32)
    Wx = np.asarray(inputs["L0_attn_Wx"], dtype=np.float32)
    ba = np.asarray(inputs["L0_attn_ba"], dtype=np.float32)
    bx = np.asarray(inputs["L0_attn_bx"], dtype=np.float32)
    lam = np.asarray(inputs["L0_attn_Lam"], dtype=np.float32)
    w1 = np.asarray(inputs["L0_ffn_w1"], dtype=np.float32)
    w2 = np.asarray(inputs["L0_ffn_w2"], dtype=np.float32)
    w3 = np.asarray(inputs["L0_ffn_w3"], dtype=np.float32)

    wa_t = f(Wa.T.reshape(G, 128, G, 128).transpose(2, 1, 0, 3)
             .reshape(G, 128, G * 128))
    wx_t = f(Wx.T.reshape(G, 128, G, 128).transpose(2, 1, 0, 3)
             .reshape(G, 128, G * 128))
    w12 = f(np.stack([
        w1.T.reshape(G, 128, HSN, 512).transpose(2, 1, 0, 3).reshape(HSN, 128, G * 512),
        w2.T.reshape(G, 128, HSN, 512).transpose(2, 1, 0, 3).reshape(HSN, 128, G * 512),
    ]))
    w3_tiled = f(w3.T.reshape(HCN, 128, G, 128).transpose(2, 1, 0, 3)
                 .reshape(G, 128, HCN * 128))

    def col128(vec):
        return f(vec.reshape(G, 128).T)

    lam_c, ba_c, bx_c = col128(lam), col128(ba), col128(bx)
    ident = np.eye(128, dtype=np.float32)

    def slices(w):
        flat = f(w).reshape(NCORE, 128, NSL)
        return [f(flat[c]) for c in range(NCORE)]

    w1sl, w2sl, w3sl = slices(w1), slices(w2), slices(w3)

    in_maps = []
    for c in range(NCORE):
        mask = np.zeros((128, 2 * NCORE), dtype=np.float32)
        mask[:, 0:NCORE] = (np.arange(NCORE) < c).astype(np.float32)[None, :]
        mask[:, NCORE:] = 1.0 - mask[:, 0:NCORE]
        in_maps.append({
            "x": f(x[c * TC:(c + 1) * TC]),
            "wa_t": wa_t, "wx_t": wx_t, "w12_t": w12, "w3_t": w3_tiled,
            "lam": lam_c, "ba": ba_c, "bx": bx_c,
            "mask": mask, "ident": ident,
            "onesc": np.ones((128, 1), dtype=np.float32),
            "onesr": np.ones((1, 128), dtype=np.float32),
            "w1s": w1sl[c], "w2s": w2sl[c], "w3s": w3sl[c],
        })
    return in_maps


def kernel(**inputs):
    in_maps = _prep(inputs)
    nc = build_nc()
    res = run_bass_kernel_spmd(nc, in_maps, core_ids=list(range(NCORE)))
    out = np.concatenate([np.asarray(r["out"]) for r in res.results], axis=0)
    return out.reshape(1, S, D).astype(np.float32)
